# version 12
# baseline (speedup 1.0000x reference)
import os
import math
import contextlib
import numpy as np
import ml_dtypes
import concourse.bass as bass
import concourse.mybir as mybir
from concourse.bass_utils import run_bass_kernel_spmd

F32 = mybir.dt.float32
BF16 = mybir.dt.bfloat16
AF = mybir.ActivationFunctionType
ALU = mybir.AluOpType
AX = mybir.AxisListType

NCORE = 8
SEQ = 2048
D = 1024
BPC = 2
TOK = BPC * SEQ
TT = 512
NTT = TOK // TT
NTTS = SEQ // TT
NCH = D // 128

HEADS = 16
DH = 64
WIN = 256
INNER = 1024
L0DEPTH = 6
DIN = 2730
DPAD = 2816
NH1 = 2 * DPAD
QK_SCALE = 8.0

DI = 2048
DS = 16
DC = 4
NEXP = 8
EH = 256
ER = 64

FUSED = True

CONST_VALS = [1e-5, 1.0 / 1024.0, 1e-12, -1.0, 0.5, 2.0]
P0_STAGE = int(os.environ.get('P0_STAGE', '2'))


def hbm_registry():
    R = {}
    R["xT"] = ([D, TOK], F32, True)
    R["w_inT"] = ([D, D], F32, True)
    R["w_gateT"] = ([D, D], F32, True)
    R["w_bT"] = ([D, D], F32, True)
    R["w_outT"] = ([D, D], F32, True)
    R["aT"] = ([D, D], F32, True)
    R["b_in"] = ([1, D], F32, True)
    R["b_gate"] = ([1, D], F32, True)
    R["b_out"] = ([1, D], F32, True)
    for l in range(L0DEPTH):
        R[f"wqkv{l}"] = ([D, 3 * INNER], F32, True)
        R[f"wo{l}"] = ([INNER, D], F32, True)
        R[f"w1p{l}"] = ([D, NH1], F32, True)
        R[f"b1p{l}"] = ([128, NH1 // 128], F32, True)
        R[f"w2p{l}"] = ([DPAD, D], F32, True)
        R[f"b2{l}"] = ([128, D // 128], F32, True)
        R[f"qsrep{l}"] = ([128, 1], F32, True)
        R[f"ksrep{l}"] = ([128, 1], F32, True)
        R[f"lng{l}"] = ([128, NCH], F32, True)
        R[f"lnb{l}"] = ([128, NCH], F32, True)
    R["inprojT"] = ([D, 2 * DI], F32, True)
    R["convw"] = ([128, DI // 128 * DC], F32, True)
    R["convb"] = ([128, DI // 128], F32, True)
    R["xprojT"] = ([DI, 33], F32, True)
    R["dtw"] = ([128, DI // 128], F32, True)
    R["dtb"] = ([128, DI // 128], F32, True)
    R["alog"] = ([DI, DS], F32, True)
    R["dvec"] = ([128, DI // 128], F32, True)
    R["outprojT"] = ([DI, D], F32, True)
    R["wbase"] = ([D, D], F32, True)
    R["bbase"] = ([128, D // 128], F32, True)
    R["wd1"] = ([NEXP, D, EH], F32, True)
    R["bd1"] = ([NEXP, 128, EH // 128], F32, True)
    R["wd2"] = ([NEXP, EH, D], F32, True)
    R["bd2"] = ([NEXP, 128, D // 128], F32, True)
    R["wadc"] = ([D, NEXP * ER], F32, True)
    R["badc"] = ([128, NEXP * ER // 128], F32, True)
    R["wauc"] = ([NEXP * ER, D], F32, True)
    R["bauc"] = ([NEXP, D], F32, True)
    R["k_ident"] = ([128, 128], BF16, True)
    R["k_ones1"] = ([1, TT], BF16, True)
    R["k_ones128"] = ([128, 1], BF16, True)
    R["k_onesk1"] = ([1, 128], BF16, True)
    R["k_onesk1f"] = ([1, 128], F32, True)
    R["k_bd2h"] = ([128, 2], BF16, True)
    R["k_sel2"] = ([2, 128], BF16, True)
    R["k_rotT"] = ([128, 128], BF16, True)
    R["k_maskband"] = ([128, 384], BF16, True)
    R["k_sel2f"] = ([2, 128], F32, True)
    R["k_s8"] = ([8, 128], F32, True)
    R["k_m84"] = ([8, 4], F32, True)
    R["k_ones8"] = ([8, TT], F32, True)
    R["k_ones128f"] = ([128, 1], F32, True)
    R["k_maskUT"] = ([128, 128], BF16, True)
    R["k_maskLT"] = ([128, 128], BF16, True)
    R["k_scal"] = ([1, 16], F32, True)
    R["k_scal_b"] = ([1, 16], BF16, True)
    R["t_cosq"] = ([128, SEQ], F32, True)
    R["t_sinq"] = ([128, SEQ], F32, True)
    R["t_cosk"] = ([128, SEQ], F32, True)
    R["t_sink"] = ([128, SEQ], F32, True)
    R["xn_f"] = ([D, TOK], F32, False)
    R["xn_b"] = ([D, TOK], BF16, False)
    R["u_b"] = ([D, TOK], BF16, False)
    R["g_f"] = ([D, TOK], F32, False)
    R["inp_f"] = ([D, TOK], F32, False)
    R["states_b"] = ([D, TOK], BF16, False)
    R["s0_f"] = ([D, TOK], F32, False)
    R["ga_f"] = ([D, TOK], F32, False)
    R["gb_f"] = ([D, TOK], F32, False)
    R["x1_f"] = ([D, TOK], F32, False)
    R["xn1_b"] = ([D, TOK], BF16, False)
    R["u0_b"] = ([DI, TOK], BF16, False)
    R["zs_b"] = ([DI, TOK], BF16, False)
    R["u2_b"] = ([DI, TOK], BF16, False)
    R["dte_f"] = ([DI, TOK], F32, False)
    R["du_b"] = ([DI, TOK], BF16, False)
    R["bc_f"] = ([33, TOK], F32, False)
    R["y_b"] = ([DI, TOK], BF16, False)
    R["s1_f"] = ([D, TOK], F32, False)
    R["outT"] = ([D, TOK], F32, False)
    return R


def build_host_inputs(inp):
    bf = ml_dtypes.bfloat16
    shared = {}
    shared["w_inT"] = np.ascontiguousarray(inp["L0_attn_W_in"].T)
    shared["w_gateT"] = np.ascontiguousarray(inp["L0_attn_W_gate"].T)
    shared["w_bT"] = np.ascontiguousarray(inp["L0_attn_B"].T)
    shared["w_outT"] = np.ascontiguousarray(inp["L0_attn_W_out"].T)
    shared["aT"] = np.ascontiguousarray(inp["L0_attn_A"].T)
    shared["b_in"] = inp["L0_attn_b_in"].reshape(1, D)
    shared["b_gate"] = inp["L0_attn_b_gate"].reshape(1, D)
    shared["b_out"] = inp["L0_attn_b_out"].reshape(1, D)

    shared["wqkv"] = np.ascontiguousarray(inp["L0_ffn_Wqkv"])
    shared["wo"] = np.ascontiguousarray(inp["L0_ffn_Wo"])
    w1 = inp["L0_ffn_W1"]
    b1 = inp["L0_ffn_b1"]
    w1a = np.zeros((L0DEPTH, D, DPAD), np.float32)
    w1g = np.zeros((L0DEPTH, D, DPAD), np.float32)
    w1a[:, :, :DIN] = w1[:, :, :DIN]
    w1g[:, :, :DIN] = w1[:, :, DIN:]
    b1a = np.zeros((L0DEPTH, DPAD), np.float32)
    b1g = np.zeros((L0DEPTH, DPAD), np.float32)
    b1a[:, :DIN] = b1[:, :DIN]
    b1g[:, :DIN] = b1[:, DIN:]
    w1i = np.zeros((L0DEPTH, D, NH1), np.float32)
    b1i = np.zeros((L0DEPTH, 1, NH1), np.float32)
    nblk = DPAD // 128
    for j in range(nblk):
        w1i[:, :, (2 * j) * 128:(2 * j + 1) * 128] = w1a[:, :, j * 128:(j + 1) * 128]
        w1i[:, :, (2 * j + 1) * 128:(2 * j + 2) * 128] = w1g[:, :, j * 128:(j + 1) * 128]
        b1i[:, 0, (2 * j) * 128:(2 * j + 1) * 128] = b1a[:, j * 128:(j + 1) * 128]
        b1i[:, 0, (2 * j + 1) * 128:(2 * j + 2) * 128] = b1g[:, j * 128:(j + 1) * 128]
    shared["w1p"] = w1i
    shared["b1p"] = np.ascontiguousarray(b1i.reshape(L0DEPTH, NH1 // 128, 128).transpose(0, 2, 1))
    w2 = np.zeros((L0DEPTH, DPAD, D), np.float32)
    w2[:, :DIN, :] = inp["L0_ffn_W2"]
    shared["w2p"] = w2
    shared["b2"] = np.ascontiguousarray(
        inp["L0_ffn_b2"].reshape(L0DEPTH, D // 128, 128).transpose(0, 2, 1))
    shared["qsrep"] = np.tile(inp["L0_ffn_q_scale"], (1, 2)).reshape(L0DEPTH, 128, 1)
    shared["ksrep"] = np.tile(inp["L0_ffn_k_scale"], (1, 2)).reshape(L0DEPTH, 128, 1)
    shared["lng"] = np.ascontiguousarray(
        inp["L0_ffn_ln_g"].reshape(L0DEPTH, NCH, 128).transpose(0, 2, 1))
    shared["lnb"] = np.ascontiguousarray(
        inp["L0_ffn_ln_b"].reshape(L0DEPTH, NCH, 128).transpose(0, 2, 1))

    shared["inprojT"] = np.ascontiguousarray(inp["L1_attn_in_proj_w"].T)
    shared["convw"] = np.ascontiguousarray(
        inp["L1_attn_conv_w"][:, 0, :].reshape(DI // 128, 128, DC)
        .transpose(1, 0, 2).reshape(128, DI // 128 * DC))
    shared["convb"] = np.ascontiguousarray(inp["L1_attn_conv_b"].reshape(DI // 128, 128).T)
    shared["xprojT"] = np.ascontiguousarray(inp["L1_attn_x_proj_w"].T)
    shared["dtw"] = np.ascontiguousarray(inp["L1_attn_dt_proj_w"].reshape(DI // 128, 128).T)
    shared["dtb"] = np.ascontiguousarray(inp["L1_attn_dt_proj_b"].reshape(DI // 128, 128).T)
    shared["alog"] = np.ascontiguousarray(inp["L1_attn_A_log"])
    shared["dvec"] = np.ascontiguousarray(inp["L1_attn_D"].reshape(DI // 128, 128).T)
    shared["outprojT"] = np.ascontiguousarray(inp["L1_attn_out_proj_w"].T)

    shared["wbase"] = np.ascontiguousarray(inp["L1_ffn_W_base"])
    shared["bbase"] = np.ascontiguousarray(
        inp["L1_ffn_b_base"].reshape(D // 128, 128).T)
    shared["wd1"] = np.ascontiguousarray(inp["L1_ffn_Wd1"])
    shared["bd1"] = np.ascontiguousarray(
        inp["L1_ffn_bd1"].reshape(NEXP, EH // 128, 128).transpose(0, 2, 1))
    shared["wd2"] = np.ascontiguousarray(inp["L1_ffn_Wd2"])
    shared["bd2"] = np.ascontiguousarray(
        inp["L1_ffn_bd2"].reshape(NEXP, D // 128, 128).transpose(0, 2, 1))
    shared["wadc"] = np.ascontiguousarray(
        inp["L1_ffn_Wa_down"].transpose(1, 0, 2).reshape(D, NEXP * ER))
    shared["badc"] = np.ascontiguousarray(
        inp["L1_ffn_ba_down"].reshape(NEXP * ER // 128, 128).T)
    shared["wauc"] = np.ascontiguousarray(inp["L1_ffn_Wa_up"].reshape(NEXP * ER, D))
    shared["bauc"] = np.ascontiguousarray(inp["L1_ffn_ba_up"])

    shared["k_scal"] = np.zeros((1, 16), np.float32)
    shared["k_scal"][0, :len(CONST_VALS)] = CONST_VALS
    shared["k_scal_b"] = shared["k_scal"].astype(bf)
    shared["k_ident"] = np.eye(128, dtype=np.float32).astype(bf)
    shared["k_ones1"] = np.ones((1, TT), np.float32).astype(bf)
    shared["k_ones128"] = np.ones((128, 1), np.float32).astype(bf)
    shared["k_onesk1"] = np.ones((1, 128), np.float32).astype(bf)
    shared["k_onesk1f"] = np.ones((1, 128), np.float32)
    bd2h = np.zeros((128, 2), np.float32)
    bd2h[:64, 0] = 1.0
    bd2h[64:, 1] = 1.0
    shared["k_bd2h"] = bd2h.astype(bf)
    shared["k_sel2"] = np.ascontiguousarray(bd2h.T).astype(bf)
    shared["k_sel2f"] = np.ascontiguousarray(bd2h.T).astype(np.float32)
    s8 = np.zeros((8, 128), np.float32)
    for e in range(8):
        s8[e, (e % 2) * 64:(e % 2) * 64 + 64] = 1.0
    shared["k_s8"] = s8
    m84 = np.zeros((8, 4), np.float32)
    for e in range(8):
        m84[e, e // 2] = 1.0
    shared["k_m84"] = m84
    shared["k_ones8"] = np.ones((8, TT), np.float32)
    shared["k_ones128f"] = np.ones((128, 1), np.float32)
    mb = np.zeros((128, 384), np.float32)
    ii = np.arange(128)[:, None]
    jj = np.arange(128)[None, :]
    mb[:, 0:128] = np.where(jj >= ii, 0.0, -1e30)
    mb[:, 256:384] = np.where(jj <= ii, 0.0, -1e30)
    shared["k_maskband"] = mb
    r1 = np.zeros((64, 64), np.float32)
    r1[0:32, 32:64] = -np.eye(32)
    r1[32:64, 0:32] = np.eye(32)
    R = np.zeros((128, 128), np.float32)
    R[:64, :64] = r1
    R[64:, 64:] = r1
    shared["k_rotT"] = np.ascontiguousarray(R.T).astype(bf)
    mm, ii = np.meshgrid(np.arange(128), np.arange(128), indexing="ij")
    shared["k_maskUT"] = (mm >= ii).astype(np.float32).astype(bf)
    shared["k_maskLT"] = (mm <= ii).astype(np.float32).astype(bf)
    half = np.arange(0, DH, 2, dtype=np.float32)
    inv_freq = 1.0 / 10000.0 ** (half / DH)
    t = np.arange(SEQ, dtype=np.float32)
    freqs = t[:, None] * inv_freq[None, :]
    freqs = np.concatenate([freqs, freqs], -1)
    base_scale = (half + 0.4 * DH) / (1.4 * DH)
    power = (t - SEQ // 2) / (WIN // 2)
    scale = base_scale[None, :] ** power[:, None]
    scale = np.concatenate([scale, scale], -1)
    cos = np.cos(freqs)
    sin = np.sin(freqs)
    sig = np.concatenate([np.arange(32, 64), np.arange(0, 32)])
    def dup(a):
        return np.ascontiguousarray(np.tile(a.T, (2, 1)).astype(np.float32))
    shared["t_cosq"] = dup(cos * scale * QK_SCALE)
    shared["t_sinq"] = dup((sin * scale)[:, sig] * QK_SCALE)
    shared["t_cosk"] = dup(cos / scale)
    shared["t_sink"] = dup((sin / scale)[:, sig])

    for nm in ("wqkv", "wo", "w1p", "b1p", "w2p", "b2", "qsrep", "ksrep", "lng", "lnb"):
        full = shared.pop(nm)
        for l in range(L0DEPTH):
            shared[f"{nm}{l}"] = np.ascontiguousarray(full[l])

    x = np.asarray(inp["L0_attn_x"], np.float32)
    cores = []
    for c in range(NCORE):
        xc = x[c * BPC:(c + 1) * BPC]
        xT = np.ascontiguousarray(xc.transpose(2, 0, 1).reshape(D, TOK))
        d = dict(shared)
        d["xT"] = xT
        cores.append(d)
    return cores


class Prog:
    def __init__(self, nc, block, stack, hbm, pooled=False):
        self.nc = nc
        self.block = block
        self.es = stack
        self.hbm = hbm
        self._n = 0
        self._chain = {}
        self.pooled = pooled
        self._pool = []
        self._pool_i = 0
        self._barA = None
        self._barB = None
        self._nbound = 0

    def phase_begin(self):
        self._pool_i = 0

    def init_pool(self, n=64):
        self._pool = [self.es.enter_context(self.nc.semaphore(f"pool_{i}"))
                      for i in range(n)]
        for i, en in enumerate(("vector", "scalar", "gpsimd")):
            self._chain[en] = [self.es.enter_context(self.nc.semaphore(f"chain_{en}")), 0]
        nums = [s.num for s in self._pool] + [c[0].num for c in self._chain.values()]
        lo, hi = min(nums), max(nums)
        assert hi - lo + 1 == len(nums), "sem pool not contiguous"
        self._pool_range = range(lo, hi + 1)
        self._barA = self.es.enter_context(self.nc.semaphore("barA"))
        self._barB = self.es.enter_context(self.nc.semaphore("barB"))

    def phase_boundary(self):
        b = self.block
        self._nbound += 1
        k = self._nbound
        barA, barB = self._barA, self._barB
        rng = self._pool_range

        def f_gp(eng):
            eng.drain()
            eng.sem_inc(barA, 1)
            eng.wait_ge(barA, 5 * k)
            eng.dma_reset(rng)
            eng.sem_clear(rng)
            eng.sem_inc(barB, 1)
            eng.wait_ge(barB, k)

        def f_other(eng):
            eng.drain()
            eng.sem_inc(barA, 1)
            eng.wait_ge(barB, k)

        b.tensor(f_other)
        b.scalar(f_other)
        b.vector(f_other)
        b.sync(f_other)
        b.gpsimd(f_gp)
        for s in self._chain.values():
            s[1] = 0
        self._pool_i = 0

    def mark(self, ename, ins):
        if ename not in self._chain:
            self._chain[ename] = [self.sem(f"ch_{ename}"), 0]
        s = self._chain[ename]
        ins.then_inc(s[0], 1)
        s[1] += 1
        return s[1]

    def mark_nop(self, eng, ename):
        if ename not in self._chain:
            self._chain[ename] = [self.sem(f"ch_{ename}"), 0]
        s = self._chain[ename]
        eng.nop().then_inc(s[0], 1)
        s[1] += 1
        return s[1]

    def chain_count(self, ename):
        return self._chain[ename][1] if ename in self._chain else 0

    def chain_sem(self, ename):
        if ename not in self._chain:
            self._chain[ename] = [self.sem(f"ch_{ename}"), 0]
        return self._chain[ename][0]

    def join(self, eng, ename, n=None):
        if ename not in self._chain:
            return
        s = self._chain[ename]
        eng.wait_ge(s[0], s[1] if n is None else n)

    def uid(self):
        self._n += 1
        return self._n

    def sem(self, name="s"):
        if self.pooled:
            if self._pool_i < len(self._pool):
                s = self._pool[self._pool_i]
            else:
                s = self.es.enter_context(self.nc.semaphore(f"pool_{len(self._pool)}"))
                self._pool.append(s)
            self._pool_i += 1
            return s
        return self.es.enter_context(self.nc.semaphore(f"{name}_{self.uid()}"))

    def sbuf(self, shape, dt, name="t", stack=None):
        st = stack if stack is not None else self.es
        return st.enter_context(self.nc.sbuf_tensor(f"{name}_{self.uid()}", list(shape), dt))

    def psum(self, shape, dt, name="p", stack=None):
        st = stack if stack is not None else self.es
        return st.enter_context(self.nc.psum_tensor(f"{name}_{self.uid()}", list(shape), dt))


def emit_setup(P):
    nc, b = P.nc, P.block
    P.const_sb = P.sbuf([1, 16], F32, "kscal")
    P.const_sbb = P.sbuf([1, 16], BF16, "kscalb")
    kready = P.sem("kready")

    @b.sync
    def _(sync):
        sync.dma_start(out=P.const_sb[:], in_=P.hbm["k_scal"]).then_inc(kready, 16)
        sync.dma_start(out=P.const_sbb[:], in_=P.hbm["k_scal_b"]).then_inc(kready, 16)

    for attr in ("tensor", "scalar", "vector", "gpsimd"):
        def f(eng):
            eng.wait_ge(kready, 32)
        getattr(b, attr)(f)
    for i, v in enumerate(CONST_VALS):
        nc.const_aps.register(F32, float(v), P.const_sb[0:1, i:i + 1])
        nc.const_aps.register(BF16, float(v), P.const_sbb[0:1, i:i + 1])


def emit_barrier(P):
    bar = P.sem("bar")
    b = P.block
    for attr in ("sync", "tensor", "scalar", "vector", "gpsimd"):
        def f(eng):
            eng.sem_inc(bar, 1)
            eng.wait_ge(bar, 5)
        getattr(b, attr)(f)


def p0_ln_proj(P):
    nc, b, H = P.nc, P.block, P.hbm
    with contextlib.ExitStack() as S:
        wload = P.sbuf([128, 2 * 1024], F32, "p0wl", S)
        w_in = P.sbuf([128, NCH * 1024], BF16, "p0win", S)
        w_gate = P.sbuf([128, NCH * 1024], BF16, "p0wg", S)
        w_bm = P.sbuf([128, NCH * 1024], BF16, "p0wb", S)
        biases = P.sbuf([1, 3 * D], BF16, "p0bias", S)
        bias_f = P.sbuf([1, 3 * D], F32, "p0biasf", S)
        ones1 = P.sbuf([1, TT], BF16, "p0ones", S)
        ones128 = P.sbuf([128, 1], BF16, "p0o128", S)
        onesk1 = P.sbuf([1, 128], BF16, "p0ok1", S)
        onesk1f = P.sbuf([1, 128], F32, "p0ok1f", S)
        xf = P.sbuf([128, NCH * TT], F32, "p0xf", S)
        xb = P.sbuf([128, NCH * TT], BF16, "p0xb", S)
        sq = P.sbuf([128, NCH * TT], BF16, "p0sq", S)
        t1 = P.sbuf([128, TT], F32, "p0t1", S)
        xnf = P.sbuf([128, NCH * TT], F32, "p0xnf", S)
        xnb = P.sbuf([128, NCH * TT], BF16, "p0xnb", S)
        inp_b = P.sbuf([128, NCH * TT], BF16, "p0inp", S)
        inp_f = P.sbuf([128, NCH * TT], F32, "p0inpf", S)
        gate_f = P.sbuf([128, NCH * TT], F32, "p0gate", S)
        u_bs = P.sbuf([128, NCH * TT], BF16, "p0u", S)
        stat = P.sbuf([1, 6 * TT], F32, "p0stat", S)
        stat_b = P.sbuf([1, 2 * TT], BF16, "p0statb", S)

        p_sum = P.psum([1, TT], F32, "p0psA", S)
        p_sq = P.psum([1, TT], F32, "p0psB", S)
        p_mu = P.psum([128, TT], F32, "p0pmu", S)
        p_rs = P.psum([128, TT], F32, "p0prs", S)
        p_m = [P.psum([128, TT], F32, "p0pm", S) for _ in range(2)]

        ld = P.sem("p0ld")
        kld = P.sem("p0kld")
        wldp = [P.sem("p0wld0"), P.sem("p0wld1")]
        wcast = P.sem("p0wcast")
        xcast = P.sem("p0xcast")
        smm = P.sem("p0smm")
        tail = P.sem("p0tail")
        bmm = P.sem("p0bmm")
        appl = P.sem("p0appl")
        pj = P.sem("p0pj")
        pje = P.sem("p0pje")
        st = P.sem("p0st")

        NST = 40 * 16

        @b.sync
        def _(sync):
            sync.dma_start(out=bias_f[:, 0:D], in_=H["b_in"]).then_inc(kld, 16)
            sync.dma_start(out=bias_f[:, D:2 * D], in_=H["b_gate"]).then_inc(kld, 16)
            sync.dma_start(out=ones1[:], in_=H["k_ones1"]).then_inc(kld, 16)
            sync.dma_start(out=ones128[:], in_=H["k_ones128"]).then_inc(kld, 16)
            sync.dma_start(out=onesk1[:], in_=H["k_onesk1"]).then_inc(kld, 16)
            sync.dma_start(out=onesk1f[:], in_=H["k_onesk1f"]).then_inc(kld, 16)
            for i, wn in enumerate(("w_inT", "w_gateT", "w_bT")):
                for k in range(NCH):
                    j = i * NCH + k
                    if j >= 2:
                        sync.wait_ge(wcast, j - 1)
                    sync.dma_start(out=wload[:, (j % 2) * 1024:(j % 2 + 1) * 1024],
                                   in_=H[wn][k * 128:(k + 1) * 128, :]).then_inc(wldp[j % 2], 16)
            for tt in range(NTT):
                if tt >= 1:
                    if P0_STAGE >= 1:
                        sync.wait_ge(appl, tt * 8)
                    else:
                        sync.wait_ge(xcast, tt * 8)
                for ch in range(NCH):
                    sync.dma_start(out=xf[:, ch * TT:(ch + 1) * TT],
                                   in_=H["xT"][ch * 128:(ch + 1) * 128, tt * TT:(tt + 1) * TT]
                                   ).then_inc(ld, 16)

        @b.gpsimd
        def _(gpsimd):
            gpsimd.wait_ge(kld, 16 * 6)
            gpsimd.tensor_copy(biases[:, 0:2 * D], bias_f[:, 0:2 * D])
            dsts = [w_in, w_gate, w_bm]
            for i in range(3):
                for k in range(NCH):
                    j = i * NCH + k
                    gpsimd.wait_ge(wldp[j % 2], 16 * (j // 2 + 1))
                    gpsimd.tensor_copy(dsts[i][:, k * 1024:(k + 1) * 1024],
                                       wload[:, (j % 2) * 1024:(j % 2 + 1) * 1024]
                                       ).then_inc(wcast, 1)

        @b.vector
        def _(vector):
            for tt in range(NTT):
                P.join(vector, "vector")
                if tt >= 1 and P0_STAGE >= 1:
                    vector.wait_ge(smm, 2 * tt - 1)
                    vector.wait_ge(tail, 5 * (tt - 1) + 1)
                    vector.wait_ge(st, (NST if P0_STAGE >= 2 else 16 * 16) * tt)
                vector.wait_ge(ld, 16 * NCH * (tt + 1))
                for ch in range(NCH):
                    vector.tensor_copy(xb[:, ch * TT:(ch + 1) * TT],
                                       xf[:, ch * TT:(ch + 1) * TT]).then_inc(xcast, 1)
                if P0_STAGE < 1:
                    continue
                vector.wait_ge(tail, 5 * tt + 2)
                vector.tensor_sub(stat[:, 3 * TT:4 * TT], stat[:, 2 * TT:3 * TT],
                                  stat[:, 1 * TT:2 * TT]).then_inc(tail, 1)
                vector.wait_ge(tail, 5 * tt + 4)
                vector.reciprocal(stat[:, 5 * TT:6 * TT],
                                  stat[:, 4 * TT:5 * TT]).then_inc(tail, 1)
                vector.wait_ge(bmm, tt * 2 + 2)
                for ch in range(NCH):
                    P.join(vector, "vector")
                    ins = vector.tensor_sub(t1[:], xf[:, ch * TT:(ch + 1) * TT], p_mu[:, :])
                    P.mark("vector", ins)
                    P.join(vector, "vector")
                    ins = vector.tensor_mul(xnf[:, ch * TT:(ch + 1) * TT], t1[:], p_rs[:, :])
                    P.mark("vector", ins)
                    P.join(vector, "vector")
                    vector.tensor_copy(xnb[:, ch * TT:(ch + 1) * TT],
                                       xnf[:, ch * TT:(ch + 1) * TT]).then_inc(appl, 1)

        @b.scalar
        def _(scalar):
            for tt in range(NTT):
                if P0_STAGE < 1:
                    continue
                P.join(scalar, "scalar")
                for ch in range(NCH):
                    scalar.wait_ge(xcast, tt * NCH + ch + 1)
                    ins = scalar.activation(sq[:, ch * TT:(ch + 1) * TT],
                                            xb[:, ch * TT:(ch + 1) * TT], AF.Square)
                ins.then_inc(tail, 1)
                scalar.wait_ge(smm, tt * 2 + 1)
                ins = scalar.activation(stat[:, 0:TT], p_sum[:, :], AF.Identity, scale=1.0 / D)
                P.mark("scalar", ins)
                P.join(scalar, "scalar")
                scalar.activation(stat[:, 1 * TT:2 * TT], stat[:, 0:TT], AF.Square)
                scalar.wait_ge(smm, tt * 2 + 2)
                scalar.activation(stat[:, 2 * TT:3 * TT], p_sq[:, :], AF.Identity,
                                  scale=1.0 / D).then_inc(tail, 1)
                scalar.wait_ge(tail, 5 * tt + 3)
                scalar.activation(stat[:, 4 * TT:5 * TT], stat[:, 3 * TT:4 * TT],
                                  AF.Sqrt, bias=1e-5).then_inc(tail, 1)
                if P0_STAGE < 2:
                    continue
                if tt >= 1:
                    scalar.wait_ge(st, NST * tt)
                for m in range(NCH):
                    scalar.wait_ge(pj, tt * 24 + m + 1)
                    ins = scalar.activation(inp_f[:, m * TT:(m + 1) * TT],
                                            p_m[m % 2][:, :], AF.Identity)
                    P.mark("scalar", ins)
                    scalar.activation(inp_b[:, m * TT:(m + 1) * TT],
                                      p_m[m % 2][:, :], AF.Identity).then_inc(pje, 1)
                for m in range(NCH):
                    scalar.wait_ge(pj, tt * 24 + NCH + m + 1)
                    scalar.activation(gate_f[:, m * TT:(m + 1) * TT],
                                      p_m[m % 2][:, :], AF.Sigmoid).then_inc(pje, 1)
                for m in range(NCH):
                    scalar.wait_ge(pj, tt * 24 + 2 * NCH + m + 1)
                    scalar.activation(u_bs[:, m * TT:(m + 1) * TT],
                                      p_m[m % 2][:, :], AF.Identity).then_inc(pje, 1)

        @b.tensor
        def _(tensor):
            tensor.wait_ge(wcast, 24)
            tensor.wait_ge(kld, 16 * 6)
            for tt in range(NTT):
                if P0_STAGE < 1:
                    continue
                if tt >= 1:
                    tensor.wait_ge(tail, 5 * (tt - 1) + 2)
                for ch in range(NCH):
                    tensor.wait_ge(xcast, tt * NCH + ch + 1)
                    ins = tensor.matmul(p_sum[:, :], ones128[:],
                                        xb[:, ch * TT:(ch + 1) * TT],
                                        start=(ch == 0), stop=(ch == NCH - 1))
                ins.then_inc(smm, 1)
                tensor.wait_ge(tail, 5 * tt + 1)
                for ch in range(NCH):
                    ins = tensor.matmul(p_sq[:, :], ones128[:],
                                        sq[:, ch * TT:(ch + 1) * TT],
                                        start=(ch == 0), stop=(ch == NCH - 1))
                ins.then_inc(smm, 1)
                tensor.wait_ge(tail, 5 * tt + 5)
                tensor.matmul(p_mu[:, :], onesk1f[:], stat[:, 0:TT],
                              start=True, stop=True).then_inc(bmm, 1)
                tensor.matmul(p_rs[:, :], onesk1f[:], stat[:, 5 * TT:6 * TT],
                              start=True, stop=True).then_inc(bmm, 1)
                tensor.wait_ge(appl, tt * NCH + NCH)
                if P0_STAGE < 2:
                    continue
                for wi, (wt, bias_off, src) in enumerate((
                        (w_in, 0, xnb), (w_gate, D, inp_b), (w_bm, None, inp_b))):
                    for m in range(NCH):
                        pm = p_m[m % 2]
                        gstep = tt * 24 + wi * NCH + m
                        if gstep >= 2:
                            tensor.wait_ge(pje, gstep - 1)
                        if wi > 0 and m == 0:
                            tensor.wait_ge(pje, tt * 24 + wi * NCH)
                        for k in range(NCH):
                            ins = tensor.matmul(
                                pm[:, :],
                                wt[:, k * 1024 + m * 128:k * 1024 + (m + 1) * 128],
                                src[:, k * TT:(k + 1) * TT],
                                start=(k == 0),
                                stop=(bias_off is None and k == NCH - 1))
                        if bias_off is not None:
                            ins = tensor.matmul(
                                pm[:, :],
                                biases[:, bias_off + m * 128:bias_off + (m + 1) * 128],
                                ones1[:], start=False, stop=True)
                        ins.then_inc(pj, 1)

        @b.gpsimd
        def _(sync):
            for tt in range(NTT):
                if P0_STAGE < 1:
                    continue
                sync.wait_ge(appl, tt * NCH + NCH)
                for ch in range(NCH):
                    sync.dma_start(out=H["xn_f"][ch * 128:(ch + 1) * 128, tt * TT:(tt + 1) * TT],
                                   in_=xnf[:, ch * TT:(ch + 1) * TT]).then_inc(st, 16)
                    sync.dma_start(out=H["xn_b"][ch * 128:(ch + 1) * 128, tt * TT:(tt + 1) * TT],
                                   in_=xnb[:, ch * TT:(ch + 1) * TT]).then_inc(st, 16)
                if P0_STAGE < 2:
                    continue
                sync.wait_ge(pje, tt * 24 + 24)
                for ch in range(NCH):
                    sync.dma_start(out=H["u_b"][ch * 128:(ch + 1) * 128, tt * TT:(tt + 1) * TT],
                                   in_=u_bs[:, ch * TT:(ch + 1) * TT]).then_inc(st, 16)
                    sync.dma_start(out=H["g_f"][ch * 128:(ch + 1) * 128, tt * TT:(tt + 1) * TT],
                                   in_=gate_f[:, ch * TT:(ch + 1) * TT]).then_inc(st, 16)
                    sync.dma_start(out=H["inp_f"][ch * 128:(ch + 1) * 128, tt * TT:(tt + 1) * TT],
                                   in_=inp_f[:, ch * TT:(ch + 1) * TT]).then_inc(st, 16)
            if P0_STAGE >= 1:
                sync.wait_ge(st, (NST if P0_STAGE >= 2 else 16 * 16) * NTT)
            else:
                sync.wait_ge(xcast, NTT * NCH)


def emit_proj(P, src_name, w_name, bias_name, out_specs, Kdim, Mdim, act=None):
    nc, b, H = P.nc, P.block, P.hbm
    KC = Kdim // 128
    MC = Mdim // 128
    nout = len(out_specs)
    with contextlib.ExitStack() as S:
        wload = P.sbuf([128, 2 * Mdim], F32, "pjwl", S)
        wsb = P.sbuf([128, KC * Mdim], BF16, "pjw", S)
        bias_f = P.sbuf([1, Mdim], F32, "pjbf", S) if bias_name else None
        bias_b = P.sbuf([1, Mdim], BF16, "pjbb", S) if bias_name else None
        ones1 = P.sbuf([1, TT], BF16, "pjones", S)
        xsb = P.sbuf([128, KC * TT], BF16, "pjx", S)
        osb = [P.sbuf([128, 2 * TT], d, "pjo", S) for (_, d) in out_specs]
        p_m = [P.psum([128, TT], F32, "pjp", S) for _ in range(2)]

        kld = P.sem("pjkld")
        wldp = [P.sem("pjw0"), P.sem("pjw1")]
        wcast = P.sem("pjwc")
        ld = P.sem("pjld")
        pj = P.sem("pjpj")
        pje = P.sem("pjpje")
        stp2 = [P.sem("pjs0"), P.sem("pjs1")]
        nkon = 2 if bias_name else 1

        @b.sync
        def _(sync):
            if bias_name:
                sync.dma_start(out=bias_f[:], in_=H[bias_name]).then_inc(kld, 16)
            sync.dma_start(out=ones1[:], in_=H["k_ones1"]).then_inc(kld, 16)
            for j in range(KC * MC):
                kc, mc = j // MC, j % MC
                if j >= 2:
                    sync.wait_ge(wcast, j - 1)
                sync.dma_start(out=wload[:, (j % 2) * Mdim + mc * 128:
                                         (j % 2) * Mdim + (mc + 1) * 128],
                               in_=H[w_name][kc * 128:(kc + 1) * 128, mc * 128:(mc + 1) * 128]
                               ).then_inc(wldp[j % 2], 16)
            for tt in range(NTT):
                if tt >= 1:
                    sync.wait_ge(pj, tt * MC)
                for kc in range(KC):
                    sync.dma_start(out=xsb[:, kc * TT:(kc + 1) * TT],
                                   in_=H[src_name][kc * 128:(kc + 1) * 128,
                                                   tt * TT:(tt + 1) * TT]).then_inc(ld, 16)

        @b.gpsimd
        def _(gpsimd):
            if bias_name:
                gpsimd.wait_ge(kld, 32)
                gpsimd.tensor_copy(bias_b[:], bias_f[:])
            for j in range(KC * MC):
                kc, mc = j // MC, j % MC
                gpsimd.wait_ge(wldp[j % 2], 16 * (j // 2 + 1))
                gpsimd.tensor_copy(wsb[:, kc * Mdim + mc * 128:kc * Mdim + (mc + 1) * 128],
                                   wload[:, (j % 2) * Mdim + mc * 128:
                                         (j % 2) * Mdim + (mc + 1) * 128]).then_inc(wcast, 1)
            for tt in range(NTT):
                for mc in range(MC):
                    g = tt * MC + mc
                    gpsimd.wait_ge(pje, g + 1)
                    for oi, (oname, _) in enumerate(out_specs):
                        gpsimd.dma_start(
                            out=H[oname][mc * 128:(mc + 1) * 128, tt * TT:(tt + 1) * TT],
                            in_=osb[oi][:, (g % 2) * TT:(g % 2 + 1) * TT]
                        ).then_inc(stp2[g % 2], 16)
            NG = NTT * MC
            gpsimd.wait_ge(stp2[0], 16 * nout * ((NG + 1) // 2))
            gpsimd.wait_ge(stp2[1], 16 * nout * (NG // 2))

        @b.tensor
        def _(tensor):
            tensor.wait_ge(wcast, KC * MC)
            tensor.wait_ge(kld, 16 * nkon)
            for tt in range(NTT):
                tensor.wait_ge(ld, 16 * KC * (tt + 1))
                for mc in range(MC):
                    pm = p_m[mc % 2]
                    g = tt * MC + mc
                    if g >= 2:
                        tensor.wait_ge(pje, g - 1)
                    for kc in range(KC):
                        ins = tensor.matmul(
                            pm[:, :], wsb[:, kc * Mdim + mc * 128:kc * Mdim + (mc + 1) * 128],
                            xsb[:, kc * TT:(kc + 1) * TT],
                            start=(kc == 0), stop=(bias_name is None and kc == KC - 1))
                    if bias_name:
                        ins = tensor.matmul(pm[:, :], bias_b[:, mc * 128:(mc + 1) * 128],
                                            ones1[:], start=False, stop=True)
                    ins.then_inc(pj, 1)

        @b.scalar
        def _(scalar):
            fn = act if act is not None else AF.Identity
            for tt in range(NTT):
                for mc in range(MC):
                    g = tt * MC + mc
                    scalar.wait_ge(pj, g + 1)
                    if g >= 2:
                        scalar.wait_ge(stp2[g % 2], 16 * nout * (g // 2))
                    if nout == 1:
                        scalar.activation(osb[0][:, (g % 2) * TT:(g % 2 + 1) * TT],
                                          p_m[mc % 2][:, :], fn).then_inc(pje, 1)
                    else:
                        ins0 = scalar.activation(osb[0][:, (g % 2) * TT:(g % 2 + 1) * TT],
                                                 p_m[mc % 2][:, :], fn)
                        P.mark("scalar", ins0)
                        scalar.activation(osb[1][:, (g % 2) * TT:(g % 2 + 1) * TT],
                                          p_m[mc % 2][:, :], fn).then_inc(pje, 1)


def l0_attn_consts(l):
    return [f"wqkv{l}", f"wo{l}", f"qsrep{l}", f"ksrep{l}", "t_cosq", "t_sinq",
            "t_cosk", "t_sink", "k_ident", "k_rotT", "k_bd2h", "k_sel2f", "k_maskband"]


def emit_l0_attn(P, l, src_name, dst_name):
    nc, b, H = P.nc, P.block, P.hbm
    NU = HEADS * 2 * 2
    NPAIR = NU // 2
    with contextlib.ExitStack() as S:
        wload = P.sbuf([128, 2 * 128], F32, "alwl", S)
        wq = P.sbuf([128, NCH * 3 * INNER], BF16, "alwq", S)
        wos = P.sbuf([128, NCH * D], BF16, "alwo", S)
        tcq = P.sbuf([128, 2 * TT], F32, "altcq", S)
        tsq = P.sbuf([128, 2 * TT], F32, "altsq", S)
        tck = P.sbuf([128, 2 * TT], F32, "altck", S)
        tsk = P.sbuf([128, 2 * TT], F32, "altsk", S)
        qscale = P.sbuf([128, 2], F32, "alqs", S)
        iden = P.sbuf([128, 128], BF16, "alid", S)
        rotT = P.sbuf([128, 128], BF16, "alrot", S)
        bd2h = P.sbuf([128, 2], BF16, "albd", S)
        sel2f = P.sbuf([2, 128], F32, "alsel", S)
        mband = P.sbuf([128, 384], BF16, "almb", S)
        xg = P.sbuf([128, NCH * 513], F32, "alxg", S)
        xsb = P.sbuf([128, NCH * TT], BF16, "alxs", S)
        qf = P.sbuf([128, 4 * TT], F32, "alqf", S)
        kf = P.sbuf([128, 4 * TT], F32, "alkf", S)
        vb = P.sbuf([128, NCH * TT], BF16, "alvb", S)
        sqs = P.sbuf([128, 2 * TT], BF16, "alsq", S)
        nsb = P.sbuf([2, 2 * TT], F32, "alnsb", S)
        p2f = P.sbuf([128, 2 * TT], F32, "alp2", S)
        m1b = P.sbuf([128, 2 * TT], BF16, "alm1", S)
        m2b = P.sbuf([128, 2 * TT], BF16, "alm2", S)
        qhat = P.sbuf([128, NCH * TT], BF16, "alqh", S)
        kspan = P.sbuf([128, NCH * 768], BF16, "alks", S)
        vt = P.sbuf([128, 6 * HEADS * DH], BF16, "alvt", S)
        mxn = P.sbuf([128, 4], F32, "almx", S)
        pbb = P.sbuf([128, 3 * 384], BF16, "alpb", S)
        rinv = P.sbuf([128, 2], F32, "alri", S)
        ptb = P.sbuf([128, 2 * 384], BF16, "alpt", S)
        avb = P.sbuf([128, NCH * TT], BF16, "alav", S)
        gout = P.sbuf([128, 2 * TT], F32, "algo", S)

        BK = [P.psum([128, 512], F32, "albk", S) for _ in range(8)]
        pm = [BK[0], BK[1]]
        nrmv = BK[2][0:2, :]
        rnbv = BK[3]
        scv = [BK[4][:, 0:384], BK[5][:, 0:384], BK[2][:, 0:384]]
        vtv = [BK[4][:, :].bitcast(BF16)[:, 0:64], BK[5][:, :].bitcast(BF16)[:, 0:64]]
        ptv = [BK[6][:, :].bitcast(BF16), BK[3][:, :].bitcast(BF16)]
        avv = BK[7][0:64, :]

        kld = P.sem("akld")
        wldp = [P.sem("aw0"), P.sem("aw1")]
        wcast = P.sem("awc")
        ld = P.sem("ald")
        xs_done = P.sem("axs")
        qmm = P.sem("aqmm")
        qkev = P.sem("aqkev")
        vev = P.sem("avev")
        psq = P.sem("apsq")
        pnm = P.sem("apnm")
        nsq = P.sem("ansq")
        nrn = P.sem("anrn")
        prnb = P.sem("aprnb")
        pp2 = P.sem("app2")
        pm12 = P.sem("apm12")
        prot = P.sem("aprot")
        pqh = P.sem("apqh")
        tcp = P.sem("atcp")
        ptv_s = P.sem("aptv")
        pve = P.sem("apve")
        aqk = P.sem("aaqk")
        amask = P.sem("aamask")
        aexp = P.sem("aaexp")
        asum = P.sem("aasum")
        apt = P.sem("aapt")
        apte = P.sem("aapte")
        aav = P.sem("aaav")
        ave = P.sem("aave")
        wmm = P.sem("awmm")
        woe = P.sem("awoe")
        stp2 = [P.sem("ast0"), P.sem("ast1")]

        EORD = [0, 1, 2, 3, 8, 9, 10, 11, 4, 12, 5, 13, 6, 14, 7, 15]
        QKEV_POS = {m: i + 1 for i, m in enumerate(EORD)}
        GATED = [4, 12, 5, 13, 6, 14, 7, 15]
        QKV_ORD = [0, 1, 2, 3, 8, 9, 10, 11, 16, 17, 18, 19, 20, 21, 22, 23] + GATED
        QKV_POS = {m: i for i, m in enumerate(QKV_ORD)}

        def units_for(tts):
            out = []
            for h in range(HEADS):
                for w in range(2):
                    for ic in range(2):
                        qo = w * 256 + ic * 128
                        start = 256 if (tts == 0 and w == 0) else qo
                        width = qo + 384 - start
                        chunks = [start // 128 + j for j in range(width // 128)]
                        out.append((h, w, ic, qo, start, width, chunks))
            return out

        UN = [units_for(0), units_for(1)]

        def unit(tt, ui):
            return UN[min(tt % NTTS, 1)][ui]

        NW_Q = NCH * 24
        NW_ALL = NW_Q + NCH * NCH

        @b.sync
        def _(sync):
            for nm, dst in (("k_ident", iden), ("k_rotT", rotT),
                            ("k_bd2h", bd2h), ("k_sel2f", sel2f), ("k_maskband", mband)):
                sync.dma_start(out=dst[:], in_=H[nm]).then_inc(kld, 16)
            sync.dma_start(out=qscale[:, 0:1], in_=H[f"qsrep{l}"]).then_inc(kld, 16)
            sync.dma_start(out=qscale[:, 1:2], in_=H[f"ksrep{l}"]).then_inc(kld, 16)

            def emit_acts(tt):
                tts = tt % NTTS
                tpar = tt % 2
                for nm, dst in (("t_cosq", tcq), ("t_sinq", tsq), ("t_cosk", tck),
                                ("t_sink", tsk)):
                    sync.dma_start(out=dst[:, tpar * TT:(tpar + 1) * TT],
                                   in_=H[nm][:, tts * TT:(tts + 1) * TT]).then_inc(ld, 16)
                for ch in range(NCH):
                    if tts == 0:
                        sync.dma_start(out=xg[:, ch * 513 + 1:(ch + 1) * 513],
                                       in_=H[src_name][ch * 128:(ch + 1) * 128,
                                                       tt * TT:(tt + 1) * TT]).then_inc(ld, 16)
                    else:
                        sync.dma_start(out=xg[:, ch * 513:(ch + 1) * 513],
                                       in_=H[src_name][ch * 128:(ch + 1) * 128,
                                                       tt * TT - 1:(tt + 1) * TT]).then_inc(ld, 16)

            emit_acts(0)
            for j in range(NW_ALL):
                if j >= 2:
                    sync.wait_ge(wcast, j - 1)
                if j < NW_Q:
                    pos, kc = j // NCH, j % NCH
                    m = QKV_ORD[pos]
                    srcw = H[f"wqkv{l}"][kc * 128:(kc + 1) * 128, m * 128:(m + 1) * 128]
                else:
                    mc, kc = (j - NW_Q) // NCH, (j - NW_Q) % NCH
                    srcw = H[f"wo{l}"][kc * 128:(kc + 1) * 128, mc * 128:(mc + 1) * 128]
                sync.dma_start(out=wload[:, (j % 2) * 128:(j % 2 + 1) * 128],
                               in_=srcw).then_inc(wldp[j % 2], 16)
            for tt in range(1, NTT):
                sync.wait_ge(woe, NCH * tt)
                if tt >= 2:
                    sync.wait_ge(pm12, 16 * (tt - 1))
                emit_acts(tt)

        @b.gpsimd
        def _(gpsimd):
            def emit_head(tt):
                tts = tt % NTTS
                gpsimd.wait_ge(ld, 16 * 12 * (tt + 1))
                if tt >= 1:
                    gpsimd.wait_ge(qmm, 24 * tt)
                if tts == 0:
                    for ch in range(NCH):
                        ins = gpsimd.memset(xg[:, ch * 513:ch * 513 + 1], 0.0)
                    P.mark("gpsimd", ins)
                    P.join(gpsimd, "gpsimd")
                gpsimd.wait_ge(aav, NU * tt)
                if tts != 0:
                    for ch in range(NCH):
                        gpsimd.tensor_copy(kspan[:, ch * 768:ch * 768 + 256],
                                           kspan[:, ch * 768 + 512:(ch + 1) * 768]
                                           ).then_inc(tcp, 1)
                    gpsimd.tensor_copy(vt[:, 0:2 * HEADS * DH],
                                       vt[:, 4 * HEADS * DH:6 * HEADS * DH]).then_inc(tcp, 1)
                else:
                    for _i in range(9):
                        gpsimd.nop().then_inc(tcp, 1)
                for ch in range(NCH):
                    off = 1 if ch < 4 else 0
                    gpsimd.tensor_copy(xsb[:, ch * TT:(ch + 1) * TT],
                                       xg[:, ch * 513 + off:ch * 513 + off + TT]
                                       ).then_inc(xs_done, 1)

            emit_head(0)
            for j in range(NW_ALL):
                gpsimd.wait_ge(wldp[j % 2], 16 * (j // 2 + 1))
                if j < NW_Q:
                    pos, kc = j // NCH, j % NCH
                    m = QKV_ORD[pos]
                    dst = wq[:, kc * 3 * INNER + m * 128:kc * 3 * INNER + (m + 1) * 128]
                else:
                    mc, kc = (j - NW_Q) // NCH, (j - NW_Q) % NCH
                    dst = wos[:, kc * D + mc * 128:kc * D + (mc + 1) * 128]
                gpsimd.tensor_copy(dst, wload[:, (j % 2) * 128:(j % 2 + 1) * 128]
                                   ).then_inc(wcast, 1)
            for tt in range(NTT):
                tts = tt % NTTS
                if tt >= 1:
                    emit_head(tt)
                for c in range(17):
                    if c < 16:
                        is_q = (c % 2 == 0)
                        cc = c // 2
                        gpsimd.wait_ge(qkev, 16 * tt + QKEV_POS[cc if is_q else 8 + cc])
                        srcf = qf if is_q else kf
                        if 16 * tt + c >= 2:
                            gpsimd.wait_ge(pnm, 16 * tt + c - 1)
                        gpsimd.tensor_mul(sqs[:, (c % 2) * TT:(c % 2 + 1) * TT],
                                          srcf[:, (cc % 4) * TT:(cc % 4 + 1) * TT],
                                          srcf[:, (cc % 4) * TT:(cc % 4 + 1) * TT]
                                          ).then_inc(psq, 1)
                    if c >= 1:
                        cm = c - 1
                        is_q = (cm % 2 == 0)
                        gc = 16 * tt + cm
                        gpsimd.wait_ge(pp2, gc + 1)
                        if gc >= 2:
                            gpsimd.wait_ge(prot, gc - 1)
                        par = cm % 2
                        tpar = tt % 2
                        tc = tcq if is_q else tck
                        ts = tsq if is_q else tsk
                        gpsimd.tensor_mul(m1b[:, par * TT:(par + 1) * TT],
                                          p2f[:, par * TT:(par + 1) * TT],
                                          tc[:, tpar * TT:(tpar + 1) * TT])
                        gpsimd.tensor_mul(m2b[:, par * TT:(par + 1) * TT],
                                          p2f[:, par * TT:(par + 1) * TT],
                                          ts[:, tpar * TT:(tpar + 1) * TT]).then_inc(pm12, 1)
                for mc in range(NCH):
                    g = tt * NCH + mc
                    gpsimd.wait_ge(woe, g + 1)
                    gpsimd.dma_start(
                        out=H[dst_name][mc * 128:(mc + 1) * 128, tt * TT:(tt + 1) * TT],
                        in_=gout[:, (g % 2) * TT:(g % 2 + 1) * TT]).then_inc(stp2[g % 2], 16)
            NG = NTT * NCH
            gpsimd.wait_ge(stp2[0], 16 * ((NG + 1) // 2))
            gpsimd.wait_ge(stp2[1], 16 * (NG // 2))

        def emit_ptrans(tensor, gv, uv):
            (hv, wv, icv, qov, startv, widthv, chunksv) = uv
            tensor.wait_ge(asum, gv + 1)
            if gv >= 2:
                tensor.wait_ge(apte, gv - 1)
            for j in range(widthv // 128):
                ins = tensor.transpose(
                    ptv[gv % 2][:, j * 128:(j + 1) * 128],
                    pbb[:, (gv % 3) * 384 + j * 128:(gv % 3) * 384 + (j + 1) * 128],
                    iden[:])
            ins.then_inc(apt, 1)

        def emit_av(tensor, gv, uv):
            (hv, wv, icv, qov, startv, widthv, chunksv) = uv
            tensor.wait_ge(apte, gv + 1)
            if gv % 2 == 0 and gv >= 2:
                tensor.wait_ge(ave, gv // 2)
            nchk = widthv // 128
            for j, mcg in enumerate(chunksv):
                ins = tensor.matmul(
                    avv[:, icv * 128:(icv + 1) * 128],
                    vt[:, (mcg * HEADS + hv) * DH:(mcg * HEADS + hv + 1) * DH],
                    ptb[:, (gv % 2) * 384 + j * 128:(gv % 2) * 384 + (j + 1) * 128],
                    start=(j == 0), stop=(j == nchk - 1))
            ins.then_inc(aav, 1)

        @b.tensor
        def _(tensor):
            tensor.wait_ge(kld, 16 * 7)
            for tt in range(NTT):
                tts = tt % NTTS
                if tt >= 1:
                    tensor.wait_ge(woe, NCH * tt)
                tensor.wait_ge(xs_done, 8 * (tt + 1))
                def emit_qkv_mm(pos):
                    m = QKV_ORD[pos]
                    g = tt * 24 + pos
                    if tt == 0:
                        tensor.wait_ge(wcast, NCH * (pos + 1))
                    if g >= 2:
                        gg = g - 2
                        ttg, posg = gg // 24, gg % 24
                        mg = QKV_ORD[posg]
                        if mg < 16:
                            tensor.wait_ge(qkev, 16 * ttg + QKEV_POS[mg])
                        else:
                            tensor.wait_ge(vev, 8 * ttg + (mg - 16) + 1)
                    for kc in range(NCH):
                        ins = tensor.matmul(
                            pm[pos % 2][:, :],
                            wq[:, kc * 3 * INNER + m * 128:kc * 3 * INNER + (m + 1) * 128],
                            xsb[:, kc * TT:(kc + 1) * TT],
                            start=(kc == 0), stop=(kc == NCH - 1))
                    ins.then_inc(qmm, 1)

                def emit_prep_mms(c):
                    gc = 16 * tt + c
                    tensor.wait_ge(psq, gc + 1)
                    if gc >= 1:
                        tensor.wait_ge(nsq, gc)
                    tensor.matmul(nrmv, bd2h[:], sqs[:, (c % 2) * TT:(c % 2 + 1) * TT],
                                  start=True, stop=True).then_inc(pnm, 1)
                    tensor.wait_ge(nrn, gc + 1)
                    if gc >= 1:
                        tensor.wait_ge(pqh, gc)
                    tensor.matmul(rnbv[:, :], sel2f[:],
                                  nsb[0:2, (c % 2) * TT:(c % 2 + 1) * TT],
                                  start=True, stop=True).then_inc(prnb, 1)
                    tensor.wait_ge(pm12, gc + 1)
                    tensor.matmul(rnbv[:, :], iden[:],
                                  m1b[:, (c % 2) * TT:(c % 2 + 1) * TT],
                                  start=True, stop=False)
                    tensor.matmul(rnbv[:, :], rotT[:],
                                  m2b[:, (c % 2) * TT:(c % 2 + 1) * TT],
                                  start=False, stop=True).then_inc(prot, 1)

                for pos in range(16):
                    emit_qkv_mm(pos)
                for c in range(8):
                    emit_prep_mms(c)
                    emit_qkv_mm(16 + c)
                for c in range(8, 16):
                    emit_prep_mms(c)
                tensor.wait_ge(vev, 8 * (tt + 1))
                if tt >= 1:
                    tensor.wait_ge(aexp, NU * tt)
                for h in range(HEADS):
                    for mcn in range(4):
                        gvt = 64 * tt + h * 4 + mcn
                        if gvt >= 2:
                            tensor.wait_ge(pve, gvt - 1)
                        hb_ = (h % 2) * 64
                        ins = tensor.transpose(
                            vtv[gvt % 2],
                            vb[hb_:hb_ + 64,
                               (h // 2) * TT + mcn * 128:(h // 2) * TT + (mcn + 1) * 128],
                            iden[hb_:hb_ + 64, hb_:hb_ + 64])
                        ins.then_inc(ptv_s, 1)
                tensor.wait_ge(pqh, 16 * (tt + 1))
                tensor.wait_ge(pve, 64 * (tt + 1))
                for ui in range(NU):
                    gu = NU * tt + ui
                    (h, w, ic, qo, start, width, chunks) = unit(tt, ui)
                    par = gu % 3
                    moff = 384 - width
                    if gu >= 3:
                        tensor.wait_ge(aexp, gu - 2)
                    tensor.matmul(scv[par][:, 0:width], iden[:],
                                  mband[:, moff:384], start=True, stop=False)
                    ins = tensor.matmul(
                        scv[par][:, 0:width],
                        qhat[(h % 2) * 64:(h % 2) * 64 + 64,
                             (h // 2) * TT + w * 256 + ic * 128:
                             (h // 2) * TT + w * 256 + (ic + 1) * 128],
                        kspan[(h % 2) * 64:(h % 2) * 64 + 64,
                              (h // 2) * 768 + start:(h // 2) * 768 + qo + 384],
                        start=False, stop=True)
                    ins.then_inc(aqk, 1)
                    if ui >= 1:
                        emit_ptrans(tensor, gu - 1, unit(tt, ui - 1))
                    if ui >= 2:
                        emit_av(tensor, gu - 2, unit(tt, ui - 2))
                emit_ptrans(tensor, NU * (tt + 1) - 1, unit(tt, NU - 1))
                emit_av(tensor, NU * (tt + 1) - 2, unit(tt, NU - 2))
                emit_av(tensor, NU * (tt + 1) - 1, unit(tt, NU - 1))
                tensor.wait_ge(ave, NPAIR * (tt + 1))
                for mc in range(NCH):
                    g = tt * NCH + mc
                    if tt == 0:
                        tensor.wait_ge(wcast, NW_Q + NCH * (mc + 1))
                    if g >= 2:
                        tensor.wait_ge(woe, g - 1)
                    for kc in range(NCH):
                        ins = tensor.matmul(
                            pm[mc % 2][:, :], wos[:, kc * D + mc * 128:kc * D + (mc + 1) * 128],
                            avb[:, kc * TT:(kc + 1) * TT],
                            start=(kc == 0), stop=(kc == NCH - 1))
                    ins.then_inc(wmm, 1)

        @b.scalar
        def _(scalar):
            for tt in range(NTT):
                def emit_evac(m):
                    scalar.wait_ge(qmm, 24 * tt + QKV_POS[m] + 1)
                    is_q = (m < 8)
                    cc = m % 8
                    if cc >= 4:
                        pidx = 16 * tt + 2 * (cc - 4) + (0 if is_q else 1)
                    elif tt >= 1:
                        pidx = 16 * (tt - 1) + 2 * (cc + 4) + (0 if is_q else 1)
                    else:
                        pidx = None
                    if pidx is not None:
                        scalar.wait_ge(pp2, pidx + 1)
                        scalar.wait_ge(psq, pidx + 1)
                    dstf = qf if is_q else kf
                    scalar.activation(dstf[:, (cc % 4) * TT:(cc % 4 + 1) * TT],
                                      pm[QKV_POS[m] % 2][:, :], AF.Identity).then_inc(qkev, 1)

                for m in EORD[:8]:
                    emit_evac(m)
                for mm in range(8):
                    m = 16 + mm
                    scalar.wait_ge(qmm, 24 * tt + QKV_POS[m] + 1)
                    if tt >= 1 and mm == 0:
                        scalar.wait_ge(ptv_s, 64 * tt)
                    scalar.activation(vb[:, mm * TT:(mm + 1) * TT],
                                      pm[QKV_POS[m] % 2][:, :],
                                      AF.Identity).then_inc(vev, 1)
                for c in range(16):
                    is_q = (c % 2 == 0)
                    cc = c // 2
                    gc = 16 * tt + c
                    scalar.wait_ge(pnm, gc + 1)
                    if gc >= 2:
                        scalar.wait_ge(prnb, gc - 1)
                    scalar.activation(nsb[0:2, (c % 2) * TT:(c % 2 + 1) * TT],
                                      nrmv, AF.Sqrt).then_inc(nsq, 1)
                    scalar.wait_ge(prot, gc + 1)
                    if tt >= 1 and c == 0:
                        scalar.wait_ge(aav, NU * tt)
                        scalar.wait_ge(tcp, 9 * (tt + 1))
                    if is_q:
                        dst = qhat[:, cc * TT:(cc + 1) * TT]
                    else:
                        dst = kspan[:, cc * 768 + 256:cc * 768 + 768]
                    scalar.activation(dst, rnbv[:, :], AF.Identity).then_inc(pqh, 1)
                    if c < 8:
                        emit_evac(EORD[8 + c])
                for ui in range(NU):
                    gu = NU * tt + ui
                    (h, w, ic, qo, start, width, chunks) = unit(tt, ui)
                    scalar.wait_ge(amask, gu + 1)
                    if gu >= 3:
                        scalar.wait_ge(asum, gu - 2)
                        scalar.wait_ge(apt, gu - 2)
                    scalar.activation(pbb[:, (gu % 3) * 384:(gu % 3) * 384 + width],
                                      scv[gu % 3][:, 0:width],
                                      AF.Exp, bias=mxn[:, (gu % 3):(gu % 3) + 1]
                                      ).then_inc(aexp, 1)

        @b.vector
        def _(vector):
            for tt in range(NTT):
                tts = tt % NTTS
                for c in range(16):
                    is_q = (c % 2 == 0)
                    cc = c // 2
                    gc = 16 * tt + c
                    par = c % 2
                    vector.wait_ge(nsq, gc + 1)
                    nv = nsb[0:2, par * TT:(par + 1) * TT]
                    P.join(vector, "vector")
                    ins = vector.tensor_scalar_max(nv, nv, 1e-12)
                    P.mark("vector", ins)
                    P.join(vector, "vector")
                    ins = vector.reciprocal(nv, nv)
                    ins.then_inc(nrn, 1)
                    vector.wait_ge(prnb, gc + 1)
                    srcf = qf if is_q else kf
                    scl = qscale[:, 0:1] if is_q else qscale[:, 1:2]
                    if gc >= 2:
                        vector.wait_ge(pm12, gc - 1)
                    P.join(vector, "vector")
                    ins = vector.scalar_tensor_tensor(
                        p2f[:, par * TT:(par + 1) * TT],
                        srcf[:, (cc % 4) * TT:(cc % 4 + 1) * TT],
                        scl, rnbv[:, :], ALU.mult, ALU.mult)
                    ins.then_inc(pp2, 1)
                vector.wait_ge(tcp, 9 * (tt + 1))
                for h in range(HEADS):
                    for mcn in range(4):
                        gvt = 64 * tt + h * 4 + mcn
                        vector.wait_ge(ptv_s, gvt + 1)
                        mcg = mcn + 2
                        vector.tensor_copy(
                            vt[:, (mcg * HEADS + h) * DH:(mcg * HEADS + h + 1) * DH],
                            vtv[gvt % 2]).then_inc(pve, 1)
                for pi in range(NU + 3):
                    if pi < NU:
                        gu = NU * tt + pi
                        (h, w, ic, qo, start, width, chunks) = unit(tt, pi)
                        par = gu % 3
                        vector.wait_ge(aqk, gu + 1)
                        if gu >= 3:
                            vector.wait_ge(aexp, gu - 2)
                        P.join(vector, "vector")
                        ins = vector.tensor_reduce(mxn[:, par:par + 1],
                                                   scv[par][:, 0:width],
                                                   AX.X, ALU.max)
                        P.mark("vector", ins)
                        P.join(vector, "vector")
                        ins = vector.tensor_scalar_mul(mxn[:, par:par + 1],
                                                       mxn[:, par:par + 1], -1.0)
                        ins.then_inc(amask, 1)
                    if 1 <= pi <= NU:
                        gv = NU * tt + pi - 1
                        uv = unit(tt, pi - 1)
                        (hv, wv, icv, qov, startv, widthv, chunksv) = uv
                        parv = gv % 3
                        par2 = gv % 2
                        vector.wait_ge(aexp, gv + 1)
                        P.join(vector, "vector")
                        ins = vector.tensor_reduce(rinv[:, par2:par2 + 1],
                                                   pbb[:, parv * 384:parv * 384 + widthv],
                                                   AX.X, ALU.add)
                        P.mark("vector", ins)
                        P.join(vector, "vector")
                        ins = vector.reciprocal(rinv[:, par2:par2 + 1], rinv[:, par2:par2 + 1])
                        P.mark("vector", ins)
                        P.join(vector, "vector")
                        ins = vector.tensor_scalar_mul(pbb[:, parv * 384:parv * 384 + widthv],
                                                       pbb[:, parv * 384:parv * 384 + widthv],
                                                       rinv[:, par2:par2 + 1])
                        ins.then_inc(asum, 1)
                    if 2 <= pi <= NU + 1:
                        gv = NU * tt + pi - 2
                        uv = unit(tt, pi - 2)
                        (hv, wv, icv, qov, startv, widthv, chunksv) = uv
                        vector.wait_ge(apt, gv + 1)
                        if gv >= 2:
                            vector.wait_ge(aav, gv - 1)
                        vector.tensor_copy(ptb[:, (gv % 2) * 384:(gv % 2) * 384 + widthv],
                                           ptv[gv % 2][:, 0:widthv]).then_inc(apte, 1)
                    if pi >= 4 and pi % 2 == 0 and (pi - 4) // 2 < NPAIR:
                        j = (pi - 4) // 2
                        gpair = NPAIR * tt + j
                        uv = unit(tt, 2 * j + 1)
                        (hv, wv, icv, qov, startv, widthv, chunksv) = uv
                        vector.wait_ge(aav, NU * tt + 2 * j + 2)
                        if tt >= 1 and j == 0:
                            vector.wait_ge(wmm, NCH * tt)
                        dst = avb[(hv % 2) * 64:(hv % 2) * 64 + 64,
                                  (hv // 2) * TT + wv * 256:(hv // 2) * TT + (wv + 1) * 256]
                        vector.tensor_copy(dst, avv[:, 0:256]).then_inc(ave, 1)
                for mc in range(NCH):
                    g = tt * NCH + mc
                    vector.wait_ge(wmm, g + 1)
                    if g >= 2:
                        vector.wait_ge(stp2[g % 2], 16 * (g // 2))
                    P.join(vector, "vector")
                    ins = vector.tensor_add(gout[:, (g % 2) * TT:(g % 2 + 1) * TT],
                                            pm[mc % 2][:, :],
                                            xg[:, mc * 513 + 1:(mc + 1) * 513])
                    ins.then_inc(woe, 1)


def l0_ffn_consts(l):
    return [f"w1p{l}", f"b1p{l}", f"w2p{l}", f"b2{l}", f"lng{l}", f"lnb{l}",
            "k_ones128", "k_onesk1f"]
NM1 = NH1 // 128
NK2 = DPAD // 128


def emit_l0_ffn(P, l, src_name, dst_name):
    nc, b, H = P.nc, P.block, P.hbm
    with contextlib.ExitStack() as S:
        wload = P.sbuf([128, 2 * 128], F32, "flwl", S)
        w1 = P.sbuf([128, NCH * NH1], BF16, "flw1", S)
        w2 = P.sbuf([128, NK2 * D], BF16, "flw2", S)
        b1c_f = P.sbuf([128, NM1], F32, "flb1", S)
        b2c_f = P.sbuf([128, NCH], F32, "flb2", S)
        lngc = P.sbuf([128, NCH], F32, "fllg", S)
        lnbc = P.sbuf([128, NCH], F32, "fllb", S)
        ones128 = P.sbuf([128, 1], BF16, "flo1", S)
        onesk1f = P.sbuf([1, 128], F32, "flok", S)
        xg = P.sbuf([128, NCH * 513], F32, "flxg", S)
        xsb = P.sbuf([128, NCH * TT], BF16, "flxs", S)
        xnb = P.sbuf([128, NCH * TT], BF16, "flxn", S)
        hb = P.sbuf([128, NK2 * TT], BF16, "flh", S)
        sq = hb[:, 0:NCH * TT]
        ab = P.sbuf([128, 2 * TT], BF16, "flab", S)
        gb = P.sbuf([128, 2 * TT], BF16, "flgb", S)
        t1 = P.sbuf([128, 2 * TT], F32, "flt1", S)
        gout = P.sbuf([128, 2 * TT], F32, "flgo", S)
        stat = P.sbuf([1, 3 * TT], F32, "flst", S)

        pm = [P.psum([128, TT], F32, "flpm", S) for _ in range(2)]
        p_sum = P.psum([1, TT], F32, "flps", S)
        p_sq = P.psum([1, TT], F32, "flpq", S)
        b_mu = P.psum([128, TT], F32, "flbm", S)
        b_rs = P.psum([128, TT], F32, "flbr", S)

        kld = P.sem("fkld")
        wldp = [P.sem("fw0"), P.sem("fw1")]
        wcast = P.sem("fwc")
        ld = P.sem("fld")
        xs_done = P.sem("fxs")
        psq_s = P.sem("fpsq")
        smm = P.sem("fsmm")
        tail = P.sem("ftail")
        bmm = P.sem("fbmm")
        appl = P.sem("fappl")
        w1mm = P.sem("fw1m")
        w1ev = P.sem("fw1e")
        hple = P.sem("fhp")
        w2mm = P.sem("fw2m")
        w2ev = P.sem("fw2e")
        stp2 = [P.sem("fst0"), P.sem("fst1")]

        NW1 = NCH * NM1
        NW2 = NK2 * NCH
        NW_ALL = NW1 + NW2

        @b.sync
        def _(sync):
            sync.dma_start(out=b1c_f[:], in_=H[f"b1p{l}"]).then_inc(kld, 16)
            sync.dma_start(out=b2c_f[:], in_=H[f"b2{l}"]).then_inc(kld, 16)
            sync.dma_start(out=lngc[:], in_=H[f"lng{l}"]).then_inc(kld, 16)
            sync.dma_start(out=lnbc[:], in_=H[f"lnb{l}"]).then_inc(kld, 16)
            sync.dma_start(out=ones128[:], in_=H["k_ones128"]).then_inc(kld, 16)
            sync.dma_start(out=onesk1f[:], in_=H["k_onesk1f"]).then_inc(kld, 16)
            def emit_acts(tt):
                tts = tt % NTTS
                for ch in range(NCH):
                    if tts == 0:
                        sync.dma_start(out=xg[:, ch * 513 + 1:(ch + 1) * 513],
                                       in_=H[src_name][ch * 128:(ch + 1) * 128,
                                                       tt * TT:(tt + 1) * TT]).then_inc(ld, 16)
                    else:
                        sync.dma_start(out=xg[:, ch * 513:(ch + 1) * 513],
                                       in_=H[src_name][ch * 128:(ch + 1) * 128,
                                                       tt * TT - 1:(tt + 1) * TT]).then_inc(ld, 16)

            emit_acts(0)
            for j in range(NW_ALL):
                if j >= 2:
                    sync.wait_ge(wcast, j - 1)
                if j < NW1:
                    m, kc = j // NCH, j % NCH
                    srcw = H[f"w1p{l}"][kc * 128:(kc + 1) * 128, m * 128:(m + 1) * 128]
                else:
                    mc, kc = (j - NW1) // NK2, (j - NW1) % NK2
                    srcw = H[f"w2p{l}"][kc * 128:(kc + 1) * 128, mc * 128:(mc + 1) * 128]
                sync.dma_start(out=wload[:, (j % 2) * 128:(j % 2 + 1) * 128],
                               in_=srcw).then_inc(wldp[j % 2], 16)
            for tt in range(1, NTT):
                sync.wait_ge(w2ev, NCH * tt)
                emit_acts(tt)

        @b.gpsimd
        def _(gpsimd):
            def emit_head(tt):
                tts = tt % NTTS
                gpsimd.wait_ge(ld, 16 * NCH * (tt + 1))
                if tt >= 1:
                    gpsimd.wait_ge(w1mm, NM1 * tt)
                    gpsimd.wait_ge(w2mm, NCH * tt)
                if tts == 0:
                    for ch in range(NCH):
                        ins = gpsimd.memset(xg[:, ch * 513:ch * 513 + 1], 0.0)
                    P.mark("gpsimd", ins)
                    P.join(gpsimd, "gpsimd")
                for ch in range(NCH):
                    off = 1 if ch < 4 else 0
                    gpsimd.tensor_copy(xsb[:, ch * TT:(ch + 1) * TT],
                                       xg[:, ch * 513 + off:ch * 513 + off + TT]
                                       ).then_inc(xs_done, 1)
                gpsimd.wait_ge(smm, 2 * tt + 1)
                for ch in range(NCH):
                    gpsimd.tensor_mul(sq[:, ch * TT:(ch + 1) * TT],
                                      xsb[:, ch * TT:(ch + 1) * TT],
                                      xsb[:, ch * TT:(ch + 1) * TT]).then_inc(psq_s, 1)

            emit_head(0)
            for j in range(NW_ALL):
                gpsimd.wait_ge(wldp[j % 2], 16 * (j // 2 + 1))
                if j < NW1:
                    m, kc = j // NCH, j % NCH
                    dst = w1[:, kc * NH1 + m * 128:kc * NH1 + (m + 1) * 128]
                else:
                    mc, kc = (j - NW1) // NK2, (j - NW1) % NK2
                    dst = w2[:, kc * D + mc * 128:kc * D + (mc + 1) * 128]
                gpsimd.tensor_copy(dst, wload[:, (j % 2) * 128:(j % 2 + 1) * 128]
                                   ).then_inc(wcast, 1)
            for tt in range(NTT):
                tts = tt % NTTS
                if tt >= 1:
                    emit_head(tt)
                for j in range(NK2):
                    gpsimd.wait_ge(w1ev, 44 * tt + 2 * j + 2)
                    gpsimd.tensor_mul(hb[:, j * TT:(j + 1) * TT],
                                      ab[:, (j % 2) * TT:(j % 2 + 1) * TT],
                                      gb[:, (j % 2) * TT:(j % 2 + 1) * TT]).then_inc(hple, 1)
                for mc in range(NCH):
                    g = tt * NCH + mc
                    gpsimd.wait_ge(w2ev, g + 1)
                    gpsimd.dma_start(
                        out=H[dst_name][mc * 128:(mc + 1) * 128, tt * TT:(tt + 1) * TT],
                        in_=gout[:, (g % 2) * TT:(g % 2 + 1) * TT]).then_inc(stp2[g % 2], 16)
            NG = NTT * NCH
            gpsimd.wait_ge(stp2[0], 16 * ((NG + 1) // 2))
            gpsimd.wait_ge(stp2[1], 16 * (NG // 2))

        @b.tensor
        def _(tensor):
            tensor.wait_ge(kld, 16 * 6)
            for tt in range(NTT):
                if tt >= 1:
                    tensor.wait_ge(tail, 5 * (tt - 1) + 2)
                tensor.wait_ge(xs_done, 8 * (tt + 1))
                for ch in range(NCH):
                    ins = tensor.matmul(p_sum[:, :], ones128[:],
                                        xsb[:, ch * TT:(ch + 1) * TT],
                                        start=(ch == 0), stop=(ch == NCH - 1))
                ins.then_inc(smm, 1)
                tensor.wait_ge(psq_s, 8 * (tt + 1))
                for ch in range(NCH):
                    ins = tensor.matmul(p_sq[:, :], ones128[:],
                                        sq[:, ch * TT:(ch + 1) * TT],
                                        start=(ch == 0), stop=(ch == NCH - 1))
                ins.then_inc(smm, 1)
                tensor.wait_ge(tail, 5 * tt + 5)
                tensor.wait_ge(appl, NCH * tt)
                tensor.matmul(b_mu[:, :], onesk1f[:], stat[:, 0:TT],
                              start=True, stop=True).then_inc(bmm, 1)
                tensor.matmul(b_rs[:, :], onesk1f[:], stat[:, TT:2 * TT],
                              start=True, stop=True).then_inc(bmm, 1)
                tensor.wait_ge(appl, NCH * (tt + 1))
                if tt >= 1:
                    tensor.wait_ge(w2ev, NCH * tt)
                for m in range(NM1):
                    g = tt * NM1 + m
                    if tt == 0:
                        tensor.wait_ge(wcast, NCH * (m + 1))
                    if g >= 2:
                        tensor.wait_ge(w1ev, g - 1)
                    for kc in range(NCH):
                        ins = tensor.matmul(
                            pm[m % 2][:, :], w1[:, kc * NH1 + m * 128:kc * NH1 + (m + 1) * 128],
                            xnb[:, kc * TT:(kc + 1) * TT],
                            start=(kc == 0), stop=(kc == NCH - 1))
                    ins.then_inc(w1mm, 1)
                tensor.wait_ge(w1ev, NM1 * (tt + 1))
                tensor.wait_ge(hple, NK2 * (tt + 1))
                for mc in range(NCH):
                    g = tt * NCH + mc
                    if tt == 0:
                        tensor.wait_ge(wcast, NW1 + NK2 * (mc + 1))
                    if g >= 2:
                        tensor.wait_ge(w2ev, g - 1)
                    for kc in range(NK2):
                        ins = tensor.matmul(
                            pm[mc % 2][:, :], w2[:, kc * D + mc * 128:kc * D + (mc + 1) * 128],
                            hb[:, kc * TT:(kc + 1) * TT],
                            start=(kc == 0), stop=(kc == NK2 - 1))
                    ins.then_inc(w2mm, 1)

        @b.scalar
        def _(scalar):
            for tt in range(NTT):
                scalar.wait_ge(smm, tt * 2 + 1)
                ins = scalar.activation(stat[:, 0:TT], p_sum[:, :], AF.Identity, scale=1.0 / D)
                P.mark("scalar", ins)
                P.join(scalar, "scalar")
                scalar.activation(stat[:, 1 * TT:2 * TT], stat[:, 0:TT], AF.Square)
                scalar.wait_ge(smm, tt * 2 + 2)
                scalar.activation(stat[:, 2 * TT:3 * TT], p_sq[:, :], AF.Identity,
                                  scale=1.0 / D).then_inc(tail, 1)
                scalar.wait_ge(tail, 5 * tt + 2)
                scalar.activation(stat[:, 2 * TT:3 * TT], stat[:, 1 * TT:2 * TT],
                                  AF.Sqrt, bias=1e-5).then_inc(tail, 1)
                for m in range(NM1):
                    g = tt * NM1 + m
                    scalar.wait_ge(w1mm, g + 1)
                    j = m // 2
                    if j >= 2 and m % 2 == 0:
                        scalar.wait_ge(hple, NK2 * tt + j - 1)
                    bc = b1c_f[:, m:m + 1]
                    if m % 2 == 0:
                        scalar.activation(ab[:, (j % 2) * TT:(j % 2 + 1) * TT],
                                          pm[m % 2][:, :], AF.Identity, bias=bc
                                          ).then_inc(w1ev, 1)
                    else:
                        scalar.activation(gb[:, (j % 2) * TT:(j % 2 + 1) * TT],
                                          pm[m % 2][:, :], AF.Gelu, bias=bc
                                          ).then_inc(w1ev, 1)

        @b.vector
        def _(vector):
            for tt in range(NTT):
                vector.wait_ge(tail, 5 * tt + 1)
                P.join(vector, "vector")
                ins = vector.tensor_sub(stat[:, 1 * TT:2 * TT], stat[:, 2 * TT:3 * TT],
                                        stat[:, 1 * TT:2 * TT])
                ins.then_inc(tail, 1)
                vector.wait_ge(tail, 5 * tt + 3)
                P.join(vector, "vector")
                ins = vector.reciprocal(stat[:, 1 * TT:2 * TT], stat[:, 2 * TT:3 * TT])
                ins.then_inc(tail, 2)
                vector.wait_ge(bmm, 2 * (tt + 1))
                if tt >= 1:
                    vector.wait_ge(w1mm, NM1 * tt)
                for ch in range(NCH):
                    off = 1 if ch < 4 else 0
                    par = ch % 2
                    P.join(vector, "vector")
                    ins = vector.tensor_sub(t1[:, par * TT:(par + 1) * TT],
                                            xg[:, ch * 513 + off:ch * 513 + off + TT],
                                            b_mu[:, :])
                    P.mark("vector", ins)
                    P.join(vector, "vector")
                    ins = vector.tensor_mul(t1[:, par * TT:(par + 1) * TT],
                                            t1[:, par * TT:(par + 1) * TT], b_rs[:, :])
                    P.mark("vector", ins)
                    P.join(vector, "vector")
                    ins = vector.tensor_scalar(xnb[:, ch * TT:(ch + 1) * TT],
                                               t1[:, par * TT:(par + 1) * TT],
                                               lngc[:, ch:ch + 1], lnbc[:, ch:ch + 1],
                                               ALU.mult, ALU.add)
                    ins.then_inc(appl, 1)
                for mc in range(NCH):
                    g = tt * NCH + mc
                    vector.wait_ge(w2mm, g + 1)
                    if g >= 2:
                        vector.wait_ge(stp2[g % 2], 16 * (g // 2))
                    P.join(vector, "vector")
                    ins = vector.scalar_tensor_tensor(
                        gout[:, (g % 2) * TT:(g % 2 + 1) * TT], pm[mc % 2][:, :],
                        b2c_f[:, mc:mc + 1], xg[:, mc * 513 + 1:(mc + 1) * 513],
                        ALU.add, ALU.add)
                    ins.then_inc(w2ev, 1)


def p2_wout(P):
    emit_proj(P, "states_b", "w_outT", "b_out", [("s0_f", F32)], D, D)


RNN_BLK = 128
NGRP = NCH * BPC
RNN_BW = NGRP * RNN_BLK


def p1_rnn(P):
    nc, b, H = P.nc, P.block, P.hbm
    NB = SEQ // RNN_BLK
    W = NGRP

    with contextlib.ExitStack() as S:
        aload = P.sbuf([128, 2 * 1024], F32, "p1al", S)
        atb = P.sbuf([128, 8 * 1024], BF16, "p1at", S)
        iden = P.sbuf([128, 128], BF16, "p1id", S)
        ubuf = P.sbuf([128, 2 * RNN_BW], BF16, "p1u", S)
        gbuf = P.sbuf([128, 2 * RNN_BW], F32, "p1g", S)
        ibuf = P.sbuf([128, 2 * RNN_BW], F32, "p1c", S)
        sblk = P.sbuf([128, 2 * RNN_BW], BF16, "p1s", S)
        sinit = P.sbuf([128, W], BF16, "p1s0", S)
        stil = P.sbuf([128, 2 * W], F32, "p1stil", S)
        t1b = P.sbuf([128, 2 * W], F32, "p1t1", S)
        ps = P.psum([128, 2 * W], F32, "p1ps", S)

        kld = P.sem("p1kld")
        aldp = [P.sem("p1a0"), P.sem("p1a1")]
        acast = P.sem("p1ac")
        ldp = [P.sem("p1l0"), P.sem("p1l1")]
        init_done = P.sem("p1init")
        mm_done = P.sem("p1mm")
        act_done = P.sem("p1tanh")
        dve_done = P.sem("p1dve")
        stp = [P.sem("p1st0"), P.sem("p1st1")]

        cv0 = P.chain_count("vector")
        chv = P.chain_sem("vector")

        def tile3(buf, t):
            B = t // RNN_BLK
            toff = t - B * RNN_BLK
            par = B % 2
            v = buf[:, par * RNN_BW:(par + 1) * RNN_BW]
            return v.rearrange("p (g t) -> p g t", t=RNN_BLK)[:, :, toff:toff + 1]

        @b.sync
        def _(sync):
            sync.dma_start(out=iden[:], in_=H["k_ident"]).then_inc(kld, 16)
            for j in range(NCH):
                if j >= 2:
                    sync.wait_ge(acast, j - 1)
                sync.dma_start(out=aload[:, (j % 2) * 1024:(j % 2 + 1) * 1024],
                               in_=H["aT"][j * 128:(j + 1) * 128, :]).then_inc(aldp[j % 2], 16)
            for B in range(NB):
                par = B % 2
                if B >= 2:
                    sync.wait_ge(dve_done, (B - 1) * RNN_BLK)
                t0 = B * RNN_BLK
                for ch in range(NCH):
                    for bb in range(BPC):
                        gcol = par * RNN_BW + (ch * BPC + bb) * RNN_BLK
                        for (buf, hn) in ((ubuf, "u_b"), (gbuf, "g_f"), (ibuf, "inp_f")):
                            sync.dma_start(
                                out=buf[:, gcol:gcol + RNN_BLK],
                                in_=H[hn][ch * 128:(ch + 1) * 128,
                                          bb * SEQ + t0:bb * SEQ + t0 + RNN_BLK]
                            ).then_inc(ldp[par], 16)

        @b.gpsimd
        def _(gpsimd):
            for j in range(NCH):
                gpsimd.wait_ge(aldp[j % 2], 16 * (j // 2 + 1))
                gpsimd.tensor_copy(atb[:, j * 1024:(j + 1) * 1024],
                                   aload[:, (j % 2) * 1024:(j % 2 + 1) * 1024]).then_inc(acast, 1)
            for B in range(NB):
                par = B % 2
                gpsimd.wait_ge(dve_done, (B + 1) * RNN_BLK)
                t0 = B * RNN_BLK
                for ch in range(NCH):
                    for bb in range(BPC):
                        gcol = par * RNN_BW + (ch * BPC + bb) * RNN_BLK
                        gpsimd.dma_start(
                            out=H["states_b"][ch * 128:(ch + 1) * 128,
                                              bb * SEQ + t0:bb * SEQ + t0 + RNN_BLK],
                            in_=sblk[:, gcol:gcol + RNN_BLK]).then_inc(stp[par], 16)
            gpsimd.wait_ge(stp[0], 16 * NGRP * ((NB + 1) // 2))
            gpsimd.wait_ge(stp[1], 16 * NGRP * (NB // 2))

        @b.tensor
        def _(tensor):
            tensor.wait_ge(acast, NCH)
            tensor.wait_ge(kld, 16)
            tensor.wait_ge(init_done, 1)
            for t in range(SEQ):
                B = t // RNN_BLK
                par2 = t % 2
                if t % RNN_BLK == 0:
                    tensor.wait_ge(ldp[B % 2], 16 * 3 * NGRP * (B // 2 + 1))
                if t >= 2:
                    tensor.wait_ge(act_done, t - 1)
                if t >= 1:
                    tensor.wait_ge(dve_done, t)
                s_src = sinit[:].rearrange("p (g t) -> p g t", t=1) if t == 0 \
                    else tile3(sblk, t - 1)
                ut = tile3(ubuf, t)
                for ec in range(NCH):
                    tensor.matmul(ps[:, par2 * W + ec * BPC:par2 * W + (ec + 1) * BPC],
                                  iden[:], ut[:, ec * BPC:(ec + 1) * BPC, :],
                                  start=True, stop=False)
                    for dc in range(NCH):
                        ins = tensor.matmul(
                            ps[:, par2 * W + ec * BPC:par2 * W + (ec + 1) * BPC],
                            atb[:, dc * 1024 + ec * 128:dc * 1024 + (ec + 1) * 128],
                            s_src[:, dc * BPC:(dc + 1) * BPC, :],
                            start=False, stop=(dc == NCH - 1))
                ins.then_inc(mm_done, 1)

        @b.scalar
        def _(scalar):
            for t in range(SEQ):
                par2 = t % 2
                scalar.wait_ge(mm_done, t + 1)
                if t >= 2:
                    scalar.wait_ge(chv, cv0 + 2 * (t - 1))
                scalar.activation(stil[:, par2 * W:(par2 + 1) * W],
                                  ps[:, par2 * W:(par2 + 1) * W],
                                  AF.Tanh).then_inc(act_done, 1)

        @b.vector
        def _(vector):
            vector.memset(sinit[:], 0.0).then_inc(init_done, 1)
            for t in range(SEQ):
                B = t // RNN_BLK
                par2 = t % 2
                vector.wait_ge(act_done, t + 1)
                if t % RNN_BLK == 0 and B >= 2:
                    vector.wait_ge(stp[B % 2], 16 * NGRP * (B // 2))
                P.join(vector, "vector")
                ins = vector.tensor_sub(t1b[:, par2 * W:(par2 + 1) * W],
                                        stil[:, par2 * W:(par2 + 1) * W],
                                        tile3(ibuf, t))
                P.mark("vector", ins)
                P.join(vector, "vector")
                ins = vector.tensor_mul(t1b[:, par2 * W:(par2 + 1) * W],
                                        t1b[:, par2 * W:(par2 + 1) * W],
                                        tile3(gbuf, t))
                P.mark("vector", ins)
                P.join(vector, "vector")
                vector.tensor_add(tile3(sblk, t), t1b[:, par2 * W:(par2 + 1) * W],
                                  tile3(ibuf, t)).then_inc(dve_done, 1)


def p9_combine(P):
    nc, b, H = P.nc, P.block, P.hbm
    with contextlib.ExitStack() as S:
        xf = P.sbuf([128, NCH * TT], F32, "c9x", S)
        sf = P.sbuf([128, NCH * TT], F32, "c9s", S)
        gf = P.sbuf([128, NCH * TT], F32, "c9g", S)
        sgf = P.sbuf([128, 2 * TT], F32, "c9sg", S)
        x1f = P.sbuf([128, NCH * TT], F32, "c9x1", S)
        x1b = P.sbuf([128, NCH * TT], BF16, "c9xb", S)
        sq = P.sbuf([128, NCH * TT], BF16, "c9sq", S)
        xnb = P.sbuf([128, NCH * TT], BF16, "c9xn", S)
        t1 = P.sbuf([128, 2 * TT], F32, "c9t1", S)
        stat = P.sbuf([1, 6 * TT], F32, "c9st", S)
        ones128 = P.sbuf([128, 1], BF16, "c9o", S)
        onesk1f = P.sbuf([1, 128], F32, "c9of", S)
        p_sum = P.psum([1, TT], F32, "c9ps", S)
        p_sq = P.psum([1, TT], F32, "c9pq", S)
        b_mu = P.psum([128, TT], F32, "c9bm", S)
        b_rs = P.psum([128, TT], F32, "c9br", S)

        kld = P.sem("c9kld")
        ld = P.sem("c9ld")
        silu_d = P.sem("c9si")
        comb = P.sem("c9cb")
        psq_s = P.sem("c9psq")
        smm = P.sem("c9smm")
        tail = P.sem("c9tl")
        bmm = P.sem("c9bmm")
        appl = P.sem("c9ap")
        st = P.sem("c9st")

        NST = 16 * 16

        @b.sync
        def _(sync):
            sync.dma_start(out=ones128[:], in_=H["k_ones128"]).then_inc(kld, 16)
            sync.dma_start(out=onesk1f[:], in_=H["k_onesk1f"]).then_inc(kld, 16)
            for tt in range(NTT):
                if tt >= 1:
                    sync.wait_ge(appl, 8 * tt)
                for ch in range(NCH):
                    sync.dma_start(out=xf[:, ch * TT:(ch + 1) * TT],
                                   in_=H["xT"][ch * 128:(ch + 1) * 128, tt * TT:(tt + 1) * TT]
                                   ).then_inc(ld, 16)
                    sync.dma_start(out=sf[:, ch * TT:(ch + 1) * TT],
                                   in_=H["s0_f"][ch * 128:(ch + 1) * 128, tt * TT:(tt + 1) * TT]
                                   ).then_inc(ld, 16)
                    sync.dma_start(out=gf[:, ch * TT:(ch + 1) * TT],
                                   in_=H["gb_f"][ch * 128:(ch + 1) * 128, tt * TT:(tt + 1) * TT]
                                   ).then_inc(ld, 16)

        @b.scalar
        def _(scalar):
            for tt in range(NTT):
                scalar.wait_ge(ld, 16 * 24 * (tt + 1))
                for ch in range(NCH):
                    par = ch % 2
                    if tt * 8 + ch >= 2:
                        scalar.wait_ge(comb, tt * 8 + ch - 1)
                    scalar.activation(sgf[:, par * TT:(par + 1) * TT],
                                      gf[:, ch * TT:(ch + 1) * TT], AF.Silu
                                      ).then_inc(silu_d, 1)
                scalar.wait_ge(smm, tt * 2 + 1)
                ins = scalar.activation(stat[:, 0:TT], p_sum[:, :], AF.Identity, scale=1.0 / D)
                P.mark("scalar", ins)
                P.join(scalar, "scalar")
                scalar.activation(stat[:, 1 * TT:2 * TT], stat[:, 0:TT], AF.Square)
                scalar.wait_ge(smm, tt * 2 + 2)
                scalar.activation(stat[:, 2 * TT:3 * TT], p_sq[:, :], AF.Identity,
                                  scale=1.0 / D).then_inc(tail, 1)
                scalar.wait_ge(tail, 5 * tt + 2)
                scalar.activation(stat[:, 4 * TT:5 * TT], stat[:, 3 * TT:4 * TT],
                                  AF.Sqrt, bias=1e-5).then_inc(tail, 1)

        @b.vector
        def _(vector):
            for tt in range(NTT):
                for ch in range(NCH):
                    par = ch % 2
                    g = tt * 8 + ch
                    vector.wait_ge(silu_d, g + 1)
                    if tt >= 1 and ch == 0:
                        vector.wait_ge(st, NST * tt)
                        vector.wait_ge(psq_s, 8 * tt)
                        vector.wait_ge(smm, 2 * tt)
                    P.join(vector, "vector")
                    ins = vector.tensor_mul(t1[:, par * TT:(par + 1) * TT],
                                            sf[:, ch * TT:(ch + 1) * TT],
                                            sgf[:, par * TT:(par + 1) * TT])
                    P.mark("vector", ins)
                    P.join(vector, "vector")
                    ins = vector.tensor_add(x1f[:, ch * TT:(ch + 1) * TT],
                                            xf[:, ch * TT:(ch + 1) * TT],
                                            t1[:, par * TT:(par + 1) * TT])
                    P.mark("vector", ins)
                    P.join(vector, "vector")
                    vector.tensor_copy(x1b[:, ch * TT:(ch + 1) * TT],
                                       x1f[:, ch * TT:(ch + 1) * TT]).then_inc(comb, 1)
                vector.wait_ge(tail, 5 * tt + 1)
                P.join(vector, "vector")
                ins = vector.tensor_sub(stat[:, 3 * TT:4 * TT], stat[:, 2 * TT:3 * TT],
                                        stat[:, 1 * TT:2 * TT])
                ins.then_inc(tail, 1)
                vector.wait_ge(tail, 5 * tt + 3)
                P.join(vector, "vector")
                ins = vector.reciprocal(stat[:, 5 * TT:6 * TT], stat[:, 4 * TT:5 * TT])
                ins.then_inc(tail, 2)
                vector.wait_ge(bmm, 2 * (tt + 1))
                for ch in range(NCH):
                    par = ch % 2
                    P.join(vector, "vector")
                    ins = vector.tensor_sub(t1[:, par * TT:(par + 1) * TT],
                                            x1f[:, ch * TT:(ch + 1) * TT], b_mu[:, :])
                    P.mark("vector", ins)
                    P.join(vector, "vector")
                    ins = vector.tensor_mul(xnb[:, ch * TT:(ch + 1) * TT],
                                            t1[:, par * TT:(par + 1) * TT], b_rs[:, :])
                    ins.then_inc(appl, 1)

        @b.gpsimd
        def _(gpsimd):
            for tt in range(NTT):
                for ch in range(NCH):
                    gpsimd.wait_ge(comb, tt * 8 + ch + 1)
                    gpsimd.tensor_mul(sq[:, ch * TT:(ch + 1) * TT],
                                      x1b[:, ch * TT:(ch + 1) * TT],
                                      x1b[:, ch * TT:(ch + 1) * TT]).then_inc(psq_s, 1)
                for ch in range(NCH):
                    gpsimd.dma_start(out=H["x1_f"][ch * 128:(ch + 1) * 128,
                                                   tt * TT:(tt + 1) * TT],
                                     in_=x1f[:, ch * TT:(ch + 1) * TT]).then_inc(st, 16)
                gpsimd.wait_ge(appl, 8 * (tt + 1))
                for ch in range(NCH):
                    gpsimd.dma_start(out=H["xn1_b"][ch * 128:(ch + 1) * 128,
                                                    tt * TT:(tt + 1) * TT],
                                     in_=xnb[:, ch * TT:(ch + 1) * TT]).then_inc(st, 16)
            gpsimd.wait_ge(st, NST * NTT)

        @b.tensor
        def _(tensor):
            tensor.wait_ge(kld, 32)
            for tt in range(NTT):
                if tt >= 1:
                    tensor.wait_ge(tail, 5 * (tt - 1) + 2)
                tensor.wait_ge(comb, 8 * (tt + 1))
                for ch in range(NCH):
                    ins = tensor.matmul(p_sum[:, :], ones128[:],
                                        x1b[:, ch * TT:(ch + 1) * TT],
                                        start=(ch == 0), stop=(ch == NCH - 1))
                ins.then_inc(smm, 1)
                tensor.wait_ge(psq_s, 8 * (tt + 1))
                for ch in range(NCH):
                    ins = tensor.matmul(p_sq[:, :], ones128[:],
                                        sq[:, ch * TT:(ch + 1) * TT],
                                        start=(ch == 0), stop=(ch == NCH - 1))
                ins.then_inc(smm, 1)
                tensor.wait_ge(tail, 5 * tt + 5)
                if tt >= 1:
                    tensor.wait_ge(appl, 8 * tt)
                tensor.matmul(b_mu[:, :], onesk1f[:], stat[:, 0:TT],
                              start=True, stop=True).then_inc(bmm, 1)
                tensor.matmul(b_rs[:, :], onesk1f[:], stat[:, 5 * TT:6 * TT],
                              start=True, stop=True).then_inc(bmm, 1)


def p10a_inproj(P):
    nc, b, H = P.nc, P.block, P.hbm
    Kdim, Mdim = D, 2 * DI
    KC, MC = Kdim // 128, Mdim // 128
    with contextlib.ExitStack() as S:
        wload = P.sbuf([128, 2 * 128], F32, "iawl", S)
        wsb = P.sbuf([128, KC * Mdim], BF16, "iaw", S)
        xsb = P.sbuf([128, KC * TT], BF16, "iax", S)
        osb = P.sbuf([128, 2 * TT], BF16, "iao", S)
        p_m = [P.psum([128, TT], F32, "iap", S) for _ in range(2)]
        wldp = [P.sem("iaw0"), P.sem("iaw1")]
        wcast = P.sem("iawc")
        ld = P.sem("iald")
        pj = P.sem("iapj")
        pje = P.sem("iapje")
        stp2 = [P.sem("ias0"), P.sem("ias1")]

        @b.sync
        def _(sync):
            for j in range(KC * MC):
                kc, mc = j // MC, j % MC
                if j >= 2:
                    sync.wait_ge(wcast, j - 1)
                sync.dma_start(out=wload[:, (j % 2) * 128:(j % 2 + 1) * 128],
                               in_=H["inprojT"][kc * 128:(kc + 1) * 128, mc * 128:(mc + 1) * 128]
                               ).then_inc(wldp[j % 2], 16)
            for tt in range(NTT):
                if tt >= 1:
                    sync.wait_ge(pj, tt * MC)
                for kc in range(KC):
                    sync.dma_start(out=xsb[:, kc * TT:(kc + 1) * TT],
                                   in_=H["xn1_b"][kc * 128:(kc + 1) * 128,
                                                  tt * TT:(tt + 1) * TT]).then_inc(ld, 16)

        @b.gpsimd
        def _(gpsimd):
            for j in range(KC * MC):
                kc, mc = j // MC, j % MC
                gpsimd.wait_ge(wldp[j % 2], 16 * (j // 2 + 1))
                gpsimd.tensor_copy(wsb[:, kc * Mdim + mc * 128:kc * Mdim + (mc + 1) * 128],
                                   wload[:, (j % 2) * 128:(j % 2 + 1) * 128]).then_inc(wcast, 1)
            for tt in range(NTT):
                for mc in range(MC):
                    g = tt * MC + mc
                    gpsimd.wait_ge(pje, g + 1)
                    if mc < 16:
                        dsth = H["u0_b"][mc * 128:(mc + 1) * 128, tt * TT:(tt + 1) * TT]
                    else:
                        dsth = H["zs_b"][(mc - 16) * 128:(mc - 15) * 128, tt * TT:(tt + 1) * TT]
                    gpsimd.dma_start(out=dsth, in_=osb[:, (g % 2) * TT:(g % 2 + 1) * TT]
                                     ).then_inc(stp2[g % 2], 16)
            NG = NTT * MC
            gpsimd.wait_ge(stp2[0], 16 * ((NG + 1) // 2))
            gpsimd.wait_ge(stp2[1], 16 * (NG // 2))

        @b.tensor
        def _(tensor):
            tensor.wait_ge(wcast, KC * MC)
            for tt in range(NTT):
                tensor.wait_ge(ld, 16 * KC * (tt + 1))
                for mc in range(MC):
                    g = tt * MC + mc
                    if g >= 2:
                        tensor.wait_ge(pje, g - 1)
                    for kc in range(KC):
                        ins = tensor.matmul(
                            p_m[mc % 2][:, :],
                            wsb[:, kc * Mdim + mc * 128:kc * Mdim + (mc + 1) * 128],
                            xsb[:, kc * TT:(kc + 1) * TT],
                            start=(kc == 0), stop=(kc == KC - 1))
                    ins.then_inc(pj, 1)

        @b.scalar
        def _(scalar):
            for tt in range(NTT):
                for mc in range(MC):
                    g = tt * MC + mc
                    scalar.wait_ge(pj, g + 1)
                    if g >= 2:
                        scalar.wait_ge(stp2[g % 2], 16 * (g // 2))
                    fn = AF.Identity if mc < 16 else AF.Silu
                    scalar.activation(osb[:, (g % 2) * TT:(g % 2 + 1) * TT],
                                      p_m[mc % 2][:, :], fn).then_inc(pje, 1)


def p10b_conv(P):
    nc, b, H = P.nc, P.block, P.hbm
    KC = DI // 128
    with contextlib.ExitStack() as S:
        u0 = P.sbuf([128, KC * (TT + 3)], BF16, "cbu0", S)
        zs = P.sbuf([128, KC * TT], BF16, "cbzs", S)
        wxp = P.sbuf([128, KC * 64], BF16, "cbwx", S)
        wxl = P.sbuf([128, 2 * 64], F32, "cbwxl", S)
        convw = P.sbuf([128, KC * 4], F32, "cbcw", S)
        convb = P.sbuf([128, KC], F32, "cbcb", S)
        dtw = P.sbuf([128, KC], F32, "cbdw", S)
        dtb = P.sbuf([128, KC], F32, "cbdb", S)
        onesk1f = P.sbuf([1, 128], F32, "cbok", S)
        acc = P.sbuf([128, 2 * TT], F32, "cbac", S)
        u2b = P.sbuf([128, KC * TT], BF16, "cbu2", S)
        bcf = P.sbuf([33, TT], F32, "cbbc", S)
        dpre = P.sbuf([128, 2 * TT], F32, "cbdp", S)
        tso = P.sbuf([128, 2 * TT], F32, "cbts", S)
        delt = P.sbuf([128, 2 * TT], F32, "cbdl", S)
        dtef = P.sbuf([128, 2 * TT], F32, "cbdt", S)
        dub = P.sbuf([128, 2 * TT], BF16, "cbdu", S)
        p_bc = P.psum([33, TT], F32, "cbpb", S)
        p_dr = P.psum([128, TT], F32, "cbpd", S)

        kld = P.sem("cbkld")
        wldp = [P.sem("cbw0"), P.sem("cbw1")]
        wcast = P.sem("cbwc")
        ld = P.sem("cbld")
        cvd = P.sem("cbcv")
        xpm = P.sem("cbxm")
        xpe = P.sem("cbxe")
        drb = P.sem("cbdr")
        dpd = P.sem("cbdpd")
        spd = P.sem("cbsp")
        dl = P.sem("cbdle")
        ded = P.sem("cbded")
        dted = P.sem("cbdtd")
        st = P.sem("cbst")
        hz = P.sem("cbhz")
        stdt = [P.sem("cbsd0"), P.sem("cbsd1")]

        NST = (KC + 1) * 16

        @b.sync
        def _(sync):
            sync.dma_start(out=convw[:], in_=H["convw"]).then_inc(kld, 16)
            sync.dma_start(out=convb[:], in_=H["convb"]).then_inc(kld, 16)
            sync.dma_start(out=dtw[:], in_=H["dtw"]).then_inc(kld, 16)
            sync.dma_start(out=dtb[:], in_=H["dtb"]).then_inc(kld, 16)
            sync.dma_start(out=onesk1f[:], in_=H["k_onesk1f"]).then_inc(kld, 16)
            for j in range(KC):
                if j >= 2:
                    sync.wait_ge(wcast, j - 1)
                sync.dma_start(out=wxl[:, (j % 2) * 64:(j % 2) * 64 + 33],
                               in_=H["xprojT"][j * 128:(j + 1) * 128, :]).then_inc(wldp[j % 2], 16)
            for tt in range(NTT):
                tts = tt % NTTS
                if tt >= 1:
                    sync.wait_ge(xpm, tt)
                    sync.wait_ge(dted, 16 * tt)
                for ch in range(KC):
                    if tts == 0:
                        sync.dma_start(out=u0[:, ch * (TT + 3) + 3:(ch + 1) * (TT + 3)],
                                       in_=H["u0_b"][ch * 128:(ch + 1) * 128,
                                                     tt * TT:(tt + 1) * TT]).then_inc(ld, 16)
                    else:
                        sync.dma_start(out=u0[:, ch * (TT + 3):(ch + 1) * (TT + 3)],
                                       in_=H["u0_b"][ch * 128:(ch + 1) * 128,
                                                     tt * TT - 3:(tt + 1) * TT]).then_inc(ld, 16)
                    sync.dma_start(out=zs[:, ch * TT:(ch + 1) * TT],
                                   in_=H["zs_b"][ch * 128:(ch + 1) * 128,
                                                 tt * TT:(tt + 1) * TT]).then_inc(ld, 16)

        @b.gpsimd
        def _(gpsimd):
            gpsimd.wait_ge(kld, 16 * 5)
            for j in range(KC):
                gpsimd.wait_ge(wldp[j % 2], 16 * (j // 2 + 1))
                ins = gpsimd.memset(wxp[:, j * 64:(j + 1) * 64], 0.0)
                P.mark("gpsimd", ins)
                P.join(gpsimd, "gpsimd")
                gpsimd.tensor_copy(wxp[:, j * 64:j * 64 + 33],
                                   wxl[:, (j % 2) * 64:(j % 2) * 64 + 33]).then_inc(wcast, 1)
            for tt in range(NTT):
                tts = tt % NTTS
                gpsimd.wait_ge(ld, 16 * 2 * KC * (tt + 1))
                if tts == 0:
                    if tt >= 1:
                        gpsimd.wait_ge(cvd, 16 * tt)
                    for ch in range(KC):
                        ins = gpsimd.memset(u0[:, ch * (TT + 3):ch * (TT + 3) + 3], 0.0)
                    ins.then_inc(hz, 1)
                for ch in range(KC):
                    gpsimd.wait_ge(dted, 16 * tt + ch + 1)
                    gpsimd.dma_start(out=H["u2_b"][ch * 128:(ch + 1) * 128,
                                                   tt * TT:(tt + 1) * TT],
                                     in_=u2b[:, ch * TT:(ch + 1) * TT]).then_inc(st, 16)
                    gpsimd.dma_start(out=H["dte_f"][ch * 128:(ch + 1) * 128,
                                                    tt * TT:(tt + 1) * TT],
                                     in_=dtef[:, (ch % 2) * TT:(ch % 2 + 1) * TT]
                                     ).then_inc(stdt[ch % 2], 16)
                    gpsimd.dma_start(out=H["du_b"][ch * 128:(ch + 1) * 128,
                                                   tt * TT:(tt + 1) * TT],
                                     in_=dub[:, (ch % 2) * TT:(ch % 2 + 1) * TT]
                                     ).then_inc(stdt[ch % 2], 16)
                gpsimd.wait_ge(xpe, tt + 1)
                gpsimd.dma_start(out=H["bc_f"][:, tt * TT:(tt + 1) * TT],
                                 in_=bcf[:]).then_inc(st, 16)
            gpsimd.wait_ge(st, NST * NTT)
            gpsimd.wait_ge(stdt[0], 32 * 8 * NTT)
            gpsimd.wait_ge(stdt[1], 32 * 8 * NTT)

        @b.vector
        def _(vector):
            for tt in range(NTT):
                vector.wait_ge(ld, 16 * 2 * KC * (tt + 1))
                if tt % NTTS == 0:
                    vector.wait_ge(hz, tt // NTTS + 1)
                if tt >= 1:
                    vector.wait_ge(st, NST * tt)
                for ch in range(KC):
                    par = ch % 2
                    base = ch * (TT + 3)
                    P.join(vector, "vector")
                    ins = vector.tensor_scalar_mul(acc[:, par * TT:(par + 1) * TT],
                                                   u0[:, base:base + TT],
                                                   convw[:, ch * 4:ch * 4 + 1])
                    P.mark("vector", ins)
                    for k in range(1, 4):
                        P.join(vector, "vector")
                        ins = vector.scalar_tensor_tensor(
                            acc[:, par * TT:(par + 1) * TT],
                            u0[:, base + k:base + k + TT],
                            convw[:, ch * 4 + k:ch * 4 + k + 1],
                            acc[:, par * TT:(par + 1) * TT],
                            ALU.mult, ALU.add)
                        P.mark("vector", ins)
                    P.join(vector, "vector")
                    ins = vector.scalar_tensor_tensor(
                        u2b[:, ch * TT:(ch + 1) * TT],
                        acc[:, par * TT:(par + 1) * TT],
                        convb[:, ch:ch + 1],
                        zs[:, ch * TT:(ch + 1) * TT],
                        ALU.add, ALU.mult)
                    ins.then_inc(cvd, 1)
                vector.wait_ge(xpm, tt + 1)
                P.join(vector, "vector")
                ins = vector.tensor_copy(bcf[:], p_bc[:, :])
                ins.then_inc(xpe, 1)
                vector.wait_ge(drb, tt + 1)
                for ch in range(KC):
                    par = ch % 2
                    gc = 16 * tt + ch
                    if gc >= 2:
                        vector.wait_ge(spd, gc - 1)
                    P.join(vector, "vector")
                    ins = vector.tensor_scalar(dpre[:, par * TT:(par + 1) * TT],
                                               p_dr[:, :], dtw[:, ch:ch + 1],
                                               dtb[:, ch:ch + 1], ALU.mult, ALU.add)
                    ins.then_inc(dpd, 1)
                    vector.wait_ge(spd, gc + 1)
                    if gc >= 2:
                        vector.wait_ge(ded, gc - 1)
                    P.join(vector, "vector")
                    ins = vector.scalar_tensor_tensor(
                        delt[:, par * TT:(par + 1) * TT],
                        dpre[:, par * TT:(par + 1) * TT], 0.0,
                        tso[:, par * TT:(par + 1) * TT], ALU.max, ALU.add)
                    ins.then_inc(dl, 1)
                    vector.wait_ge(ded, gc + 1)
                    if gc >= 2:
                        vector.wait_ge(stdt[par], 32 * (gc // 2))
                    P.join(vector, "vector")
                    ins = vector.tensor_mul(dub[:, par * TT:(par + 1) * TT],
                                            dtef[:, par * TT:(par + 1) * TT],
                                            u2b[:, ch * TT:(ch + 1) * TT])
                    ins.then_inc(dted, 1)

        @b.scalar
        def _(scalar):
            for tt in range(NTT):
                for ch in range(KC):
                    par = ch % 2
                    gc = 16 * tt + ch
                    scalar.wait_ge(dpd, gc + 1)
                    if gc >= 2:
                        scalar.wait_ge(dl, gc - 1)
                    ins = scalar.activation(tso[:, par * TT:(par + 1) * TT],
                                            dpre[:, par * TT:(par + 1) * TT], AF.Abs)
                    P.mark("scalar", ins)
                    P.join(scalar, "scalar")
                    ins = scalar.activation(tso[:, par * TT:(par + 1) * TT],
                                            tso[:, par * TT:(par + 1) * TT], AF.Exp,
                                            scale=-1.0)
                    P.mark("scalar", ins)
                    P.join(scalar, "scalar")
                    scalar.activation(tso[:, par * TT:(par + 1) * TT],
                                      tso[:, par * TT:(par + 1) * TT], AF.Ln,
                                      bias=1.0).then_inc(spd, 1)
                    scalar.wait_ge(dl, gc + 1)
                    if gc >= 2:
                        scalar.wait_ge(dted, gc - 1)
                        scalar.wait_ge(stdt[par], 32 * (gc // 2))
                    scalar.activation(dtef[:, par * TT:(par + 1) * TT],
                                      delt[:, par * TT:(par + 1) * TT],
                                      AF.Exp).then_inc(ded, 1)

        @b.tensor
        def _(tensor):
            tensor.wait_ge(wcast, KC)
            tensor.wait_ge(kld, 16 * 5)
            for tt in range(NTT):
                tensor.wait_ge(cvd, 16 * (tt + 1))
                if tt >= 1:
                    tensor.wait_ge(xpe, tt)
                for kc in range(KC):
                    ins = tensor.matmul(p_bc[0:33, :], wxp[:, kc * 64:kc * 64 + 33],
                                        u2b[:, kc * TT:(kc + 1) * TT],
                                        start=(kc == 0), stop=(kc == KC - 1))
                ins.then_inc(xpm, 1)
                tensor.wait_ge(xpe, tt + 1)
                if tt >= 1:
                    tensor.wait_ge(dpd, 16 * tt)
                tensor.matmul(p_dr[:, :], onesk1f[:], bcf[0:1, :],
                              start=True, stop=True).then_inc(drb, 1)


def p11_scan_old(P):
    nc, b, H = P.nc, P.block, P.hbm
    KC = DI // 128
    with contextlib.ExitStack() as S:
        alog = P.sbuf([128, KC * DS], F32, "snal", S)
        asb = P.sbuf([128, KC * DS], F32, "snA", S)
        onesk1f = P.sbuf([1, 128], F32, "snok", S)
        iden = P.sbuf([128, 128], BF16, "snid", S)
        dvec = P.sbuf([128, KC], F32, "snD", S)
        btile = P.sbuf([1, 2 * SEQ], F32, "snbt", S)
        ctile = P.sbuf([1, 2 * SEQ], F32, "snct", S)
        dte = P.sbuf([128, 2 * SEQ], F32, "sndte", S)
        dub = P.sbuf([128, 2 * SEQ], BF16, "sndu", S)
        u2b = P.sbuf([128, 2 * SEQ], BF16, "snu2", S)
        brt = P.sbuf([128, 2 * SEQ], BF16, "snbr", S)
        crt = P.sbuf([128, 2 * SEQ], BF16, "sncr", S)
        af = P.sbuf([128, 2 * SEQ], F32, "sna", S)
        binb = P.sbuf([128, 2 * SEQ], BF16, "snb", S)
        hb = P.sbuf([128, 2 * SEQ], BF16, "snh", S)
        hcb = P.sbuf([128, 2 * SEQ], BF16, "snhc", S)
        yob = P.sbuf([128, 2 * SEQ], BF16, "snyo", S)
        p_bc = [P.psum([128, 512], F32, "snpb", S) for _ in range(2)]
        p_y = [P.psum([128, 512], F32, "snpy", S) for _ in range(4)]

        kld = P.sem("snkld")
        acast = P.sem("snac")
        ldt = P.sem("snldt")
        ldbc = [P.sem("snlb0"), P.sem("snlb1")]
        bcm = P.sem("snbcm")
        bce = P.sem("snbce")
        aex = P.sem("snaex")
        bin_d = P.sem("snbin")
        scn = P.sem("snscn")
        hcd = P.sem("snhc")
        yac = P.sem("snyac")
        yev = P.sem("snyev")
        stp = [P.sem("snst0"), P.sem("snst1")]

        def sidx(bb, ic, s):
            return (bb * KC + ic) * DS + s

        @b.sync
        def _(sync):
            sync.dma_start(out=alog[:].rearrange("p (c s) -> p c s", s=DS),
                           in_=H["alog"].rearrange("(c p) s -> p c s", p=128)
                           ).then_inc(kld, 16)
            sync.dma_start(out=onesk1f[:], in_=H["k_onesk1f"]).then_inc(kld, 16)
            sync.dma_start(out=iden[:], in_=H["k_ident"]).then_inc(kld, 16)
            sync.dma_start(out=dvec[:], in_=H["dvec"]).then_inc(kld, 16)
            for bb in range(BPC):
                for ic in range(KC):
                    u = bb * KC + ic
                    par = u % 2
                    if u >= 2:
                        sync.wait_ge(yev, 4 * (u - 1))
                    for (dst, hn) in ((dte, "dte_f"), (dub, "du_b"), (u2b, "u2_b")):
                        sync.dma_start(out=dst[:, par * SEQ:(par + 1) * SEQ],
                                       in_=H[hn][ic * 128:(ic + 1) * 128,
                                                 bb * SEQ:(bb + 1) * SEQ]).then_inc(ldt, 16)
                    for s in range(DS):
                        g = (bb * KC + ic) * DS + s
                        parg = g % 2
                        if g >= 2:
                            sync.wait_ge(bcm, 8 * (g - 1))
                        sync.dma_start(out=btile[0:1, parg * SEQ:(parg + 1) * SEQ],
                                       in_=H["bc_f"][1 + s:2 + s, bb * SEQ:(bb + 1) * SEQ]
                                       ).then_inc(ldbc[parg], 16)
                        sync.dma_start(out=ctile[0:1, parg * SEQ:(parg + 1) * SEQ],
                                       in_=H["bc_f"][17 + s:18 + s, bb * SEQ:(bb + 1) * SEQ]
                                       ).then_inc(ldbc[parg], 16)

        @b.tensor
        def _(tensor):
            tensor.wait_ge(kld, 16 * 4)
            tensor.wait_ge(acast, KC + 1)
            for bb in range(BPC):
                for ic in range(KC):
                    u = bb * KC + ic
                    for s in range(DS):
                        g = sidx(bb, ic, s)
                        parg = g % 2
                        tensor.wait_ge(ldbc[parg], 32 * (g // 2 + 1))
                        for tc in range(4):
                            if g >= 1 or tc >= 1:
                                tensor.wait_ge(bce, 8 * g + 2 * tc)
                            tensor.matmul(p_bc[0][:, :], onesk1f[:],
                                          btile[0:1, parg * SEQ + tc * 512:
                                                parg * SEQ + (tc + 1) * 512],
                                          start=True, stop=True).then_inc(bcm, 1)
                            tensor.matmul(p_bc[1][:, :], onesk1f[:],
                                          ctile[0:1, parg * SEQ + tc * 512:
                                                parg * SEQ + (tc + 1) * 512],
                                          start=True, stop=True).then_inc(bcm, 1)
                        tensor.wait_ge(hcd, g + 1)
                        if s == 0 and u >= 1:
                            tensor.wait_ge(yev, 4 * u)
                        for tc in range(4):
                            ins = tensor.matmul(
                                p_y[tc][:, :], iden[:],
                                hcb[:, parg * SEQ + tc * 512:parg * SEQ + (tc + 1) * 512],
                                start=(s == 0), stop=(s == DS - 1))
                        ins.then_inc(yac, 1)

        @b.scalar
        def _(scalar):
            scalar.wait_ge(kld, 16 * 4)
            for c in range(KC):
                scalar.activation(asb[:, c * DS:(c + 1) * DS],
                                  alog[:, c * DS:(c + 1) * DS], AF.Exp).then_inc(acast, 1)
            for bb in range(BPC):
                for ic in range(KC):
                    u = bb * KC + ic
                    par = u % 2
                    for s in range(DS):
                        g = sidx(bb, ic, s)
                        parg = g % 2
                        for tc in range(4):
                            scalar.wait_ge(bcm, 8 * g + 2 * tc + 2)
                            if g >= 2 and tc == 0:
                                scalar.wait_ge(bin_d, g - 1)
                                scalar.wait_ge(hcd, g - 1)
                            ins = scalar.activation(
                                brt[:, parg * SEQ + tc * 512:parg * SEQ + (tc + 1) * 512],
                                p_bc[0][:, :], AF.Identity)
                            P.mark("scalar", ins)
                            scalar.activation(
                                crt[:, parg * SEQ + tc * 512:parg * SEQ + (tc + 1) * 512],
                                p_bc[1][:, :], AF.Identity).then_inc(bce, 2)
                        scalar.wait_ge(ldt, 16 * 3 * (u + 1))
                        if g >= 2:
                            scalar.wait_ge(scn, g - 1)
                        scalar.activation(af[:, parg * SEQ:(parg + 1) * SEQ],
                                          dte[:, par * SEQ:(par + 1) * SEQ], AF.Exp,
                                          scale=asb[:, ic * DS + s:ic * DS + s + 1]
                                          ).then_inc(aex, 1)

        @b.gpsimd
        def _(gpsimd):
            for bb in range(BPC):
                for ic in range(KC):
                    u = bb * KC + ic
                    par = u % 2
                    if u >= 1:
                        up = u - 1
                        gpsimd.wait_ge(yev, 4 * (up + 1))
                        gpsimd.dma_start(
                            out=H["y_b"][(up % KC) * 128:(up % KC + 1) * 128,
                                         (up // KC) * SEQ:(up // KC + 1) * SEQ],
                            in_=yob[:, (up % 2) * SEQ:(up % 2 + 1) * SEQ]
                            ).then_inc(stp[up % 2], 16)
                    for s in range(DS):
                        g = sidx(bb, ic, s)
                        parg = g % 2
                        gpsimd.wait_ge(bce, 8 * g + 8)
                        gpsimd.wait_ge(ldt, 16 * 3 * (u + 1))
                        if g >= 2:
                            gpsimd.wait_ge(scn, g - 1)
                        gpsimd.tensor_mul(binb[:, parg * SEQ:(parg + 1) * SEQ],
                                          dub[:, par * SEQ:(par + 1) * SEQ],
                                          brt[:, parg * SEQ:(parg + 1) * SEQ]
                                          ).then_inc(bin_d, 1)
                        gpsimd.wait_ge(scn, g + 1)
                        if g >= 2:
                            gpsimd.wait_ge(yac, g - 1)
                        gpsimd.tensor_mul(hcb[:, parg * SEQ:(parg + 1) * SEQ],
                                          hb[:, parg * SEQ:(parg + 1) * SEQ],
                                          crt[:, parg * SEQ:(parg + 1) * SEQ]).then_inc(hcd, 1)
            up = BPC * KC - 1
            gpsimd.wait_ge(yev, 4 * (up + 1))
            gpsimd.dma_start(
                out=H["y_b"][(up % KC) * 128:(up % KC + 1) * 128,
                             (up // KC) * SEQ:(up // KC + 1) * SEQ],
                in_=yob[:, (up % 2) * SEQ:(up % 2 + 1) * SEQ]).then_inc(stp[up % 2], 16)
            gpsimd.wait_ge(stp[0], 16 * ((BPC * KC + 1) // 2))
            gpsimd.wait_ge(stp[1], 16 * (BPC * KC // 2))

        @b.vector
        def _(vector):
            vector.wait_ge(acast, KC)
            P.join(vector, "vector")
            ins = vector.tensor_scalar_mul(asb[:], asb[:], -1.0)
            ins.then_inc(acast, 1)
            for bb in range(BPC):
                for ic in range(KC):
                    u = bb * KC + ic
                    par = u % 2
                    for s in range(DS):
                        g = sidx(bb, ic, s)
                        parg = g % 2
                        vector.wait_ge(aex, g + 1)
                        vector.wait_ge(bin_d, g + 1)
                        if g >= 2:
                            vector.wait_ge(hcd, g - 1)
                        P.join(vector, "vector")
                        ins = vector.tensor_tensor_scan(
                            hb[:, parg * SEQ:(parg + 1) * SEQ],
                            af[:, parg * SEQ:(parg + 1) * SEQ],
                            binb[:, parg * SEQ:(parg + 1) * SEQ],
                            0.0, ALU.mult, ALU.add)
                        ins.then_inc(scn, 1)
                    vector.wait_ge(yac, sidx(bb, ic, DS - 1) + 1)
                    if u >= 2:
                        vector.wait_ge(stp[u % 2], 16 * (u // 2))
                    for tc in range(4):
                        P.join(vector, "vector")
                        ins = vector.scalar_tensor_tensor(
                            yob[:, par * SEQ + tc * 512:par * SEQ + (tc + 1) * 512],
                            u2b[:, par * SEQ + tc * 512:par * SEQ + (tc + 1) * 512],
                            dvec[:, ic:ic + 1],
                            p_y[tc][:, :], ALU.mult, ALU.add)
                        ins.then_inc(yev, 1)


def p11_scan(P):
    nc, b, H = P.nc, P.block, P.hbm
    KC = DI // 128
    NPAIR = DS * 4
    with contextlib.ExitStack() as S:
        alog = P.sbuf([128, KC * DS], F32, "snal", S)
        asb = P.sbuf([128, KC * DS], F32, "snA", S)
        onesk1f = P.sbuf([1, 128], F32, "snok", S)
        iden = P.sbuf([128, 128], BF16, "snid", S)
        dvec = P.sbuf([128, KC], F32, "snD", S)
        btile = P.sbuf([1, 2 * 512], F32, "snbal", S)
        ctile = P.sbuf([1, 2 * 512], F32, "sncal", S)
        brt_all = P.sbuf([128, DS * SEQ], BF16, "snbra", S)
        crt_all = P.sbuf([128, DS * SEQ], BF16, "sncra", S)
        dte = P.sbuf([128, SEQ], F32, "sndte", S)
        dub = P.sbuf([128, SEQ], BF16, "sndu", S)
        u2b = P.sbuf([128, SEQ], BF16, "snu2", S)
        af = P.sbuf([128, SEQ], F32, "sna", S)
        binb = P.sbuf([128, 2 * SEQ], BF16, "snb", S)
        hb = P.sbuf([128, SEQ], BF16, "snh", S)
        hcb = P.sbuf([128, SEQ], BF16, "snhc", S)
        yob = P.sbuf([128, SEQ], BF16, "snyo", S)
        p_bc = [P.psum([128, 512], F32, "snpb", S) for _ in range(4)]
        p_y = [P.psum([128, 512], F32, "snpy", S) for _ in range(4)]

        kld = P.sem("snkld")
        acast = P.sem("snac")
        bld = P.sem("snbld")
        bcm = P.sem("snbcm")
        bce = P.sem("snbce")
        ldt = P.sem("snldt")
        aex = P.sem("snaex")
        bin_d = P.sem("snbin")
        scn = P.sem("snscn")
        hcd = P.sem("snhc")
        yac = P.sem("snyac")
        yev = P.sem("snyev")
        stp = P.sem("snstp")

        @b.sync
        def _(sync):
            sync.dma_start(out=alog[:].rearrange("p (c s) -> p c s", s=DS),
                           in_=H["alog"].rearrange("(c p) s -> p c s", p=128)
                           ).then_inc(kld, 16)
            sync.dma_start(out=onesk1f[:], in_=H["k_onesk1f"]).then_inc(kld, 16)
            sync.dma_start(out=iden[:], in_=H["k_ident"]).then_inc(kld, 16)
            sync.dma_start(out=dvec[:], in_=H["dvec"]).then_inc(kld, 16)
            for bb in range(BPC):
                for s in range(DS):
                    for tc in range(4):
                        j = bb * NPAIR + s * 4 + tc
                        if j >= 2:
                            sync.wait_ge(bcm, 2 * j - 2)
                        sync.dma_start(
                            out=btile[0:1, (j % 2) * 512:(j % 2 + 1) * 512],
                            in_=H["bc_f"][1 + s:2 + s, bb * SEQ + tc * 512:
                                          bb * SEQ + (tc + 1) * 512]
                            ).then_inc(bld, 16)
                        sync.dma_start(
                            out=ctile[0:1, (j % 2) * 512:(j % 2 + 1) * 512],
                            in_=H["bc_f"][1 + DS + s:2 + DS + s, bb * SEQ + tc * 512:
                                          bb * SEQ + (tc + 1) * 512]
                            ).then_inc(bld, 16)
                for ic in range(KC):
                    u = bb * KC + ic
                    sync.wait_ge(aex, DS * u)
                    sync.wait_ge(bin_d, DS * u)
                    sync.wait_ge(yev, 4 * u)
                    for (dst, hn) in ((dte, "dte_f"), (dub, "du_b"), (u2b, "u2_b")):
                        sync.dma_start(out=dst[:],
                                       in_=H[hn][ic * 128:(ic + 1) * 128,
                                                 bb * SEQ:(bb + 1) * SEQ]
                                       ).then_inc(ldt, 16)

        @b.tensor
        def _(tensor):
            tensor.wait_ge(kld, 16 * 4)
            for bb in range(BPC):
                for s in range(DS):
                    for tc in range(4):
                        j = bb * NPAIR + s * 4 + tc
                        tensor.wait_ge(bld, 32 * (j + 1))
                        if j >= 2:
                            tensor.wait_ge(bce, 2 * j - 2)
                        tensor.matmul(p_bc[j % 2][:, :], onesk1f[:],
                                      btile[0:1, (j % 2) * 512:(j % 2 + 1) * 512],
                                      start=True, stop=True).then_inc(bcm, 1)
                        tensor.matmul(p_bc[2 + j % 2][:, :], onesk1f[:],
                                      ctile[0:1, (j % 2) * 512:(j % 2 + 1) * 512],
                                      start=True, stop=True).then_inc(bcm, 1)
                for ic in range(KC):
                    u = bb * KC + ic
                    for s in range(DS):
                        g = u * DS + s
                        tensor.wait_ge(hcd, g + 1)
                        if s == 0:
                            tensor.wait_ge(yev, 4 * u)
                        for tc in range(4):
                            ins = tensor.matmul(
                                p_y[tc][:, :], iden[:],
                                hcb[:, tc * 512:(tc + 1) * 512],
                                start=(s == 0), stop=(s == DS - 1))
                        ins.then_inc(yac, 1)

        @b.scalar
        def _(scalar):
            scalar.wait_ge(kld, 16 * 4)
            for c in range(KC):
                scalar.activation(asb[:, c * DS:(c + 1) * DS],
                                  alog[:, c * DS:(c + 1) * DS], AF.Exp).then_inc(acast, 1)
            scalar.wait_ge(acast, KC + 1)
            for bb in range(BPC):
                if bb >= 1:
                    scalar.wait_ge(bin_d, DS * KC * bb)
                    scalar.wait_ge(hcd, DS * KC * bb)
                for s in range(DS):
                    for tc in range(4):
                        j = bb * NPAIR + s * 4 + tc
                        scalar.wait_ge(bcm, 2 * j + 2)
                        ins = scalar.activation(
                            brt_all[:, s * SEQ + tc * 512:s * SEQ + (tc + 1) * 512],
                            p_bc[j % 2][:, :], AF.Identity)
                        ins.then_inc(bce, 1)
                        scalar.activation(
                            crt_all[:, s * SEQ + tc * 512:s * SEQ + (tc + 1) * 512],
                            p_bc[2 + j % 2][:, :], AF.Identity).then_inc(bce, 1)
                for ic in range(KC):
                    u = bb * KC + ic
                    for s in range(DS):
                        g = u * DS + s
                        scalar.wait_ge(ldt, 48 * (u + 1))
                        if g >= 1:
                            scalar.wait_ge(scn, g)
                        scalar.activation(af[:], dte[:], AF.Exp,
                                          scale=asb[:, ic * DS + s:ic * DS + s + 1]
                                          ).then_inc(aex, 1)

        @b.gpsimd
        def _(gpsimd):
            for bb in range(BPC):
                for ic in range(KC):
                    u = bb * KC + ic
                    if u >= 1:
                        up = u - 1
                        gpsimd.wait_ge(yev, 4 * u)
                        gpsimd.dma_start(
                            out=H["y_b"][(up % KC) * 128:(up % KC + 1) * 128,
                                         (up // KC) * SEQ:(up // KC + 1) * SEQ],
                            in_=yob[:]).then_inc(stp, 16)
                    for s in range(DS):
                        g = u * DS + s
                        gpsimd.wait_ge(bce, 2 * NPAIR * (bb + 1))
                        gpsimd.wait_ge(ldt, 48 * (u + 1))
                        if g >= 2:
                            gpsimd.wait_ge(scn, g - 1)
                        gpsimd.tensor_mul(binb[:, (g % 2) * SEQ:(g % 2 + 1) * SEQ],
                                          dub[:],
                                          brt_all[:, s * SEQ:(s + 1) * SEQ]
                                          ).then_inc(bin_d, 1)
            up = BPC * KC - 1
            gpsimd.wait_ge(yev, 4 * (up + 1))
            gpsimd.dma_start(
                out=H["y_b"][(up % KC) * 128:(up % KC + 1) * 128,
                             (up // KC) * SEQ:(up // KC + 1) * SEQ],
                in_=yob[:]).then_inc(stp, 16)
            gpsimd.wait_ge(stp, 16 * BPC * KC)

        @b.vector
        def _(vector):
            vector.wait_ge(acast, KC)
            P.join(vector, "vector")
            ins = vector.tensor_scalar_mul(asb[:], asb[:], -1.0)
            ins.then_inc(acast, 1)
            for bb in range(BPC):
                for ic in range(KC):
                    u = bb * KC + ic
                    for s in range(DS):
                        g = u * DS + s
                        vector.wait_ge(aex, g + 1)
                        vector.wait_ge(bin_d, g + 1)
                        P.join(vector, "vector")
                        ins = vector.tensor_tensor_scan(
                            hb[:],
                            af[:],
                            binb[:, (g % 2) * SEQ:(g % 2 + 1) * SEQ],
                            0.0, ALU.mult, ALU.add)
                        ins.then_inc(scn, 1)
                        if g >= 1:
                            vector.wait_ge(yac, g)
                        P.join(vector, "vector")
                        ins = vector.tensor_mul(hcb[:], hb[:],
                                                crt_all[:, s * SEQ:(s + 1) * SEQ])
                        ins.then_inc(hcd, 1)
                    vector.wait_ge(yac, DS * (u + 1))
                    if u >= 1:
                        vector.wait_ge(stp, 16 * u)
                    for tc in range(4):
                        P.join(vector, "vector")
                        ins = vector.scalar_tensor_tensor(
                            yob[:, tc * 512:(tc + 1) * 512],
                            u2b[:, tc * 512:(tc + 1) * 512],
                            dvec[:, ic:ic + 1],
                            p_y[tc][:, :], ALU.mult, ALU.add)
                        ins.then_inc(yev, 1)


def p12_outproj(P):
    emit_proj(P, "y_b", "outprojT", None, [("s1_f", F32)], DI, D)


def p13_moe(P):
    nc, b, H = P.nc, P.block, P.hbm
    NT4 = SEQ // TT
    with contextlib.ExitStack() as S:
        wload = P.sbuf([128, 2 * 128], F32, "mwl", S)
        wd1 = P.sbuf([128, NEXP * NCH * 2 * 128], BF16, "mw1", S)
        wd2 = P.sbuf([128, NEXP * 2 * NCH * 128], BF16, "mw2", S)
        wad = P.sbuf([128, NCH * 4 * 128], BF16, "mwa", S)
        wau = P.sbuf([128, 4 * NCH * 128], BF16, "mwu", S)
        wbs = P.sbuf([128, NCH * NCH * 128], BF16, "mwb", S)
        bd1c = P.sbuf([128, NEXP * 2], F32, "mb1", S)
        bd2c = P.sbuf([128, NEXP * NCH], F32, "mb2", S)
        badcc = P.sbuf([128, 4], F32, "mbad", S)
        bbc = P.sbuf([128, NCH], F32, "mbb", S)
        bauc = P.sbuf([8, D], F32, "mbau", S)
        baub = P.sbuf([8, D], BF16, "mbaub", S)
        ones128f = P.sbuf([128, 1], F32, "mo1", S)
        ones11 = P.sbuf([1, 1], F32, "mo11", S)
        s8c = P.sbuf([8, 128], F32, "ms8", S)
        m84 = P.sbuf([8, 4], F32, "mm84", S)
        ones8 = P.sbuf([8, TT], F32, "mo8", S)
        xn1 = P.sbuf([128, NCH * SEQ], BF16, "mxn", S)
        x1f = P.sbuf([128, NCH * TT], F32, "mx1", S)
        s1f = P.sbuf([128, NCH * TT], F32, "ms1", S)
        h1b = P.sbuf([128, 2 * TT], BF16, "mh1", S)
        db = P.sbuf([128, 2 * TT], BF16, "md", S)
        dsq = P.sbuf([128, 2 * TT], BF16, "mdsq", S)
        lacc = P.sbuf([128, 64], F32, "mla", S)
        ltmp = P.sbuf([128, 2], F32, "mlt", S)
        lrow = P.sbuf([1, 64], F32, "mlr", S)
        loss8 = P.sbuf([1, 8], F32, "mls", S)
        lmin = P.sbuf([1, 1], F32, "mlm", S)
        ohrow = P.sbuf([1, 8], F32, "moh", S)
        oh8 = P.sbuf([8, 1], F32, "mo8c", S)
        rhs84 = P.sbuf([8, 4], F32, "mr84", S)
        ohR = P.sbuf([128, 4], F32, "mohR", S)
        ohones = P.sbuf([8, TT], BF16, "mooh", S)
        arel = P.sbuf([128, 4 * TT], BF16, "mar", S)
        g1s = P.sbuf([128, 2 * TT], BF16, "mg1", S)
        t1 = P.sbuf([128, 2 * TT], F32, "mt1", S)
        outf = P.sbuf([128, 2 * TT], F32, "mout", S)
        pA = [P.psum([128, 512], F32, "mpa", S) for _ in range(2)]
        pB = [P.psum([128, 512], F32, "mpb", S) for _ in range(2)]
        pL = P.psum([128, 512], F32, "mpl", S)

        kld = P.sem("mkld")
        wldp = [P.sem("mw0"), P.sem("mw1s")]
        wcast = P.sem("mwc")
        ldx = P.sem("mldx")
        ldr = P.sem("mldr")
        h1m = P.sem("mh1m")
        h1e = P.sem("mh1e")
        rcm = P.sem("mrcm")
        rce = P.sem("mrce")
        sqd = P.sem("msqd")
        lin = P.sem("mlin")
        ohd = P.sem("mohd")
        ohfin = P.sem("mohf")
        aam = P.sem("maam")
        aae = P.sem("maae")
        asd = P.sem("masd")
        bmm_ = P.sem("mbmm")
        bev = P.sem("mbev")
        fin = P.sem("mfin")
        stp2 = [P.sem("mst0"), P.sem("mst1")]

        NW1 = NEXP * NCH * 2
        NW2 = NEXP * 2 * NCH
        NWA = NCH * 4
        NWU = 4 * NCH
        NWB = NCH * NCH
        WALL = NW1 + NW2 + NWA + NWU + NWB

        def wsrc(j):
            if j < NW1:
                e, kc, mc = j // 16, (j % 16) // 2, j % 2
                return (H["wd1"][e, kc * 128:(kc + 1) * 128, mc * 128:(mc + 1) * 128],
                        wd1[:, j * 128:(j + 1) * 128])
            j2 = j - NW1
            if j2 < NW2:
                e, kc, mc = j2 // 16, (j2 % 16) // 8, j2 % 8
                return (H["wd2"][e, kc * 128:(kc + 1) * 128, mc * 128:(mc + 1) * 128],
                        wd2[:, j2 * 128:(j2 + 1) * 128])
            j3 = j2 - NW2
            if j3 < NWA:
                kc, mc = j3 // 4, j3 % 4
                return (H["wadc"][kc * 128:(kc + 1) * 128, mc * 128:(mc + 1) * 128],
                        wad[:, j3 * 128:(j3 + 1) * 128])
            j4 = j3 - NWA
            if j4 < NWU:
                kc, mc = j4 // 8, j4 % 8
                return (H["wauc"][kc * 128:(kc + 1) * 128, mc * 128:(mc + 1) * 128],
                        wau[:, j4 * 128:(j4 + 1) * 128])
            j5 = j4 - NWU
            kc, mc = j5 // 8, j5 % 8
            return (H["wbase"][kc * 128:(kc + 1) * 128, mc * 128:(mc + 1) * 128],
                    wbs[:, j5 * 128:(j5 + 1) * 128])

        @b.sync
        def _(sync):
            sync.dma_start(out=bd1c[:].rearrange("p (e c) -> p e c", c=2),
                           in_=H["bd1"]).then_inc(kld, 16)
            sync.dma_start(out=bd2c[:].rearrange("p (e c) -> p e c", c=NCH),
                           in_=H["bd2"]).then_inc(kld, 16)
            sync.dma_start(out=badcc[:], in_=H["badc"]).then_inc(kld, 16)
            sync.dma_start(out=bbc[:], in_=H["bbase"]).then_inc(kld, 16)
            sync.dma_start(out=bauc[:], in_=H["bauc"]).then_inc(kld, 16)
            sync.dma_start(out=ones128f[:], in_=H["k_ones128f"]).then_inc(kld, 16)
            sync.dma_start(out=ones11[:], in_=H["k_onesk1f"][:, 0:1]).then_inc(kld, 16)
            sync.dma_start(out=s8c[:], in_=H["k_s8"]).then_inc(kld, 16)
            sync.dma_start(out=m84[:], in_=H["k_m84"]).then_inc(kld, 16)
            sync.dma_start(out=ones8[:], in_=H["k_ones8"]).then_inc(kld, 16)
            for j in range(WALL):
                if j >= 2:
                    sync.wait_ge(wcast, j - 1)
                srcw, _dst = wsrc(j)
                sync.dma_start(out=wload[:, (j % 2) * 128:(j % 2 + 1) * 128],
                               in_=srcw).then_inc(wldp[j % 2], 16)
            for sb in range(BPC):
                if sb >= 1:
                    sync.wait_ge(fin, 8 * NT4 * sb)
                for ch in range(NCH):
                    sync.dma_start(out=xn1[:, ch * SEQ:(ch + 1) * SEQ],
                                   in_=H["xn1_b"][ch * 128:(ch + 1) * 128,
                                                  sb * SEQ:(sb + 1) * SEQ]).then_inc(ldx, 16)
                for t4 in range(NT4):
                    gt = sb * NT4 + t4
                    if gt >= 1:
                        sync.wait_ge(fin, 8 * gt)
                    for ch in range(NCH):
                        sync.dma_start(out=x1f[:, ch * TT:(ch + 1) * TT],
                                       in_=H["x1_f"][ch * 128:(ch + 1) * 128,
                                                     sb * SEQ + t4 * TT:sb * SEQ + (t4 + 1) * TT]
                                       ).then_inc(ldr, 16)
                        sync.dma_start(out=s1f[:, ch * TT:(ch + 1) * TT],
                                       in_=H["s1_f"][ch * 128:(ch + 1) * 128,
                                                     sb * SEQ + t4 * TT:sb * SEQ + (t4 + 1) * TT]
                                       ).then_inc(ldr, 16)

        @b.gpsimd
        def _(gpsimd):
            gpsimd.wait_ge(kld, 16 * 10)
            gpsimd.tensor_copy(baub[:], bauc[:])
            for j in range(WALL):
                gpsimd.wait_ge(wldp[j % 2], 16 * (j // 2 + 1))
                _src, dstw = wsrc(j)
                gpsimd.tensor_copy(dstw, wload[:, (j % 2) * 128:(j % 2 + 1) * 128]
                                   ).then_inc(wcast, 1)
            for sb in range(BPC):
                for t4 in range(NT4):
                    gt = sb * NT4 + t4
                    for mc in range(NCH):
                        gf = gt * NCH + mc
                        gpsimd.wait_ge(fin, gf + 1)
                        gpsimd.dma_start(
                            out=H["outT"][mc * 128:(mc + 1) * 128,
                                          sb * SEQ + t4 * TT:sb * SEQ + (t4 + 1) * TT],
                            in_=outf[:, (gf % 2) * TT:(gf % 2 + 1) * TT]
                            ).then_inc(stp2[gf % 2], 16)
            NGF = BPC * NT4 * NCH
            gpsimd.wait_ge(stp2[0], 16 * ((NGF + 1) // 2))
            gpsimd.wait_ge(stp2[1], 16 * (NGF // 2))

        @b.tensor
        def _(tensor):
            tensor.wait_ge(wcast, WALL)
            tensor.wait_ge(kld, 16 * 10)
            for sb in range(BPC):
                tensor.wait_ge(ldx, 16 * NCH * (sb + 1))
                for t4 in range(NT4):
                    gt = sb * NT4 + t4
                    for e in range(NEXP):
                        ge = gt * NEXP + e
                        for mc in range(2):
                            ih = 2 * ge + mc
                            if ih >= 2:
                                tensor.wait_ge(h1e, ih - 1)
                            for kc in range(NCH):
                                ins = tensor.matmul(
                                    pA[mc % 2][:, :],
                                    wd1[:, (e * 16 + kc * 2 + mc) * 128:
                                        (e * 16 + kc * 2 + mc + 1) * 128],
                                    xn1[:, kc * SEQ + t4 * TT:kc * SEQ + (t4 + 1) * TT],
                                    start=(kc == 0), stop=(kc == NCH - 1))
                            ins.then_inc(h1m, 1)
                        tensor.wait_ge(h1e, 2 * ge + 2)
                        for mc in range(NCH):
                            ir = 8 * ge + mc
                            if ir >= 2:
                                tensor.wait_ge(rce, ir - 1)
                            for kc in range(2):
                                ins = tensor.matmul(
                                    pB[mc % 2][:, :],
                                    wd2[:, (e * 16 + kc * 8 + mc) * 128:
                                        (e * 16 + kc * 8 + mc + 1) * 128],
                                    h1b[:, kc * TT:(kc + 1) * TT],
                                    start=(kc == 0), stop=(kc == 1))
                            ins.then_inc(rcm, 1)
                tensor.wait_ge(sqd, 8 * NEXP * NT4 * (sb + 1))
                if sb >= 1:
                    tensor.wait_ge(ohd, 6 * sb)
                tensor.matmul(pL[0:1, 0:64], ones128f[:], lacc[:], start=True,
                              stop=True).then_inc(ohd, 1)
                tensor.wait_ge(ohd, 6 * sb + 3)
                tensor.matmul(pL[0:8, 64:65], ohrow[:], ones11[:], start=True,
                              stop=True).then_inc(ohd, 1)
                tensor.wait_ge(ohd, 6 * sb + 5)
                tensor.matmul(pL[0:128, 128:132], s8c[:], rhs84[:], start=True,
                              stop=True).then_inc(ohd, 1)
                tensor.wait_ge(ohfin, 2 * (sb + 1))
                for t4 in range(NT4):
                    gt = sb * NT4 + t4
                    for mc in range(4):
                        ia = 4 * gt + mc
                        if ia >= 2:
                            tensor.wait_ge(aae, ia - 1)
                        for kc in range(NCH):
                            ins = tensor.matmul(
                                pA[mc % 2][:, :],
                                wad[:, (kc * 4 + mc) * 128:(kc * 4 + mc + 1) * 128],
                                xn1[:, kc * SEQ + t4 * TT:kc * SEQ + (t4 + 1) * TT],
                                start=(kc == 0), stop=(kc == NCH - 1))
                        ins.then_inc(aam, 1)
                    tensor.wait_ge(asd, 4 * (gt + 1))
                    for mc in range(NCH):
                        gf = gt * NCH + mc
                        if gf >= 2:
                            tensor.wait_ge(bev, gf - 1)
                        for kc in range(NCH):
                            tensor.matmul(
                                pB[mc % 2][:, :],
                                wbs[:, (kc * 8 + mc) * 128:(kc * 8 + mc + 1) * 128],
                                xn1[:, kc * SEQ + t4 * TT:kc * SEQ + (t4 + 1) * TT],
                                start=(kc == 0), stop=False)
                        for kc in range(4):
                            tensor.matmul(
                                pB[mc % 2][:, :],
                                wau[:, (kc * 8 + mc) * 128:(kc * 8 + mc + 1) * 128],
                                arel[:, kc * TT:(kc + 1) * TT],
                                start=False, stop=False)
                        ins = tensor.matmul(pB[mc % 2][:, :],
                                            baub[:, mc * 128:(mc + 1) * 128],
                                            ohones[:], start=False, stop=True)
                        ins.then_inc(bmm_, 1)

        @b.scalar
        def _(scalar):
            for sb in range(BPC):
                for t4 in range(NT4):
                    gt = sb * NT4 + t4
                    for e in range(NEXP):
                        ge = gt * NEXP + e
                        for mc in range(2):
                            ih = 2 * ge + mc
                            scalar.wait_ge(h1m, ih + 1)
                            if ge >= 1 and mc == 0:
                                scalar.wait_ge(rcm, 8 * ge)
                            scalar.activation(h1b[:, mc * TT:(mc + 1) * TT],
                                              pA[mc % 2][:, :], AF.Relu,
                                              bias=bd1c[:, e * 2 + mc:e * 2 + mc + 1]
                                              ).then_inc(h1e, 1)
                for t4 in range(NT4):
                    gt = sb * NT4 + t4
                    for mc in range(4):
                        ia = 4 * gt + mc
                        scalar.wait_ge(aam, ia + 1)
                        if gt >= 1 and mc == 0:
                            scalar.wait_ge(bmm_, 8 * gt)
                        scalar.activation(arel[:, mc * TT:(mc + 1) * TT],
                                          pA[mc % 2][:, :], AF.Relu,
                                          bias=badcc[:, mc:mc + 1]).then_inc(aae, 1)
                    for mc in range(NCH):
                        gf = gt * NCH + mc
                        scalar.wait_ge(bmm_, gf + 1)
                        if gf >= 2:
                            scalar.wait_ge(fin, gf - 1)
                        scalar.activation(g1s[:, (gf % 2) * TT:(gf % 2 + 1) * TT],
                                          pB[mc % 2][:, :], AF.Silu,
                                          bias=bbc[:, mc:mc + 1]).then_inc(bev, 1)

        @b.vector
        def _(vector):
            for sb in range(BPC):
                if sb >= 1:
                    vector.wait_ge(ohd, 6 * sb)
                P.join(vector, "vector")
                ins = vector.memset(lacc[:], 0.0)
                ins.then_inc(lin, 1)
                for t4 in range(NT4):
                    gt = sb * NT4 + t4
                    for e in range(NEXP):
                        ge = gt * NEXP + e
                        for mc in range(NCH):
                            ir = 8 * ge + mc
                            parr = ir % 2
                            vector.wait_ge(rcm, ir + 1)
                            if ir >= 2:
                                vector.wait_ge(sqd, ir - 1)
                            P.join(vector, "vector")
                            ins = vector.scalar_tensor_tensor(
                                db[:, parr * TT:(parr + 1) * TT],
                                pB[mc % 2][:, :],
                                bd2c[:, e * NCH + mc:e * NCH + mc + 1],
                                xn1[:, mc * SEQ + t4 * TT:mc * SEQ + (t4 + 1) * TT],
                                ALU.add, ALU.subtract)
                            ins.then_inc(rce, 1)
                            vector.wait_ge(rce, ir + 1)
                            ins = vector.scalar_tensor_tensor(
                                dsq[:, parr * TT:(parr + 1) * TT],
                                db[:, parr * TT:(parr + 1) * TT], 1.0,
                                db[:, parr * TT:(parr + 1) * TT],
                                ALU.mult, ALU.mult,
                                accum_out=ltmp[:, parr:parr + 1])
                            P.mark("vector", ins)
                            P.join(vector, "vector")
                            ins = vector.tensor_add(lacc[:, e * NCH + mc:e * NCH + mc + 1],
                                                    lacc[:, e * NCH + mc:e * NCH + mc + 1],
                                                    ltmp[:, parr:parr + 1])
                            ins.then_inc(sqd, 1)
                vector.wait_ge(ohd, 6 * sb + 1)
                P.join(vector, "vector")
                ins = vector.tensor_copy(lrow[:], pL[0:1, 0:64])
                P.mark("vector", ins)
                P.join(vector, "vector")
                ins = vector.tensor_reduce(loss8[:],
                                           lrow[:].rearrange("o (e c) -> o e c", c=NCH),
                                           AX.X, ALU.add)
                P.mark("vector", ins)
                P.join(vector, "vector")
                ins = vector.tensor_reduce(lmin[:], loss8[:], AX.X, ALU.min)
                P.mark("vector", ins)
                P.join(vector, "vector")
                ins = vector.tensor_single_scalar(ohrow[:], loss8[:], lmin[:, 0:1],
                                                  ALU.is_le)
                ins.then_inc(ohd, 2)
                vector.wait_ge(ohd, 6 * sb + 4)
                P.join(vector, "vector")
                ins = vector.tensor_copy(oh8[:], pL[0:8, 64:65])
                P.mark("vector", ins)
                P.join(vector, "vector")
                ins = vector.tensor_scalar_mul(rhs84[:], m84[:], oh8[:, 0:1])
                ins.then_inc(ohd, 1)
                vector.wait_ge(ohd, 6 * sb + 6)
                P.join(vector, "vector")
                ins = vector.tensor_copy(ohR[:], pL[0:128, 128:132])
                ins.then_inc(ohfin, 1)
                P.join(vector, "vector")
                ins = vector.tensor_scalar_mul(ohones[:], ones8[:], oh8[:, 0:1])
                ins.then_inc(ohfin, 1)
                for t4 in range(NT4):
                    gt = sb * NT4 + t4
                    vector.wait_ge(ldr, 16 * 2 * NCH * (gt + 1))
                    for mc in range(4):
                        ia = 4 * gt + mc
                        vector.wait_ge(aae, ia + 1)
                        P.join(vector, "vector")
                        ins = vector.tensor_scalar_mul(arel[:, mc * TT:(mc + 1) * TT],
                                                       arel[:, mc * TT:(mc + 1) * TT],
                                                       ohR[:, mc:mc + 1])
                        ins.then_inc(asd, 1)
                    for mc in range(NCH):
                        gf = gt * NCH + mc
                        vector.wait_ge(bev, gf + 1)
                        if gf >= 2:
                            vector.wait_ge(stp2[gf % 2], 16 * (gf // 2))
                        P.join(vector, "vector")
                        ins = vector.tensor_mul(t1[:, (gf % 2) * TT:(gf % 2 + 1) * TT],
                                                s1f[:, mc * TT:(mc + 1) * TT],
                                                g1s[:, (gf % 2) * TT:(gf % 2 + 1) * TT])
                        P.mark("vector", ins)
                        P.join(vector, "vector")
                        ins = vector.tensor_add(outf[:, (gf % 2) * TT:(gf % 2 + 1) * TT],
                                                x1f[:, mc * TT:(mc + 1) * TT],
                                                t1[:, (gf % 2) * TT:(gf % 2 + 1) * TT])
                        ins.then_inc(fin, 1)


def _mk_attn(l):
    src_n = "xn_f" if l == 0 else "gb_f"
    return (f"p3l{l}a", lambda P, l=l, s=src_n: emit_l0_attn(P, l, s, "ga_f"),
            [src_n] + l0_attn_consts(l), ["ga_f"])


def _mk_ffn(l):
    return (f"p3l{l}f", lambda P, l=l: emit_l0_ffn(P, l, "ga_f", "gb_f"),
            ["ga_f"] + l0_ffn_consts(l), ["gb_f"])


PHASES_TAIL = [
    ("p1", p1_rnn, ["u_b", "g_f", "inp_f", "aT", "k_ident"], ["states_b"]),
    ("p2", p2_wout, ["states_b", "w_outT", "b_out", "k_ones1"], ["s0_f"]),
] + [p for l in range(L0DEPTH) for p in (_mk_attn(l), _mk_ffn(l))] + [
    ("p9", p9_combine, ["xT", "s0_f", "gb_f", "k_ones128", "k_onesk1f"],
     ["x1_f", "xn1_b"]),
    ("p10a", p10a_inproj, ["xn1_b", "inprojT"], ["u0_b", "zs_b"]),
    ("p10b", p10b_conv, ["u0_b", "zs_b", "convw", "convb", "dtw", "dtb", "xprojT",
                         "k_onesk1f"], ["u2_b", "dte_f", "du_b", "bc_f"]),
    ("p11", p11_scan, ["dte_f", "du_b", "u2_b", "bc_f", "alog", "dvec",
                       "k_onesk1f", "k_ident"], ["y_b"]),
    ("p12", p12_outproj, ["y_b", "outprojT", "k_ones1"], ["s1_f"]),
    ("p13", p13_moe, ["xn1_b", "x1_f", "s1_f", "wd1", "bd1", "wd2", "bd2", "wadc",
                      "badc", "wauc", "bauc", "wbase", "bbase", "k_ones128f",
                      "k_onesk1f", "k_s8", "k_m84", "k_ones8"], ["outT"]),
]
PHASES = [
    ("p0", p0_ln_proj,
     ["xT", "w_inT", "w_gateT", "w_bT", "b_in", "b_gate",
      "k_ones1", "k_ones128", "k_onesk1", "k_onesk1f"],
     ["xn_f", "xn_b", "u_b", "g_f", "inp_f"]),
] + PHASES_TAIL


def _np_dt(dt):
    return {F32: np.float32, BF16: ml_dtypes.bfloat16}[dt]


def run_phases(cores_host, phases=None, state=None):
    reg = hbm_registry()
    if state is None:
        state = [dict(c) for c in cores_host]
    plist = PHASES if phases is None else [p for p in PHASES if p[0] in phases]
    for (name, emit, reads, writes) in plist:
        nc = bass.Bass("TRN2", target_bir_lowering=False)
        hbm = {}
        for r in list(dict.fromkeys(list(reads) + ["k_scal", "k_scal_b"])):
            sh, dt, _ = reg[r]
            hbm[r] = nc.dram_tensor(r, sh, dt, kind="ExternalInput").ap()
        for w in writes:
            sh, dt, _ = reg[w]
            hbm[w] = nc.dram_tensor(w, sh, dt, kind="ExternalOutput").ap()
        with contextlib.ExitStack() as es:
            block = es.enter_context(nc.Block())
            P = Prog(nc, block, es, hbm)
            emit_setup(P)
            emit(P)
        rdall = list(dict.fromkeys(list(reads) + ["k_scal", "k_scal_b"]))
        in_maps = [{r: np.ascontiguousarray(state[c][r], _np_dt(reg[r][1]))
                    for r in rdall} for c in range(NCORE)]
        res = run_bass_kernel_spmd(nc, in_maps, core_ids=list(range(NCORE)))
        for c in range(NCORE):
            state[c].update(res.results[c])
    return state


def run_fused(cores_host):
    reg = hbm_registry()
    nc = bass.Bass("TRN2", target_bir_lowering=False)
    hbm = {}
    for name, (sh, dt, is_in) in reg.items():
        if is_in:
            hbm[name] = nc.dram_tensor(name, sh, dt, kind="ExternalInput").ap()
        elif name == "outT":
            hbm[name] = nc.dram_tensor(name, sh, dt, kind="ExternalOutput").ap()
        else:
            hbm[name] = nc.dram_tensor(name, sh, dt, kind="Internal").ap()
    with contextlib.ExitStack() as es:
        block = es.enter_context(nc.Block())
        P = Prog(nc, block, es, hbm, pooled=(os.environ.get("FUSE_POOL", "1") == "1"))
        if P.pooled:
            P.init_pool(64)
        emit_setup(P)
        plist = PHASES[:int(os.environ.get("FUSE_N", str(len(PHASES))))]
        skip = set(os.environ.get("FUSE_SKIP", "").split(","))
        plist = [p for p in plist if p[0] not in skip]
        for i, (name, emit, reads, writes) in enumerate(plist):
            emit(P)
            if i < len(plist) - 1:
                P.phase_boundary()
    in_names = [n for n, (sh, dt, is_in) in reg.items() if is_in]
    in_maps = [{n: np.ascontiguousarray(cores_host[c][n], _np_dt(reg[n][1]))
                for n in in_names} for c in range(NCORE)]
    res = run_bass_kernel_spmd(nc, in_maps, core_ids=list(range(NCORE)))
    return [{"outT": res.results[c]["outT"]} for c in range(NCORE)]


def kernel(**inputs):
    cores_host = build_host_inputs(inputs)
    if FUSED:
        state = run_fused(cores_host)
    else:
        state = run_phases(cores_host)
    outs = []
    for c in range(NCORE):
        oT = np.asarray(state[c]["outT"], np.float32)
        outs.append(oT.reshape(D, BPC, SEQ).transpose(1, 2, 0))
    return np.concatenate(outs, 0)


# version 14 (final)
# speedup vs baseline: 1.0491x; 1.0491x over previous
import os
import math
import contextlib
import numpy as np
import ml_dtypes
import concourse.bass as bass
import concourse.mybir as mybir
from concourse.bass_utils import run_bass_kernel_spmd

F32 = mybir.dt.float32
BF16 = mybir.dt.bfloat16
AF = mybir.ActivationFunctionType
ALU = mybir.AluOpType
AX = mybir.AxisListType

NCORE = 8
SEQ = 2048
D = 1024
BPC = 2
TOK = BPC * SEQ
TT = 512
NTT = TOK // TT
NTTS = SEQ // TT
NCH = D // 128

HEADS = 16
DH = 64
WIN = 256
INNER = 1024
L0DEPTH = 6
DIN = 2730
DPAD = 2816
NH1 = 2 * DPAD
QK_SCALE = 8.0

DI = 2048
DS = 16
DC = 4
NEXP = 8
EH = 256
ER = 64

FUSED = True

CONST_VALS = [1e-5, 1.0 / 1024.0, 1e-12, -1.0, 0.5, 2.0]
P0_STAGE = int(os.environ.get('P0_STAGE', '2'))


def hbm_registry():
    R = {}
    R["xT"] = ([D, TOK], F32, True)
    R["w_inT"] = ([D, D], F32, True)
    R["w_gateT"] = ([D, D], F32, True)
    R["w_bT"] = ([D, D], F32, True)
    R["w_outT"] = ([D, D], F32, True)
    R["aT"] = ([D, D], F32, True)
    R["b_in"] = ([1, D], F32, True)
    R["b_gate"] = ([1, D], F32, True)
    R["b_out"] = ([1, D], F32, True)
    for l in range(L0DEPTH):
        R[f"wqkv{l}"] = ([D, 3 * INNER], F32, True)
        R[f"wo{l}"] = ([INNER, D], F32, True)
        R[f"w1p{l}"] = ([D, NH1], F32, True)
        R[f"b1p{l}"] = ([128, NH1 // 128], F32, True)
        R[f"w2p{l}"] = ([DPAD, D], F32, True)
        R[f"b2{l}"] = ([128, D // 128], F32, True)
        R[f"qsrep{l}"] = ([128, 1], F32, True)
        R[f"ksrep{l}"] = ([128, 1], F32, True)
        R[f"lng{l}"] = ([128, NCH], F32, True)
        R[f"lnb{l}"] = ([128, NCH], F32, True)
    R["inprojT"] = ([D, 2 * DI], F32, True)
    R["convw"] = ([128, DI // 128 * DC], F32, True)
    R["convb"] = ([128, DI // 128], F32, True)
    R["xprojT"] = ([DI, 33], F32, True)
    R["dtw"] = ([128, DI // 128], F32, True)
    R["dtb"] = ([128, DI // 128], F32, True)
    R["alog"] = ([DI, DS], F32, True)
    R["dvec"] = ([128, DI // 128], F32, True)
    R["outprojT"] = ([DI, D], F32, True)
    R["wbase"] = ([D, D], F32, True)
    R["bbase"] = ([128, D // 128], F32, True)
    R["wd1"] = ([NEXP, D, EH], F32, True)
    R["bd1"] = ([NEXP, 128, EH // 128], F32, True)
    R["wd2"] = ([NEXP, EH, D], F32, True)
    R["bd2"] = ([NEXP, 128, D // 128], F32, True)
    R["wadc"] = ([D, NEXP * ER], F32, True)
    R["badc"] = ([128, NEXP * ER // 128], F32, True)
    R["wauc"] = ([NEXP * ER, D], F32, True)
    R["bauc"] = ([NEXP, D], F32, True)
    R["k_ident"] = ([128, 128], BF16, True)
    R["k_ones1"] = ([1, TT], BF16, True)
    R["k_ones128"] = ([128, 1], BF16, True)
    R["k_onesk1"] = ([1, 128], BF16, True)
    R["k_onesk1f"] = ([1, 128], F32, True)
    R["k_bd2h"] = ([128, 2], BF16, True)
    R["k_sel2"] = ([2, 128], BF16, True)
    R["k_rotT"] = ([128, 128], BF16, True)
    R["k_maskband"] = ([128, 384], BF16, True)
    R["k_sel2f"] = ([2, 128], F32, True)
    R["k_s8"] = ([8, 128], F32, True)
    R["k_m84"] = ([8, 4], F32, True)
    R["k_ones8"] = ([8, TT], F32, True)
    R["k_ones128f"] = ([128, 1], F32, True)
    R["k_maskUT"] = ([128, 128], BF16, True)
    R["k_maskLT"] = ([128, 128], BF16, True)
    R["k_scal"] = ([1, 16], F32, True)
    R["k_scal_b"] = ([1, 16], BF16, True)
    R["t_cosq"] = ([128, SEQ], F32, True)
    R["t_sinq"] = ([128, SEQ], F32, True)
    R["t_cosk"] = ([128, SEQ], F32, True)
    R["t_sink"] = ([128, SEQ], F32, True)
    R["xn_f"] = ([D, TOK], F32, False)
    R["xn_b"] = ([D, TOK], BF16, False)
    R["u_b"] = ([D, TOK], BF16, False)
    R["g_f"] = ([D, TOK], F32, False)
    R["inp_f"] = ([D, TOK], F32, False)
    R["states_b"] = ([D, TOK], BF16, False)
    R["s0_f"] = ([D, TOK], F32, False)
    R["ga_f"] = ([D, TOK], F32, False)
    R["gb_f"] = ([D, TOK], F32, False)
    R["x1_f"] = ([D, TOK], F32, False)
    R["xn1_b"] = ([D, TOK], BF16, False)
    R["u0_b"] = ([DI, TOK], BF16, False)
    R["zs_b"] = ([DI, TOK], BF16, False)
    R["u2_b"] = ([DI, TOK], BF16, False)
    R["dte_f"] = ([DI, TOK], F32, False)
    R["du_b"] = ([DI, TOK], BF16, False)
    R["bc_f"] = ([33, TOK], F32, False)
    R["y_b"] = ([DI, TOK], BF16, False)
    R["s1_f"] = ([D, TOK], F32, False)
    R["outT"] = ([D, TOK], F32, False)
    return R


def build_host_inputs(inp):
    bf = ml_dtypes.bfloat16
    shared = {}
    shared["w_inT"] = np.ascontiguousarray(inp["L0_attn_W_in"].T)
    shared["w_gateT"] = np.ascontiguousarray(inp["L0_attn_W_gate"].T)
    shared["w_bT"] = np.ascontiguousarray(inp["L0_attn_B"].T)
    shared["w_outT"] = np.ascontiguousarray(inp["L0_attn_W_out"].T)
    shared["aT"] = np.ascontiguousarray(inp["L0_attn_A"].T)
    shared["b_in"] = inp["L0_attn_b_in"].reshape(1, D)
    shared["b_gate"] = inp["L0_attn_b_gate"].reshape(1, D)
    shared["b_out"] = inp["L0_attn_b_out"].reshape(1, D)

    shared["wqkv"] = np.ascontiguousarray(inp["L0_ffn_Wqkv"])
    shared["wo"] = np.ascontiguousarray(inp["L0_ffn_Wo"])
    w1 = inp["L0_ffn_W1"]
    b1 = inp["L0_ffn_b1"]
    w1a = np.zeros((L0DEPTH, D, DPAD), np.float32)
    w1g = np.zeros((L0DEPTH, D, DPAD), np.float32)
    w1a[:, :, :DIN] = w1[:, :, :DIN]
    w1g[:, :, :DIN] = w1[:, :, DIN:]
    b1a = np.zeros((L0DEPTH, DPAD), np.float32)
    b1g = np.zeros((L0DEPTH, DPAD), np.float32)
    b1a[:, :DIN] = b1[:, :DIN]
    b1g[:, :DIN] = b1[:, DIN:]
    w1i = np.zeros((L0DEPTH, D, NH1), np.float32)
    b1i = np.zeros((L0DEPTH, 1, NH1), np.float32)
    nblk = DPAD // 128
    for j in range(nblk):
        w1i[:, :, (2 * j) * 128:(2 * j + 1) * 128] = w1a[:, :, j * 128:(j + 1) * 128]
        w1i[:, :, (2 * j + 1) * 128:(2 * j + 2) * 128] = w1g[:, :, j * 128:(j + 1) * 128]
        b1i[:, 0, (2 * j) * 128:(2 * j + 1) * 128] = b1a[:, j * 128:(j + 1) * 128]
        b1i[:, 0, (2 * j + 1) * 128:(2 * j + 2) * 128] = b1g[:, j * 128:(j + 1) * 128]
    shared["w1p"] = w1i
    shared["b1p"] = np.ascontiguousarray(b1i.reshape(L0DEPTH, NH1 // 128, 128).transpose(0, 2, 1))
    w2 = np.zeros((L0DEPTH, DPAD, D), np.float32)
    w2[:, :DIN, :] = inp["L0_ffn_W2"]
    shared["w2p"] = w2
    shared["b2"] = np.ascontiguousarray(
        inp["L0_ffn_b2"].reshape(L0DEPTH, D // 128, 128).transpose(0, 2, 1))
    shared["qsrep"] = np.tile(inp["L0_ffn_q_scale"], (1, 2)).reshape(L0DEPTH, 128, 1)
    shared["ksrep"] = np.tile(inp["L0_ffn_k_scale"], (1, 2)).reshape(L0DEPTH, 128, 1)
    shared["lng"] = np.ascontiguousarray(
        inp["L0_ffn_ln_g"].reshape(L0DEPTH, NCH, 128).transpose(0, 2, 1))
    shared["lnb"] = np.ascontiguousarray(
        inp["L0_ffn_ln_b"].reshape(L0DEPTH, NCH, 128).transpose(0, 2, 1))

    shared["inprojT"] = np.ascontiguousarray(inp["L1_attn_in_proj_w"].T)
    shared["convw"] = np.ascontiguousarray(
        inp["L1_attn_conv_w"][:, 0, :].reshape(DI // 128, 128, DC)
        .transpose(1, 0, 2).reshape(128, DI // 128 * DC))
    shared["convb"] = np.ascontiguousarray(inp["L1_attn_conv_b"].reshape(DI // 128, 128).T)
    shared["xprojT"] = np.ascontiguousarray(inp["L1_attn_x_proj_w"].T)
    shared["dtw"] = np.ascontiguousarray(inp["L1_attn_dt_proj_w"].reshape(DI // 128, 128).T)
    shared["dtb"] = np.ascontiguousarray(inp["L1_attn_dt_proj_b"].reshape(DI // 128, 128).T)
    shared["alog"] = np.ascontiguousarray(inp["L1_attn_A_log"])
    shared["dvec"] = np.ascontiguousarray(inp["L1_attn_D"].reshape(DI // 128, 128).T)
    shared["outprojT"] = np.ascontiguousarray(inp["L1_attn_out_proj_w"].T)

    shared["wbase"] = np.ascontiguousarray(inp["L1_ffn_W_base"])
    shared["bbase"] = np.ascontiguousarray(
        inp["L1_ffn_b_base"].reshape(D // 128, 128).T)
    shared["wd1"] = np.ascontiguousarray(inp["L1_ffn_Wd1"])
    shared["bd1"] = np.ascontiguousarray(
        inp["L1_ffn_bd1"].reshape(NEXP, EH // 128, 128).transpose(0, 2, 1))
    shared["wd2"] = np.ascontiguousarray(inp["L1_ffn_Wd2"])
    shared["bd2"] = np.ascontiguousarray(
        inp["L1_ffn_bd2"].reshape(NEXP, D // 128, 128).transpose(0, 2, 1))
    shared["wadc"] = np.ascontiguousarray(
        inp["L1_ffn_Wa_down"].transpose(1, 0, 2).reshape(D, NEXP * ER))
    shared["badc"] = np.ascontiguousarray(
        inp["L1_ffn_ba_down"].reshape(NEXP * ER // 128, 128).T)
    shared["wauc"] = np.ascontiguousarray(inp["L1_ffn_Wa_up"].reshape(NEXP * ER, D))
    shared["bauc"] = np.ascontiguousarray(inp["L1_ffn_ba_up"])

    shared["k_scal"] = np.zeros((1, 16), np.float32)
    shared["k_scal"][0, :len(CONST_VALS)] = CONST_VALS
    shared["k_scal_b"] = shared["k_scal"].astype(bf)
    shared["k_ident"] = np.eye(128, dtype=np.float32).astype(bf)
    shared["k_ones1"] = np.ones((1, TT), np.float32).astype(bf)
    shared["k_ones128"] = np.ones((128, 1), np.float32).astype(bf)
    shared["k_onesk1"] = np.ones((1, 128), np.float32).astype(bf)
    shared["k_onesk1f"] = np.ones((1, 128), np.float32)
    bd2h = np.zeros((128, 2), np.float32)
    bd2h[:64, 0] = 1.0
    bd2h[64:, 1] = 1.0
    shared["k_bd2h"] = bd2h.astype(bf)
    shared["k_sel2"] = np.ascontiguousarray(bd2h.T).astype(bf)
    shared["k_sel2f"] = np.ascontiguousarray(bd2h.T).astype(np.float32)
    s8 = np.zeros((8, 128), np.float32)
    for e in range(8):
        s8[e, (e % 2) * 64:(e % 2) * 64 + 64] = 1.0
    shared["k_s8"] = s8
    m84 = np.zeros((8, 4), np.float32)
    for e in range(8):
        m84[e, e // 2] = 1.0
    shared["k_m84"] = m84
    shared["k_ones8"] = np.ones((8, TT), np.float32)
    shared["k_ones128f"] = np.ones((128, 1), np.float32)
    mb = np.zeros((128, 384), np.float32)
    ii = np.arange(128)[:, None]
    jj = np.arange(128)[None, :]
    mb[:, 0:128] = np.where(jj >= ii, 0.0, -1e30)
    mb[:, 256:384] = np.where(jj <= ii, 0.0, -1e30)
    shared["k_maskband"] = mb
    r1 = np.zeros((64, 64), np.float32)
    r1[0:32, 32:64] = -np.eye(32)
    r1[32:64, 0:32] = np.eye(32)
    R = np.zeros((128, 128), np.float32)
    R[:64, :64] = r1
    R[64:, 64:] = r1
    shared["k_rotT"] = np.ascontiguousarray(R.T).astype(bf)
    mm, ii = np.meshgrid(np.arange(128), np.arange(128), indexing="ij")
    shared["k_maskUT"] = (mm >= ii).astype(np.float32).astype(bf)
    shared["k_maskLT"] = (mm <= ii).astype(np.float32).astype(bf)
    half = np.arange(0, DH, 2, dtype=np.float32)
    inv_freq = 1.0 / 10000.0 ** (half / DH)
    t = np.arange(SEQ, dtype=np.float32)
    freqs = t[:, None] * inv_freq[None, :]
    freqs = np.concatenate([freqs, freqs], -1)
    base_scale = (half + 0.4 * DH) / (1.4 * DH)
    power = (t - SEQ // 2) / (WIN // 2)
    scale = base_scale[None, :] ** power[:, None]
    scale = np.concatenate([scale, scale], -1)
    cos = np.cos(freqs)
    sin = np.sin(freqs)
    sig = np.concatenate([np.arange(32, 64), np.arange(0, 32)])
    def dup(a):
        return np.ascontiguousarray(np.tile(a.T, (2, 1)).astype(np.float32))
    shared["t_cosq"] = dup(cos * scale * QK_SCALE)
    shared["t_sinq"] = dup((sin * scale)[:, sig] * QK_SCALE)
    shared["t_cosk"] = dup(cos / scale)
    shared["t_sink"] = dup((sin / scale)[:, sig])

    for nm in ("wqkv", "wo", "w1p", "b1p", "w2p", "b2", "qsrep", "ksrep", "lng", "lnb"):
        full = shared.pop(nm)
        for l in range(L0DEPTH):
            shared[f"{nm}{l}"] = np.ascontiguousarray(full[l])

    x = np.asarray(inp["L0_attn_x"], np.float32)
    cores = []
    for c in range(NCORE):
        xc = x[c * BPC:(c + 1) * BPC]
        xT = np.ascontiguousarray(xc.transpose(2, 0, 1).reshape(D, TOK))
        d = dict(shared)
        d["xT"] = xT
        cores.append(d)
    return cores


class Prog:
    def __init__(self, nc, block, stack, hbm, pooled=False):
        self.nc = nc
        self.block = block
        self.es = stack
        self.hbm = hbm
        self._n = 0
        self._chain = {}
        self.pooled = pooled
        self._pool = []
        self._pool_i = 0
        self._barA = None
        self._barB = None
        self._nbound = 0

    def phase_begin(self):
        self._pool_i = 0

    def init_pool(self, n=64):
        self._pool = [self.es.enter_context(self.nc.semaphore(f"pool_{i}"))
                      for i in range(n)]
        for i, en in enumerate(("vector", "scalar", "gpsimd")):
            self._chain[en] = [self.es.enter_context(self.nc.semaphore(f"chain_{en}")), 0]
        nums = [s.num for s in self._pool] + [c[0].num for c in self._chain.values()]
        lo, hi = min(nums), max(nums)
        assert hi - lo + 1 == len(nums), "sem pool not contiguous"
        self._pool_range = range(lo, hi + 1)
        self._barA = self.es.enter_context(self.nc.semaphore("barA"))
        self._barB = self.es.enter_context(self.nc.semaphore("barB"))

    def phase_boundary(self):
        b = self.block
        self._nbound += 1
        k = self._nbound
        barA, barB = self._barA, self._barB
        rng = self._pool_range

        def f_gp(eng):
            eng.drain()
            eng.sem_inc(barA, 1)
            eng.wait_ge(barA, 5 * k)
            eng.dma_reset(rng)
            eng.sem_clear(rng)
            eng.sem_inc(barB, 1)
            eng.wait_ge(barB, k)

        def f_other(eng):
            eng.drain()
            eng.sem_inc(barA, 1)
            eng.wait_ge(barB, k)

        b.tensor(f_other)
        b.scalar(f_other)
        b.vector(f_other)
        b.sync(f_other)
        b.gpsimd(f_gp)
        for s in self._chain.values():
            s[1] = 0
        self._pool_i = 0

    def mark(self, ename, ins):
        if ename not in self._chain:
            self._chain[ename] = [self.sem(f"ch_{ename}"), 0]
        s = self._chain[ename]
        ins.then_inc(s[0], 1)
        s[1] += 1
        return s[1]

    def mark_nop(self, eng, ename):
        if ename not in self._chain:
            self._chain[ename] = [self.sem(f"ch_{ename}"), 0]
        s = self._chain[ename]
        eng.nop().then_inc(s[0], 1)
        s[1] += 1
        return s[1]

    def chain_count(self, ename):
        return self._chain[ename][1] if ename in self._chain else 0

    def chain_sem(self, ename):
        if ename not in self._chain:
            self._chain[ename] = [self.sem(f"ch_{ename}"), 0]
        return self._chain[ename][0]

    def join(self, eng, ename, n=None):
        if ename not in self._chain:
            return
        s = self._chain[ename]
        eng.wait_ge(s[0], s[1] if n is None else n)

    def uid(self):
        self._n += 1
        return self._n

    def sem(self, name="s"):
        if self.pooled:
            if self._pool_i < len(self._pool):
                s = self._pool[self._pool_i]
            else:
                s = self.es.enter_context(self.nc.semaphore(f"pool_{len(self._pool)}"))
                self._pool.append(s)
            self._pool_i += 1
            return s
        return self.es.enter_context(self.nc.semaphore(f"{name}_{self.uid()}"))

    def sbuf(self, shape, dt, name="t", stack=None):
        st = stack if stack is not None else self.es
        return st.enter_context(self.nc.sbuf_tensor(f"{name}_{self.uid()}", list(shape), dt))

    def psum(self, shape, dt, name="p", stack=None):
        st = stack if stack is not None else self.es
        return st.enter_context(self.nc.psum_tensor(f"{name}_{self.uid()}", list(shape), dt))


def emit_setup(P):
    nc, b = P.nc, P.block
    P.const_sb = P.sbuf([1, 16], F32, "kscal")
    P.const_sbb = P.sbuf([1, 16], BF16, "kscalb")
    kready = P.sem("kready")

    @b.sync
    def _(sync):
        sync.dma_start(out=P.const_sb[:], in_=P.hbm["k_scal"]).then_inc(kready, 16)
        sync.dma_start(out=P.const_sbb[:], in_=P.hbm["k_scal_b"]).then_inc(kready, 16)

    for attr in ("tensor", "scalar", "vector", "gpsimd"):
        def f(eng):
            eng.wait_ge(kready, 32)
        getattr(b, attr)(f)
    for i, v in enumerate(CONST_VALS):
        nc.const_aps.register(F32, float(v), P.const_sb[0:1, i:i + 1])
        nc.const_aps.register(BF16, float(v), P.const_sbb[0:1, i:i + 1])


def emit_barrier(P):
    bar = P.sem("bar")
    b = P.block
    for attr in ("sync", "tensor", "scalar", "vector", "gpsimd"):
        def f(eng):
            eng.sem_inc(bar, 1)
            eng.wait_ge(bar, 5)
        getattr(b, attr)(f)


def p0_ln_proj(P):
    nc, b, H = P.nc, P.block, P.hbm
    with contextlib.ExitStack() as S:
        wload = P.sbuf([128, 2 * 1024], F32, "p0wl", S)
        w_in = P.sbuf([128, NCH * 1024], BF16, "p0win", S)
        w_gate = P.sbuf([128, NCH * 1024], BF16, "p0wg", S)
        w_bm = P.sbuf([128, NCH * 1024], BF16, "p0wb", S)
        biases = P.sbuf([1, 3 * D], BF16, "p0bias", S)
        bias_f = P.sbuf([1, 3 * D], F32, "p0biasf", S)
        ones1 = P.sbuf([1, TT], BF16, "p0ones", S)
        ones128 = P.sbuf([128, 1], BF16, "p0o128", S)
        onesk1 = P.sbuf([1, 128], BF16, "p0ok1", S)
        onesk1f = P.sbuf([1, 128], F32, "p0ok1f", S)
        xf = P.sbuf([128, NCH * TT], F32, "p0xf", S)
        xb = P.sbuf([128, NCH * TT], BF16, "p0xb", S)
        sq = P.sbuf([128, NCH * TT], BF16, "p0sq", S)
        t1 = P.sbuf([128, TT], F32, "p0t1", S)
        xnf = P.sbuf([128, NCH * TT], F32, "p0xnf", S)
        xnb = P.sbuf([128, NCH * TT], BF16, "p0xnb", S)
        inp_b = P.sbuf([128, NCH * TT], BF16, "p0inp", S)
        inp_f = P.sbuf([128, NCH * TT], F32, "p0inpf", S)
        gate_f = P.sbuf([128, NCH * TT], F32, "p0gate", S)
        u_bs = P.sbuf([128, NCH * TT], BF16, "p0u", S)
        stat = P.sbuf([1, 6 * TT], F32, "p0stat", S)
        stat_b = P.sbuf([1, 2 * TT], BF16, "p0statb", S)

        p_sum = P.psum([1, TT], F32, "p0psA", S)
        p_sq = P.psum([1, TT], F32, "p0psB", S)
        p_mu = P.psum([128, TT], F32, "p0pmu", S)
        p_rs = P.psum([128, TT], F32, "p0prs", S)
        p_m = [P.psum([128, TT], F32, "p0pm", S) for _ in range(2)]

        ld = P.sem("p0ld")
        kld = P.sem("p0kld")
        wldp = [P.sem("p0wld0"), P.sem("p0wld1")]
        wcast = P.sem("p0wcast")
        xcast = P.sem("p0xcast")
        smm = P.sem("p0smm")
        tail = P.sem("p0tail")
        bmm = P.sem("p0bmm")
        appl = P.sem("p0appl")
        pj = P.sem("p0pj")
        pje = P.sem("p0pje")
        st = P.sem("p0st")

        NST = 40 * 16

        @b.sync
        def _(sync):
            sync.dma_start(out=bias_f[:, 0:D], in_=H["b_in"]).then_inc(kld, 16)
            sync.dma_start(out=bias_f[:, D:2 * D], in_=H["b_gate"]).then_inc(kld, 16)
            sync.dma_start(out=ones1[:], in_=H["k_ones1"]).then_inc(kld, 16)
            sync.dma_start(out=ones128[:], in_=H["k_ones128"]).then_inc(kld, 16)
            sync.dma_start(out=onesk1[:], in_=H["k_onesk1"]).then_inc(kld, 16)
            sync.dma_start(out=onesk1f[:], in_=H["k_onesk1f"]).then_inc(kld, 16)
            for i, wn in enumerate(("w_inT", "w_gateT", "w_bT")):
                for k in range(NCH):
                    j = i * NCH + k
                    if j >= 2:
                        sync.wait_ge(wcast, j - 1)
                    sync.dma_start(out=wload[:, (j % 2) * 1024:(j % 2 + 1) * 1024],
                                   in_=H[wn][k * 128:(k + 1) * 128, :]).then_inc(wldp[j % 2], 16)
            for tt in range(NTT):
                if tt >= 1:
                    if P0_STAGE >= 1:
                        sync.wait_ge(appl, tt * 8)
                    else:
                        sync.wait_ge(xcast, tt * 8)
                for ch in range(NCH):
                    sync.dma_start(out=xf[:, ch * TT:(ch + 1) * TT],
                                   in_=H["xT"][ch * 128:(ch + 1) * 128, tt * TT:(tt + 1) * TT]
                                   ).then_inc(ld, 16)

        @b.gpsimd
        def _(gpsimd):
            gpsimd.wait_ge(kld, 16 * 6)
            gpsimd.tensor_copy(biases[:, 0:2 * D], bias_f[:, 0:2 * D])
            dsts = [w_in, w_gate, w_bm]
            for i in range(3):
                for k in range(NCH):
                    j = i * NCH + k
                    gpsimd.wait_ge(wldp[j % 2], 16 * (j // 2 + 1))
                    gpsimd.tensor_copy(dsts[i][:, k * 1024:(k + 1) * 1024],
                                       wload[:, (j % 2) * 1024:(j % 2 + 1) * 1024]
                                       ).then_inc(wcast, 1)

        @b.vector
        def _(vector):
            for tt in range(NTT):
                P.join(vector, "vector")
                if tt >= 1 and P0_STAGE >= 1:
                    vector.wait_ge(smm, 2 * tt - 1)
                    vector.wait_ge(tail, 5 * (tt - 1) + 1)
                    vector.wait_ge(st, (NST if P0_STAGE >= 2 else 16 * 16) * tt)
                vector.wait_ge(ld, 16 * NCH * (tt + 1))
                for ch in range(NCH):
                    vector.tensor_copy(xb[:, ch * TT:(ch + 1) * TT],
                                       xf[:, ch * TT:(ch + 1) * TT]).then_inc(xcast, 1)
                if P0_STAGE < 1:
                    continue
                vector.wait_ge(tail, 5 * tt + 2)
                vector.tensor_sub(stat[:, 3 * TT:4 * TT], stat[:, 2 * TT:3 * TT],
                                  stat[:, 1 * TT:2 * TT]).then_inc(tail, 1)
                vector.wait_ge(tail, 5 * tt + 4)
                vector.reciprocal(stat[:, 5 * TT:6 * TT],
                                  stat[:, 4 * TT:5 * TT]).then_inc(tail, 1)
                vector.wait_ge(bmm, tt * 2 + 2)
                for ch in range(NCH):
                    P.join(vector, "vector")
                    ins = vector.tensor_sub(t1[:], xf[:, ch * TT:(ch + 1) * TT], p_mu[:, :])
                    P.mark("vector", ins)
                    P.join(vector, "vector")
                    ins = vector.tensor_mul(xnf[:, ch * TT:(ch + 1) * TT], t1[:], p_rs[:, :])
                    P.mark("vector", ins)
                    P.join(vector, "vector")
                    vector.tensor_copy(xnb[:, ch * TT:(ch + 1) * TT],
                                       xnf[:, ch * TT:(ch + 1) * TT]).then_inc(appl, 1)

        @b.scalar
        def _(scalar):
            for tt in range(NTT):
                if P0_STAGE < 1:
                    continue
                P.join(scalar, "scalar")
                for ch in range(NCH):
                    scalar.wait_ge(xcast, tt * NCH + ch + 1)
                    ins = scalar.activation(sq[:, ch * TT:(ch + 1) * TT],
                                            xb[:, ch * TT:(ch + 1) * TT], AF.Square)
                ins.then_inc(tail, 1)
                scalar.wait_ge(smm, tt * 2 + 1)
                ins = scalar.activation(stat[:, 0:TT], p_sum[:, :], AF.Identity, scale=1.0 / D)
                P.mark("scalar", ins)
                P.join(scalar, "scalar")
                scalar.activation(stat[:, 1 * TT:2 * TT], stat[:, 0:TT], AF.Square)
                scalar.wait_ge(smm, tt * 2 + 2)
                scalar.activation(stat[:, 2 * TT:3 * TT], p_sq[:, :], AF.Identity,
                                  scale=1.0 / D).then_inc(tail, 1)
                scalar.wait_ge(tail, 5 * tt + 3)
                scalar.activation(stat[:, 4 * TT:5 * TT], stat[:, 3 * TT:4 * TT],
                                  AF.Sqrt, bias=1e-5).then_inc(tail, 1)
                if P0_STAGE < 2:
                    continue
                if tt >= 1:
                    scalar.wait_ge(st, NST * tt)
                for m in range(NCH):
                    scalar.wait_ge(pj, tt * 24 + m + 1)
                    ins = scalar.activation(inp_f[:, m * TT:(m + 1) * TT],
                                            p_m[m % 2][:, :], AF.Identity)
                    P.mark("scalar", ins)
                    scalar.activation(inp_b[:, m * TT:(m + 1) * TT],
                                      p_m[m % 2][:, :], AF.Identity).then_inc(pje, 1)
                for m in range(NCH):
                    scalar.wait_ge(pj, tt * 24 + NCH + m + 1)
                    scalar.activation(gate_f[:, m * TT:(m + 1) * TT],
                                      p_m[m % 2][:, :], AF.Sigmoid).then_inc(pje, 1)
                for m in range(NCH):
                    scalar.wait_ge(pj, tt * 24 + 2 * NCH + m + 1)
                    scalar.activation(u_bs[:, m * TT:(m + 1) * TT],
                                      p_m[m % 2][:, :], AF.Identity).then_inc(pje, 1)

        @b.tensor
        def _(tensor):
            tensor.wait_ge(wcast, 24)
            tensor.wait_ge(kld, 16 * 6)
            for tt in range(NTT):
                if P0_STAGE < 1:
                    continue
                if tt >= 1:
                    tensor.wait_ge(tail, 5 * (tt - 1) + 2)
                for ch in range(NCH):
                    tensor.wait_ge(xcast, tt * NCH + ch + 1)
                    ins = tensor.matmul(p_sum[:, :], ones128[:],
                                        xb[:, ch * TT:(ch + 1) * TT],
                                        start=(ch == 0), stop=(ch == NCH - 1))
                ins.then_inc(smm, 1)
                tensor.wait_ge(tail, 5 * tt + 1)
                for ch in range(NCH):
                    ins = tensor.matmul(p_sq[:, :], ones128[:],
                                        sq[:, ch * TT:(ch + 1) * TT],
                                        start=(ch == 0), stop=(ch == NCH - 1))
                ins.then_inc(smm, 1)
                tensor.wait_ge(tail, 5 * tt + 5)
                tensor.matmul(p_mu[:, :], onesk1f[:], stat[:, 0:TT],
                              start=True, stop=True).then_inc(bmm, 1)
                tensor.matmul(p_rs[:, :], onesk1f[:], stat[:, 5 * TT:6 * TT],
                              start=True, stop=True).then_inc(bmm, 1)
                tensor.wait_ge(appl, tt * NCH + NCH)
                if P0_STAGE < 2:
                    continue
                for wi, (wt, bias_off, src) in enumerate((
                        (w_in, 0, xnb), (w_gate, D, inp_b), (w_bm, None, inp_b))):
                    for m in range(NCH):
                        pm = p_m[m % 2]
                        gstep = tt * 24 + wi * NCH + m
                        if gstep >= 2:
                            tensor.wait_ge(pje, gstep - 1)
                        if wi > 0 and m == 0:
                            tensor.wait_ge(pje, tt * 24 + wi * NCH)
                        for k in range(NCH):
                            ins = tensor.matmul(
                                pm[:, :],
                                wt[:, k * 1024 + m * 128:k * 1024 + (m + 1) * 128],
                                src[:, k * TT:(k + 1) * TT],
                                start=(k == 0),
                                stop=(bias_off is None and k == NCH - 1))
                        if bias_off is not None:
                            ins = tensor.matmul(
                                pm[:, :],
                                biases[:, bias_off + m * 128:bias_off + (m + 1) * 128],
                                ones1[:], start=False, stop=True)
                        ins.then_inc(pj, 1)

        @b.gpsimd
        def _(sync):
            for tt in range(NTT):
                if P0_STAGE < 1:
                    continue
                sync.wait_ge(appl, tt * NCH + NCH)
                for ch in range(NCH):
                    sync.dma_start(out=H["xn_f"][ch * 128:(ch + 1) * 128, tt * TT:(tt + 1) * TT],
                                   in_=xnf[:, ch * TT:(ch + 1) * TT]).then_inc(st, 16)
                    sync.dma_start(out=H["xn_b"][ch * 128:(ch + 1) * 128, tt * TT:(tt + 1) * TT],
                                   in_=xnb[:, ch * TT:(ch + 1) * TT]).then_inc(st, 16)
                if P0_STAGE < 2:
                    continue
                sync.wait_ge(pje, tt * 24 + 24)
                for ch in range(NCH):
                    sync.dma_start(out=H["u_b"][ch * 128:(ch + 1) * 128, tt * TT:(tt + 1) * TT],
                                   in_=u_bs[:, ch * TT:(ch + 1) * TT]).then_inc(st, 16)
                    sync.dma_start(out=H["g_f"][ch * 128:(ch + 1) * 128, tt * TT:(tt + 1) * TT],
                                   in_=gate_f[:, ch * TT:(ch + 1) * TT]).then_inc(st, 16)
                    sync.dma_start(out=H["inp_f"][ch * 128:(ch + 1) * 128, tt * TT:(tt + 1) * TT],
                                   in_=inp_f[:, ch * TT:(ch + 1) * TT]).then_inc(st, 16)
            if P0_STAGE >= 1:
                sync.wait_ge(st, (NST if P0_STAGE >= 2 else 16 * 16) * NTT)
            else:
                sync.wait_ge(xcast, NTT * NCH)


def emit_proj(P, src_name, w_name, bias_name, out_specs, Kdim, Mdim, act=None):
    nc, b, H = P.nc, P.block, P.hbm
    KC = Kdim // 128
    MC = Mdim // 128
    nout = len(out_specs)
    with contextlib.ExitStack() as S:
        wload = P.sbuf([128, 2 * Mdim], F32, "pjwl", S)
        wsb = P.sbuf([128, KC * Mdim], BF16, "pjw", S)
        bias_f = P.sbuf([1, Mdim], F32, "pjbf", S) if bias_name else None
        bias_b = P.sbuf([1, Mdim], BF16, "pjbb", S) if bias_name else None
        ones1 = P.sbuf([1, TT], BF16, "pjones", S)
        xsb = P.sbuf([128, KC * TT], BF16, "pjx", S)
        osb = [P.sbuf([128, 2 * TT], d, "pjo", S) for (_, d) in out_specs]
        p_m = [P.psum([128, TT], F32, "pjp", S) for _ in range(2)]

        kld = P.sem("pjkld")
        wldp = [P.sem("pjw0"), P.sem("pjw1")]
        wcast = P.sem("pjwc")
        ld = P.sem("pjld")
        pj = P.sem("pjpj")
        pje = P.sem("pjpje")
        stp2 = [P.sem("pjs0"), P.sem("pjs1")]
        nkon = 2 if bias_name else 1

        @b.sync
        def _(sync):
            if bias_name:
                sync.dma_start(out=bias_f[:], in_=H[bias_name]).then_inc(kld, 16)
            sync.dma_start(out=ones1[:], in_=H["k_ones1"]).then_inc(kld, 16)
            for j in range(KC * MC):
                kc, mc = j // MC, j % MC
                if j >= 2:
                    sync.wait_ge(wcast, j - 1)
                sync.dma_start(out=wload[:, (j % 2) * Mdim + mc * 128:
                                         (j % 2) * Mdim + (mc + 1) * 128],
                               in_=H[w_name][kc * 128:(kc + 1) * 128, mc * 128:(mc + 1) * 128]
                               ).then_inc(wldp[j % 2], 16)
            for tt in range(NTT):
                if tt >= 1:
                    sync.wait_ge(pj, tt * MC)
                for kc in range(KC):
                    sync.dma_start(out=xsb[:, kc * TT:(kc + 1) * TT],
                                   in_=H[src_name][kc * 128:(kc + 1) * 128,
                                                   tt * TT:(tt + 1) * TT]).then_inc(ld, 16)

        @b.gpsimd
        def _(gpsimd):
            if bias_name:
                gpsimd.wait_ge(kld, 32)
                gpsimd.tensor_copy(bias_b[:], bias_f[:])
            for j in range(KC * MC):
                kc, mc = j // MC, j % MC
                gpsimd.wait_ge(wldp[j % 2], 16 * (j // 2 + 1))
                gpsimd.tensor_copy(wsb[:, kc * Mdim + mc * 128:kc * Mdim + (mc + 1) * 128],
                                   wload[:, (j % 2) * Mdim + mc * 128:
                                         (j % 2) * Mdim + (mc + 1) * 128]).then_inc(wcast, 1)
            for tt in range(NTT):
                for mc in range(MC):
                    g = tt * MC + mc
                    gpsimd.wait_ge(pje, g + 1)
                    for oi, (oname, _) in enumerate(out_specs):
                        gpsimd.dma_start(
                            out=H[oname][mc * 128:(mc + 1) * 128, tt * TT:(tt + 1) * TT],
                            in_=osb[oi][:, (g % 2) * TT:(g % 2 + 1) * TT]
                        ).then_inc(stp2[g % 2], 16)
            NG = NTT * MC
            gpsimd.wait_ge(stp2[0], 16 * nout * ((NG + 1) // 2))
            gpsimd.wait_ge(stp2[1], 16 * nout * (NG // 2))

        @b.tensor
        def _(tensor):
            tensor.wait_ge(wcast, KC * MC)
            tensor.wait_ge(kld, 16 * nkon)
            for tt in range(NTT):
                tensor.wait_ge(ld, 16 * KC * (tt + 1))
                for mc in range(MC):
                    pm = p_m[mc % 2]
                    g = tt * MC + mc
                    if g >= 2:
                        tensor.wait_ge(pje, g - 1)
                    for kc in range(KC):
                        ins = tensor.matmul(
                            pm[:, :], wsb[:, kc * Mdim + mc * 128:kc * Mdim + (mc + 1) * 128],
                            xsb[:, kc * TT:(kc + 1) * TT],
                            start=(kc == 0), stop=(bias_name is None and kc == KC - 1))
                    if bias_name:
                        ins = tensor.matmul(pm[:, :], bias_b[:, mc * 128:(mc + 1) * 128],
                                            ones1[:], start=False, stop=True)
                    ins.then_inc(pj, 1)

        @b.scalar
        def _(scalar):
            fn = act if act is not None else AF.Identity
            for tt in range(NTT):
                for mc in range(MC):
                    g = tt * MC + mc
                    scalar.wait_ge(pj, g + 1)
                    if g >= 2:
                        scalar.wait_ge(stp2[g % 2], 16 * nout * (g // 2))
                    if nout == 1:
                        scalar.activation(osb[0][:, (g % 2) * TT:(g % 2 + 1) * TT],
                                          p_m[mc % 2][:, :], fn).then_inc(pje, 1)
                    else:
                        ins0 = scalar.activation(osb[0][:, (g % 2) * TT:(g % 2 + 1) * TT],
                                                 p_m[mc % 2][:, :], fn)
                        P.mark("scalar", ins0)
                        scalar.activation(osb[1][:, (g % 2) * TT:(g % 2 + 1) * TT],
                                          p_m[mc % 2][:, :], fn).then_inc(pje, 1)


def l0_attn_consts(l):
    return [f"wqkv{l}", f"wo{l}", f"qsrep{l}", f"ksrep{l}", "t_cosq", "t_sinq",
            "t_cosk", "t_sink", "k_ident", "k_rotT", "k_bd2h", "k_sel2f", "k_maskband"]


def emit_l0_attn(P, l, src_name, dst_name):
    nc, b, H = P.nc, P.block, P.hbm
    NU = HEADS * 2 * 2
    NPAIR = NU // 2
    with contextlib.ExitStack() as S:
        wload = P.sbuf([128, 2 * 128], F32, "alwl", S)
        wq = P.sbuf([128, NCH * 3 * INNER], BF16, "alwq", S)
        wos = P.sbuf([128, NCH * D], BF16, "alwo", S)
        tcq = P.sbuf([128, 2 * TT], F32, "altcq", S)
        tsq = P.sbuf([128, 2 * TT], F32, "altsq", S)
        tck = P.sbuf([128, 2 * TT], F32, "altck", S)
        tsk = P.sbuf([128, 2 * TT], F32, "altsk", S)
        qscale = P.sbuf([128, 2], F32, "alqs", S)
        iden = P.sbuf([128, 128], BF16, "alid", S)
        rotT = P.sbuf([128, 128], BF16, "alrot", S)
        bd2h = P.sbuf([128, 2], BF16, "albd", S)
        sel2f = P.sbuf([2, 128], F32, "alsel", S)
        mband = P.sbuf([128, 384], BF16, "almb", S)
        xg = P.sbuf([128, NCH * 513], F32, "alxg", S)
        xsb = P.sbuf([128, NCH * TT], BF16, "alxs", S)
        qf = P.sbuf([128, 4 * TT], F32, "alqf", S)
        kf = P.sbuf([128, 4 * TT], F32, "alkf", S)
        vb = P.sbuf([128, NCH * TT], BF16, "alvb", S)
        sqs = P.sbuf([128, 2 * TT], BF16, "alsq", S)
        nsb = P.sbuf([2, 2 * TT], F32, "alnsb", S)
        p2f = P.sbuf([128, 2 * TT], F32, "alp2", S)
        m1b = P.sbuf([128, 2 * TT], BF16, "alm1", S)
        m2b = P.sbuf([128, 2 * TT], BF16, "alm2", S)
        qhat = P.sbuf([128, NCH * TT], BF16, "alqh", S)
        kspan = P.sbuf([128, NCH * 768], BF16, "alks", S)
        vt = P.sbuf([128, 6 * HEADS * DH], BF16, "alvt", S)
        mxn = P.sbuf([128, 4], F32, "almx", S)
        mxnn = P.sbuf([128, 1], F32, "almxn", S)
        sums = P.sbuf([128, 2], F32, "alsm", S)
        pbb = P.sbuf([128, 3 * 384], BF16, "alpb", S)
        rinv = P.sbuf([128, 2], F32, "alri", S)
        ptb = P.sbuf([128, 2 * 384], BF16, "alpt", S)
        avb = P.sbuf([128, NCH * TT], BF16, "alav", S)
        gout = P.sbuf([128, 2 * TT], F32, "algo", S)

        BK = [P.psum([128, 512], F32, "albk", S) for _ in range(8)]
        pm = [BK[0], BK[1]]
        nrmv = BK[2][0:2, :]
        rnbv = BK[3]
        scv = [BK[4][:, 0:384], BK[5][:, 0:384], BK[2][:, 0:384]]
        vtv = [BK[4][:, :].bitcast(BF16)[:, 0:64], BK[5][:, :].bitcast(BF16)[:, 0:64]]
        ptv = [BK[6][:, :].bitcast(BF16), BK[3][:, :].bitcast(BF16)]
        avv = BK[7][0:64, :]

        kld = P.sem("akld")
        wldp = [P.sem("aw0"), P.sem("aw1")]
        wcast = P.sem("awc")
        ld = P.sem("ald")
        xs_done = P.sem("axs")
        qmm = P.sem("aqmm")
        qkev = P.sem("aqkev")
        vev = P.sem("avev")
        psq = P.sem("apsq")
        pnm = P.sem("apnm")
        nsq = P.sem("ansq")
        nrn = P.sem("anrn")
        prnb = P.sem("aprnb")
        pp2 = P.sem("app2")
        pm12 = P.sem("apm12")
        prot = P.sem("aprot")
        pqh = P.sem("apqh")
        tcp = P.sem("atcp")
        ptv_s = P.sem("aptv")
        pve = P.sem("apve")
        aqk = P.sem("aaqk")
        amask = P.sem("aamask")
        aexp = P.sem("aaexp")
        asum = P.sem("aasum")
        apt = P.sem("aapt")
        apte = P.sem("aapte")
        aav = P.sem("aaav")
        ave = P.sem("aave")
        wmm = P.sem("awmm")
        woe = P.sem("awoe")
        stp2 = [P.sem("ast0"), P.sem("ast1")]

        EORD = [0, 1, 2, 3, 8, 9, 10, 11, 4, 12, 5, 13, 6, 14, 7, 15]
        QKEV_POS = {m: i + 1 for i, m in enumerate(EORD)}
        GATED = [4, 12, 5, 13, 6, 14, 7, 15]
        QKV_ORD = [0, 1, 2, 3, 8, 9, 10, 11, 16, 17, 18, 19, 20, 21, 22, 23] + GATED
        QKV_POS = {m: i for i, m in enumerate(QKV_ORD)}

        def units_for(tts):
            out = []
            for h in range(HEADS):
                for w in range(2):
                    for ic in range(2):
                        qo = w * 256 + ic * 128
                        start = 256 if (tts == 0 and w == 0) else qo
                        width = qo + 384 - start
                        chunks = [start // 128 + j for j in range(width // 128)]
                        out.append((h, w, ic, qo, start, width, chunks))
            return out

        UN = [units_for(0), units_for(1)]

        def unit(tt, ui):
            return UN[min(tt % NTTS, 1)][ui]

        NW_Q = NCH * 24
        NW_ALL = NW_Q + NCH * NCH

        @b.sync
        def _(sync):
            for nm, dst in (("k_ident", iden), ("k_rotT", rotT),
                            ("k_bd2h", bd2h), ("k_sel2f", sel2f), ("k_maskband", mband)):
                sync.dma_start(out=dst[:], in_=H[nm]).then_inc(kld, 16)
            sync.dma_start(out=qscale[:, 0:1], in_=H[f"qsrep{l}"]).then_inc(kld, 16)
            sync.dma_start(out=qscale[:, 1:2], in_=H[f"ksrep{l}"]).then_inc(kld, 16)

            def emit_acts(tt):
                tts = tt % NTTS
                tpar = tt % 2
                for nm, dst in (("t_cosq", tcq), ("t_sinq", tsq), ("t_cosk", tck),
                                ("t_sink", tsk)):
                    sync.dma_start(out=dst[:, tpar * TT:(tpar + 1) * TT],
                                   in_=H[nm][:, tts * TT:(tts + 1) * TT]).then_inc(ld, 16)
                for ch in range(NCH):
                    if tts == 0:
                        sync.dma_start(out=xg[:, ch * 513 + 1:(ch + 1) * 513],
                                       in_=H[src_name][ch * 128:(ch + 1) * 128,
                                                       tt * TT:(tt + 1) * TT]).then_inc(ld, 16)
                    else:
                        sync.dma_start(out=xg[:, ch * 513:(ch + 1) * 513],
                                       in_=H[src_name][ch * 128:(ch + 1) * 128,
                                                       tt * TT - 1:(tt + 1) * TT]).then_inc(ld, 16)

            emit_acts(0)
            for j in range(NW_ALL):
                if j >= 2:
                    sync.wait_ge(wcast, j - 1)
                if j < NW_Q:
                    pos, kc = j // NCH, j % NCH
                    m = QKV_ORD[pos]
                    srcw = H[f"wqkv{l}"][kc * 128:(kc + 1) * 128, m * 128:(m + 1) * 128]
                else:
                    mc, kc = (j - NW_Q) // NCH, (j - NW_Q) % NCH
                    srcw = H[f"wo{l}"][kc * 128:(kc + 1) * 128, mc * 128:(mc + 1) * 128]
                sync.dma_start(out=wload[:, (j % 2) * 128:(j % 2 + 1) * 128],
                               in_=srcw).then_inc(wldp[j % 2], 16)
            for tt in range(1, NTT):
                sync.wait_ge(woe, NCH * tt)
                if tt >= 2:
                    sync.wait_ge(pm12, 16 * (tt - 1))
                emit_acts(tt)

        @b.gpsimd
        def _(gpsimd):
            def emit_head(tt):
                tts = tt % NTTS
                gpsimd.wait_ge(ld, 16 * 12 * (tt + 1))
                if tt >= 1:
                    gpsimd.wait_ge(qmm, 24 * tt)
                if tts == 0:
                    for ch in range(NCH):
                        ins = gpsimd.memset(xg[:, ch * 513:ch * 513 + 1], 0.0)
                    P.mark("gpsimd", ins)
                    P.join(gpsimd, "gpsimd")
                gpsimd.wait_ge(aav, NU * tt)
                if tts != 0:
                    for ch in range(NCH):
                        gpsimd.tensor_copy(kspan[:, ch * 768:ch * 768 + 256],
                                           kspan[:, ch * 768 + 512:(ch + 1) * 768]
                                           ).then_inc(tcp, 1)
                    gpsimd.tensor_copy(vt[:, 0:2 * HEADS * DH],
                                       vt[:, 4 * HEADS * DH:6 * HEADS * DH]).then_inc(tcp, 1)
                else:
                    for _i in range(9):
                        gpsimd.nop().then_inc(tcp, 1)
                for ch in range(NCH):
                    off = 1 if ch < 4 else 0
                    gpsimd.tensor_copy(xsb[:, ch * TT:(ch + 1) * TT],
                                       xg[:, ch * 513 + off:ch * 513 + off + TT]
                                       ).then_inc(xs_done, 1)

            emit_head(0)
            for j in range(NW_ALL):
                gpsimd.wait_ge(wldp[j % 2], 16 * (j // 2 + 1))
                if j < NW_Q:
                    pos, kc = j // NCH, j % NCH
                    m = QKV_ORD[pos]
                    dst = wq[:, kc * 3 * INNER + m * 128:kc * 3 * INNER + (m + 1) * 128]
                else:
                    mc, kc = (j - NW_Q) // NCH, (j - NW_Q) % NCH
                    dst = wos[:, kc * D + mc * 128:kc * D + (mc + 1) * 128]
                gpsimd.tensor_copy(dst, wload[:, (j % 2) * 128:(j % 2 + 1) * 128]
                                   ).then_inc(wcast, 1)
            for tt in range(NTT):
                tts = tt % NTTS
                if tt >= 1:
                    emit_head(tt)
                for c in range(17):
                    if c < 16:
                        is_q = (c % 2 == 0)
                        cc = c // 2
                        gpsimd.wait_ge(qkev, 16 * tt + QKEV_POS[cc if is_q else 8 + cc])
                        srcf = qf if is_q else kf
                        if 16 * tt + c >= 2:
                            gpsimd.wait_ge(pnm, 16 * tt + c - 1)
                        gpsimd.tensor_mul(sqs[:, (c % 2) * TT:(c % 2 + 1) * TT],
                                          srcf[:, (cc % 4) * TT:(cc % 4 + 1) * TT],
                                          srcf[:, (cc % 4) * TT:(cc % 4 + 1) * TT]
                                          ).then_inc(psq, 1)
                    if c >= 1:
                        cm = c - 1
                        is_q = (cm % 2 == 0)
                        gc = 16 * tt + cm
                        gpsimd.wait_ge(pp2, gc + 1)
                        if gc >= 2:
                            gpsimd.wait_ge(prot, gc - 1)
                        par = cm % 2
                        tpar = tt % 2
                        tc = tcq if is_q else tck
                        ts = tsq if is_q else tsk
                        gpsimd.tensor_mul(m1b[:, par * TT:(par + 1) * TT],
                                          p2f[:, par * TT:(par + 1) * TT],
                                          tc[:, tpar * TT:(tpar + 1) * TT])
                        gpsimd.tensor_mul(m2b[:, par * TT:(par + 1) * TT],
                                          p2f[:, par * TT:(par + 1) * TT],
                                          ts[:, tpar * TT:(tpar + 1) * TT]).then_inc(pm12, 1)
                for mc in range(NCH):
                    g = tt * NCH + mc
                    gpsimd.wait_ge(woe, g + 1)
                    gpsimd.dma_start(
                        out=H[dst_name][mc * 128:(mc + 1) * 128, tt * TT:(tt + 1) * TT],
                        in_=gout[:, (g % 2) * TT:(g % 2 + 1) * TT]).then_inc(stp2[g % 2], 16)
            NG = NTT * NCH
            gpsimd.wait_ge(stp2[0], 16 * ((NG + 1) // 2))
            gpsimd.wait_ge(stp2[1], 16 * (NG // 2))

        def emit_ptrans(tensor, gv, uv):
            (hv, wv, icv, qov, startv, widthv, chunksv) = uv
            tensor.wait_ge(asum, gv + 1)
            if gv >= 2:
                tensor.wait_ge(apte, gv - 1)
            for j in range(widthv // 128):
                ins = tensor.transpose(
                    ptv[gv % 2][:, j * 128:(j + 1) * 128],
                    pbb[:, (gv % 3) * 384 + j * 128:(gv % 3) * 384 + (j + 1) * 128],
                    iden[:])
            ins.then_inc(apt, 1)

        def emit_av(tensor, gv, uv):
            (hv, wv, icv, qov, startv, widthv, chunksv) = uv
            tensor.wait_ge(apte, gv + 1)
            if gv % 2 == 0 and gv >= 2:
                tensor.wait_ge(ave, gv // 2)
            nchk = widthv // 128
            for j, mcg in enumerate(chunksv):
                ins = tensor.matmul(
                    avv[:, icv * 128:(icv + 1) * 128],
                    vt[:, (mcg * HEADS + hv) * DH:(mcg * HEADS + hv + 1) * DH],
                    ptb[:, (gv % 2) * 384 + j * 128:(gv % 2) * 384 + (j + 1) * 128],
                    start=(j == 0), stop=(j == nchk - 1))
            ins.then_inc(aav, 1)

        @b.tensor
        def _(tensor):
            tensor.wait_ge(kld, 16 * 7)
            for tt in range(NTT):
                tts = tt % NTTS
                if tt >= 1:
                    tensor.wait_ge(woe, NCH * tt)
                tensor.wait_ge(xs_done, 8 * (tt + 1))
                def emit_qkv_mm(pos):
                    m = QKV_ORD[pos]
                    g = tt * 24 + pos
                    if tt == 0:
                        tensor.wait_ge(wcast, NCH * (pos + 1))
                    if g >= 2:
                        gg = g - 2
                        ttg, posg = gg // 24, gg % 24
                        mg = QKV_ORD[posg]
                        if mg < 16:
                            tensor.wait_ge(qkev, 16 * ttg + QKEV_POS[mg])
                        else:
                            tensor.wait_ge(vev, 8 * ttg + (mg - 16) + 1)
                    for kc in range(NCH):
                        ins = tensor.matmul(
                            pm[pos % 2][:, :],
                            wq[:, kc * 3 * INNER + m * 128:kc * 3 * INNER + (m + 1) * 128],
                            xsb[:, kc * TT:(kc + 1) * TT],
                            start=(kc == 0), stop=(kc == NCH - 1))
                    ins.then_inc(qmm, 1)

                def emit_prep_mms(c):
                    gc = 16 * tt + c
                    tensor.wait_ge(psq, gc + 1)
                    if gc >= 1:
                        tensor.wait_ge(nsq, gc)
                    tensor.matmul(nrmv, bd2h[:], sqs[:, (c % 2) * TT:(c % 2 + 1) * TT],
                                  start=True, stop=True).then_inc(pnm, 1)
                    tensor.wait_ge(nrn, gc + 1)
                    if gc >= 1:
                        tensor.wait_ge(pqh, gc)
                    tensor.matmul(rnbv[:, :], sel2f[:],
                                  nsb[0:2, (c % 2) * TT:(c % 2 + 1) * TT],
                                  start=True, stop=True).then_inc(prnb, 1)
                    tensor.wait_ge(pm12, gc + 1)
                    tensor.matmul(rnbv[:, :], iden[:],
                                  m1b[:, (c % 2) * TT:(c % 2 + 1) * TT],
                                  start=True, stop=False)
                    tensor.matmul(rnbv[:, :], rotT[:],
                                  m2b[:, (c % 2) * TT:(c % 2 + 1) * TT],
                                  start=False, stop=True).then_inc(prot, 1)

                for pos in range(16):
                    emit_qkv_mm(pos)
                for c in range(8):
                    emit_prep_mms(c)
                    emit_qkv_mm(16 + c)
                for c in range(8, 16):
                    emit_prep_mms(c)
                tensor.wait_ge(vev, 8 * (tt + 1))
                if tt >= 1:
                    tensor.wait_ge(aexp, NU * tt)
                for h in range(HEADS):
                    for mcn in range(4):
                        gvt = 64 * tt + h * 4 + mcn
                        if gvt >= 2:
                            tensor.wait_ge(pve, gvt - 1)
                        hb_ = (h % 2) * 64
                        ins = tensor.transpose(
                            vtv[gvt % 2],
                            vb[hb_:hb_ + 64,
                               (h // 2) * TT + mcn * 128:(h // 2) * TT + (mcn + 1) * 128],
                            iden[hb_:hb_ + 64, hb_:hb_ + 64])
                        ins.then_inc(ptv_s, 1)
                tensor.wait_ge(pqh, 16 * (tt + 1))
                tensor.wait_ge(pve, 64 * (tt + 1))
                for ui in range(NU):
                    gu = NU * tt + ui
                    (h, w, ic, qo, start, width, chunks) = unit(tt, ui)
                    par = gu % 3
                    moff = 384 - width
                    if gu >= 3:
                        tensor.wait_ge(aexp, gu - 2)
                    tensor.matmul(scv[par][:, 0:width], iden[:],
                                  mband[:, moff:384], start=True, stop=False)
                    ins = tensor.matmul(
                        scv[par][:, 0:width],
                        qhat[(h % 2) * 64:(h % 2) * 64 + 64,
                             (h // 2) * TT + w * 256 + ic * 128:
                             (h // 2) * TT + w * 256 + (ic + 1) * 128],
                        kspan[(h % 2) * 64:(h % 2) * 64 + 64,
                              (h // 2) * 768 + start:(h // 2) * 768 + qo + 384],
                        start=False, stop=True)
                    ins.then_inc(aqk, 1)
                    if ui >= 1:
                        emit_ptrans(tensor, gu - 1, unit(tt, ui - 1))
                    if ui >= 2:
                        emit_av(tensor, gu - 2, unit(tt, ui - 2))
                emit_ptrans(tensor, NU * (tt + 1) - 1, unit(tt, NU - 1))
                emit_av(tensor, NU * (tt + 1) - 2, unit(tt, NU - 2))
                emit_av(tensor, NU * (tt + 1) - 1, unit(tt, NU - 1))
                tensor.wait_ge(ave, NPAIR * (tt + 1))
                for mc in range(NCH):
                    g = tt * NCH + mc
                    if tt == 0:
                        tensor.wait_ge(wcast, NW_Q + NCH * (mc + 1))
                    if g >= 2:
                        tensor.wait_ge(woe, g - 1)
                    for kc in range(NCH):
                        ins = tensor.matmul(
                            pm[mc % 2][:, :], wos[:, kc * D + mc * 128:kc * D + (mc + 1) * 128],
                            avb[:, kc * TT:(kc + 1) * TT],
                            start=(kc == 0), stop=(kc == NCH - 1))
                    ins.then_inc(wmm, 1)

        @b.scalar
        def _(scalar):
            for tt in range(NTT):
                def emit_evac(m):
                    scalar.wait_ge(qmm, 24 * tt + QKV_POS[m] + 1)
                    is_q = (m < 8)
                    cc = m % 8
                    if cc >= 4:
                        pidx = 16 * tt + 2 * (cc - 4) + (0 if is_q else 1)
                    elif tt >= 1:
                        pidx = 16 * (tt - 1) + 2 * (cc + 4) + (0 if is_q else 1)
                    else:
                        pidx = None
                    if pidx is not None:
                        scalar.wait_ge(pp2, pidx + 1)
                        scalar.wait_ge(psq, pidx + 1)
                    dstf = qf if is_q else kf
                    scalar.activation(dstf[:, (cc % 4) * TT:(cc % 4 + 1) * TT],
                                      pm[QKV_POS[m] % 2][:, :], AF.Identity).then_inc(qkev, 1)

                for m in EORD[:8]:
                    emit_evac(m)
                for mm in range(8):
                    m = 16 + mm
                    scalar.wait_ge(qmm, 24 * tt + QKV_POS[m] + 1)
                    if tt >= 1 and mm == 0:
                        scalar.wait_ge(ptv_s, 64 * tt)
                    scalar.activation(vb[:, mm * TT:(mm + 1) * TT],
                                      pm[QKV_POS[m] % 2][:, :],
                                      AF.Identity).then_inc(vev, 1)
                for c in range(16):
                    is_q = (c % 2 == 0)
                    cc = c // 2
                    gc = 16 * tt + c
                    scalar.wait_ge(pnm, gc + 1)
                    if gc >= 2:
                        scalar.wait_ge(prnb, gc - 1)
                    scalar.activation(nsb[0:2, (c % 2) * TT:(c % 2 + 1) * TT],
                                      nrmv, AF.Sqrt).then_inc(nsq, 1)
                    scalar.wait_ge(prot, gc + 1)
                    if tt >= 1 and c == 0:
                        scalar.wait_ge(aav, NU * tt)
                        scalar.wait_ge(tcp, 9 * (tt + 1))
                    if is_q:
                        dst = qhat[:, cc * TT:(cc + 1) * TT]
                    else:
                        dst = kspan[:, cc * 768 + 256:cc * 768 + 768]
                    scalar.activation(dst, rnbv[:, :], AF.Identity).then_inc(pqh, 1)
                    if c < 8:
                        emit_evac(EORD[8 + c])
                for ui in range(NU):
                    gu = NU * tt + ui
                    (h, w, ic, qo, start, width, chunks) = unit(tt, ui)
                    scalar.wait_ge(amask, gu + 1)
                    if gu >= 2:
                        scalar.wait_ge(asum, gu - 1)
                    if gu >= 3:
                        scalar.wait_ge(apt, gu - 2)
                    scalar.activation(mxnn[:, 0:1], mxn[:, (gu % 3):(gu % 3) + 1],
                                      AF.Identity, scale=-1.0)
                    scalar.activation(pbb[:, (gu % 3) * 384:(gu % 3) * 384 + width],
                                      scv[gu % 3][:, 0:width],
                                      AF.Exp, bias=mxnn[:, 0:1],
                                      accum_out=sums[:, (gu % 2):(gu % 2) + 1]
                                      ).then_inc(aexp, 1)

        @b.vector
        def _(vector):
            for tt in range(NTT):
                tts = tt % NTTS
                for c in range(16):
                    is_q = (c % 2 == 0)
                    cc = c // 2
                    gc = 16 * tt + c
                    par = c % 2
                    vector.wait_ge(nsq, gc + 1)
                    nv = nsb[0:2, par * TT:(par + 1) * TT]
                    P.join(vector, "vector")
                    ins = vector.tensor_scalar_max(nv, nv, 1e-12)
                    P.mark("vector", ins)
                    P.join(vector, "vector")
                    ins = vector.reciprocal(nv, nv)
                    ins.then_inc(nrn, 1)
                    vector.wait_ge(prnb, gc + 1)
                    srcf = qf if is_q else kf
                    scl = qscale[:, 0:1] if is_q else qscale[:, 1:2]
                    if gc >= 2:
                        vector.wait_ge(pm12, gc - 1)
                    P.join(vector, "vector")
                    ins = vector.scalar_tensor_tensor(
                        p2f[:, par * TT:(par + 1) * TT],
                        srcf[:, (cc % 4) * TT:(cc % 4 + 1) * TT],
                        scl, rnbv[:, :], ALU.mult, ALU.mult)
                    ins.then_inc(pp2, 1)
                vector.wait_ge(tcp, 9 * (tt + 1))
                for h in range(HEADS):
                    for mcn in range(4):
                        gvt = 64 * tt + h * 4 + mcn
                        vector.wait_ge(ptv_s, gvt + 1)
                        mcg = mcn + 2
                        vector.tensor_copy(
                            vt[:, (mcg * HEADS + h) * DH:(mcg * HEADS + h + 1) * DH],
                            vtv[gvt % 2]).then_inc(pve, 1)
                for pi in range(NU + 3):
                    if pi < NU:
                        gu = NU * tt + pi
                        (h, w, ic, qo, start, width, chunks) = unit(tt, pi)
                        par = gu % 3
                        vector.wait_ge(aqk, gu + 1)
                        if gu >= 3:
                            vector.wait_ge(aexp, gu - 2)
                        P.join(vector, "vector")
                        ins = vector.tensor_reduce(mxn[:, par:par + 1],
                                                   scv[par][:, 0:width],
                                                   AX.X, ALU.max)
                        ins.then_inc(amask, 1)
                    if 1 <= pi <= NU:
                        gv = NU * tt + pi - 1
                        uv = unit(tt, pi - 1)
                        (hv, wv, icv, qov, startv, widthv, chunksv) = uv
                        parv = gv % 3
                        par2 = gv % 2
                        vector.wait_ge(aexp, gv + 1)
                        P.join(vector, "vector")
                        ins = vector.reciprocal(rinv[:, par2:par2 + 1],
                                                sums[:, par2:par2 + 1])
                        P.mark("vector", ins)
                        P.join(vector, "vector")
                        ins = vector.tensor_scalar_mul(pbb[:, parv * 384:parv * 384 + widthv],
                                                       pbb[:, parv * 384:parv * 384 + widthv],
                                                       rinv[:, par2:par2 + 1])
                        ins.then_inc(asum, 1)
                    if 2 <= pi <= NU + 1:
                        gv = NU * tt + pi - 2
                        uv = unit(tt, pi - 2)
                        (hv, wv, icv, qov, startv, widthv, chunksv) = uv
                        vector.wait_ge(apt, gv + 1)
                        if gv >= 2:
                            vector.wait_ge(aav, gv - 1)
                        vector.tensor_copy(ptb[:, (gv % 2) * 384:(gv % 2) * 384 + widthv],
                                           ptv[gv % 2][:, 0:widthv]).then_inc(apte, 1)
                    if pi >= 4 and pi % 2 == 0 and (pi - 4) // 2 < NPAIR:
                        j = (pi - 4) // 2
                        gpair = NPAIR * tt + j
                        uv = unit(tt, 2 * j + 1)
                        (hv, wv, icv, qov, startv, widthv, chunksv) = uv
                        vector.wait_ge(aav, NU * tt + 2 * j + 2)
                        if tt >= 1 and j == 0:
                            vector.wait_ge(wmm, NCH * tt)
                        dst = avb[(hv % 2) * 64:(hv % 2) * 64 + 64,
                                  (hv // 2) * TT + wv * 256:(hv // 2) * TT + (wv + 1) * 256]
                        vector.tensor_copy(dst, avv[:, 0:256]).then_inc(ave, 1)
                for mc in range(NCH):
                    g = tt * NCH + mc
                    vector.wait_ge(wmm, g + 1)
                    if g >= 2:
                        vector.wait_ge(stp2[g % 2], 16 * (g // 2))
                    P.join(vector, "vector")
                    ins = vector.tensor_add(gout[:, (g % 2) * TT:(g % 2 + 1) * TT],
                                            pm[mc % 2][:, :],
                                            xg[:, mc * 513 + 1:(mc + 1) * 513])
                    ins.then_inc(woe, 1)


def l0_ffn_consts(l):
    return [f"w1p{l}", f"b1p{l}", f"w2p{l}", f"b2{l}", f"lng{l}", f"lnb{l}",
            "k_ones128", "k_onesk1f"]
NM1 = NH1 // 128
NK2 = DPAD // 128


def emit_l0_ffn(P, l, src_name, dst_name):
    nc, b, H = P.nc, P.block, P.hbm
    with contextlib.ExitStack() as S:
        wload = P.sbuf([128, 2 * 128], F32, "flwl", S)
        w1 = P.sbuf([128, NCH * NH1], BF16, "flw1", S)
        w2 = P.sbuf([128, NK2 * D], BF16, "flw2", S)
        b1c_f = P.sbuf([128, NM1], F32, "flb1", S)
        b2c_f = P.sbuf([128, NCH], F32, "flb2", S)
        lngc = P.sbuf([128, NCH], F32, "fllg", S)
        lnbc = P.sbuf([128, NCH], F32, "fllb", S)
        ones128 = P.sbuf([128, 1], BF16, "flo1", S)
        onesk1f = P.sbuf([1, 128], F32, "flok", S)
        xg = P.sbuf([128, NCH * 513], F32, "flxg", S)
        xsb = P.sbuf([128, NCH * TT], BF16, "flxs", S)
        xnb = P.sbuf([128, NCH * TT], BF16, "flxn", S)
        hb = P.sbuf([128, NK2 * TT], BF16, "flh", S)
        sq = hb[:, 0:NCH * TT]
        ab = P.sbuf([128, 2 * TT], BF16, "flab", S)
        gb = P.sbuf([128, 2 * TT], BF16, "flgb", S)
        t1 = P.sbuf([128, 2 * TT], F32, "flt1", S)
        gout = P.sbuf([128, 2 * TT], F32, "flgo", S)
        stat = P.sbuf([1, 3 * TT], F32, "flst", S)

        pm = [P.psum([128, TT], F32, "flpm", S) for _ in range(2)]
        p_sum = P.psum([1, TT], F32, "flps", S)
        p_sq = P.psum([1, TT], F32, "flpq", S)
        b_mu = P.psum([128, TT], F32, "flbm", S)
        b_rs = P.psum([128, TT], F32, "flbr", S)

        kld = P.sem("fkld")
        wldp = [P.sem("fw0"), P.sem("fw1")]
        wcast = P.sem("fwc")
        ld = P.sem("fld")
        xs_done = P.sem("fxs")
        psq_s = P.sem("fpsq")
        smm = P.sem("fsmm")
        tail = P.sem("ftail")
        bmm = P.sem("fbmm")
        appl = P.sem("fappl")
        w1mm = P.sem("fw1m")
        w1ev = P.sem("fw1e")
        hple = P.sem("fhp")
        w2mm = P.sem("fw2m")
        w2ev = P.sem("fw2e")
        stp2 = [P.sem("fst0"), P.sem("fst1")]

        NW1 = NCH * NM1
        NW2 = NK2 * NCH
        NW_ALL = NW1 + NW2

        @b.sync
        def _(sync):
            sync.dma_start(out=b1c_f[:], in_=H[f"b1p{l}"]).then_inc(kld, 16)
            sync.dma_start(out=b2c_f[:], in_=H[f"b2{l}"]).then_inc(kld, 16)
            sync.dma_start(out=lngc[:], in_=H[f"lng{l}"]).then_inc(kld, 16)
            sync.dma_start(out=lnbc[:], in_=H[f"lnb{l}"]).then_inc(kld, 16)
            sync.dma_start(out=ones128[:], in_=H["k_ones128"]).then_inc(kld, 16)
            sync.dma_start(out=onesk1f[:], in_=H["k_onesk1f"]).then_inc(kld, 16)
            def emit_acts(tt):
                tts = tt % NTTS
                for ch in range(NCH):
                    if tts == 0:
                        sync.dma_start(out=xg[:, ch * 513 + 1:(ch + 1) * 513],
                                       in_=H[src_name][ch * 128:(ch + 1) * 128,
                                                       tt * TT:(tt + 1) * TT]).then_inc(ld, 16)
                    else:
                        sync.dma_start(out=xg[:, ch * 513:(ch + 1) * 513],
                                       in_=H[src_name][ch * 128:(ch + 1) * 128,
                                                       tt * TT - 1:(tt + 1) * TT]).then_inc(ld, 16)

            emit_acts(0)
            for j in range(NW_ALL):
                if j >= 2:
                    sync.wait_ge(wcast, j - 1)
                if j < NW1:
                    m, kc = j // NCH, j % NCH
                    srcw = H[f"w1p{l}"][kc * 128:(kc + 1) * 128, m * 128:(m + 1) * 128]
                else:
                    mc, kc = (j - NW1) // NK2, (j - NW1) % NK2
                    srcw = H[f"w2p{l}"][kc * 128:(kc + 1) * 128, mc * 128:(mc + 1) * 128]
                sync.dma_start(out=wload[:, (j % 2) * 128:(j % 2 + 1) * 128],
                               in_=srcw).then_inc(wldp[j % 2], 16)
            for tt in range(1, NTT):
                sync.wait_ge(w2ev, NCH * tt)
                emit_acts(tt)

        @b.gpsimd
        def _(gpsimd):
            def emit_head(tt):
                tts = tt % NTTS
                gpsimd.wait_ge(ld, 16 * NCH * (tt + 1))
                if tt >= 1:
                    gpsimd.wait_ge(w1mm, NM1 * tt)
                    gpsimd.wait_ge(w2mm, NCH * tt)
                if tts == 0:
                    for ch in range(NCH):
                        ins = gpsimd.memset(xg[:, ch * 513:ch * 513 + 1], 0.0)
                    P.mark("gpsimd", ins)
                    P.join(gpsimd, "gpsimd")
                for ch in range(NCH):
                    off = 1 if ch < 4 else 0
                    gpsimd.tensor_copy(xsb[:, ch * TT:(ch + 1) * TT],
                                       xg[:, ch * 513 + off:ch * 513 + off + TT]
                                       ).then_inc(xs_done, 1)
                gpsimd.wait_ge(smm, 2 * tt + 1)
                for ch in range(NCH):
                    gpsimd.tensor_mul(sq[:, ch * TT:(ch + 1) * TT],
                                      xsb[:, ch * TT:(ch + 1) * TT],
                                      xsb[:, ch * TT:(ch + 1) * TT]).then_inc(psq_s, 1)

            emit_head(0)
            for j in range(NW_ALL):
                gpsimd.wait_ge(wldp[j % 2], 16 * (j // 2 + 1))
                if j < NW1:
                    m, kc = j // NCH, j % NCH
                    dst = w1[:, kc * NH1 + m * 128:kc * NH1 + (m + 1) * 128]
                else:
                    mc, kc = (j - NW1) // NK2, (j - NW1) % NK2
                    dst = w2[:, kc * D + mc * 128:kc * D + (mc + 1) * 128]
                gpsimd.tensor_copy(dst, wload[:, (j % 2) * 128:(j % 2 + 1) * 128]
                                   ).then_inc(wcast, 1)
            for tt in range(NTT):
                tts = tt % NTTS
                if tt >= 1:
                    emit_head(tt)
                for j in range(NK2):
                    gpsimd.wait_ge(w1ev, 44 * tt + 2 * j + 2)
                    gpsimd.tensor_mul(hb[:, j * TT:(j + 1) * TT],
                                      ab[:, (j % 2) * TT:(j % 2 + 1) * TT],
                                      gb[:, (j % 2) * TT:(j % 2 + 1) * TT]).then_inc(hple, 1)
                for mc in range(NCH):
                    g = tt * NCH + mc
                    gpsimd.wait_ge(w2ev, g + 1)
                    gpsimd.dma_start(
                        out=H[dst_name][mc * 128:(mc + 1) * 128, tt * TT:(tt + 1) * TT],
                        in_=gout[:, (g % 2) * TT:(g % 2 + 1) * TT]).then_inc(stp2[g % 2], 16)
            NG = NTT * NCH
            gpsimd.wait_ge(stp2[0], 16 * ((NG + 1) // 2))
            gpsimd.wait_ge(stp2[1], 16 * (NG // 2))

        @b.tensor
        def _(tensor):
            tensor.wait_ge(kld, 16 * 6)
            for tt in range(NTT):
                if tt >= 1:
                    tensor.wait_ge(tail, 5 * (tt - 1) + 2)
                tensor.wait_ge(xs_done, 8 * (tt + 1))
                for ch in range(NCH):
                    ins = tensor.matmul(p_sum[:, :], ones128[:],
                                        xsb[:, ch * TT:(ch + 1) * TT],
                                        start=(ch == 0), stop=(ch == NCH - 1))
                ins.then_inc(smm, 1)
                tensor.wait_ge(psq_s, 8 * (tt + 1))
                for ch in range(NCH):
                    ins = tensor.matmul(p_sq[:, :], ones128[:],
                                        sq[:, ch * TT:(ch + 1) * TT],
                                        start=(ch == 0), stop=(ch == NCH - 1))
                ins.then_inc(smm, 1)
                tensor.wait_ge(tail, 5 * tt + 5)
                tensor.wait_ge(appl, NCH * tt)
                tensor.matmul(b_mu[:, :], onesk1f[:], stat[:, 0:TT],
                              start=True, stop=True).then_inc(bmm, 1)
                tensor.matmul(b_rs[:, :], onesk1f[:], stat[:, TT:2 * TT],
                              start=True, stop=True).then_inc(bmm, 1)
                tensor.wait_ge(appl, NCH * (tt + 1))
                if tt >= 1:
                    tensor.wait_ge(w2ev, NCH * tt)
                for m in range(NM1):
                    g = tt * NM1 + m
                    if tt == 0:
                        tensor.wait_ge(wcast, NCH * (m + 1))
                    if g >= 2:
                        tensor.wait_ge(w1ev, g - 1)
                    for kc in range(NCH):
                        ins = tensor.matmul(
                            pm[m % 2][:, :], w1[:, kc * NH1 + m * 128:kc * NH1 + (m + 1) * 128],
                            xnb[:, kc * TT:(kc + 1) * TT],
                            start=(kc == 0), stop=(kc == NCH - 1))
                    ins.then_inc(w1mm, 1)
                tensor.wait_ge(w1ev, NM1 * (tt + 1))
                tensor.wait_ge(hple, NK2 * (tt + 1))
                for mc in range(NCH):
                    g = tt * NCH + mc
                    if tt == 0:
                        tensor.wait_ge(wcast, NW1 + NK2 * (mc + 1))
                    if g >= 2:
                        tensor.wait_ge(w2ev, g - 1)
                    for kc in range(NK2):
                        ins = tensor.matmul(
                            pm[mc % 2][:, :], w2[:, kc * D + mc * 128:kc * D + (mc + 1) * 128],
                            hb[:, kc * TT:(kc + 1) * TT],
                            start=(kc == 0), stop=(kc == NK2 - 1))
                    ins.then_inc(w2mm, 1)

        @b.scalar
        def _(scalar):
            for tt in range(NTT):
                scalar.wait_ge(smm, tt * 2 + 1)
                ins = scalar.activation(stat[:, 0:TT], p_sum[:, :], AF.Identity, scale=1.0 / D)
                P.mark("scalar", ins)
                P.join(scalar, "scalar")
                scalar.activation(stat[:, 1 * TT:2 * TT], stat[:, 0:TT], AF.Square)
                scalar.wait_ge(smm, tt * 2 + 2)
                scalar.activation(stat[:, 2 * TT:3 * TT], p_sq[:, :], AF.Identity,
                                  scale=1.0 / D).then_inc(tail, 1)
                scalar.wait_ge(tail, 5 * tt + 2)
                scalar.activation(stat[:, 2 * TT:3 * TT], stat[:, 1 * TT:2 * TT],
                                  AF.Sqrt, bias=1e-5).then_inc(tail, 1)
                for m in range(NM1):
                    g = tt * NM1 + m
                    scalar.wait_ge(w1mm, g + 1)
                    j = m // 2
                    if j >= 2 and m % 2 == 0:
                        scalar.wait_ge(hple, NK2 * tt + j - 1)
                    bc = b1c_f[:, m:m + 1]
                    if m % 2 == 0:
                        scalar.activation(ab[:, (j % 2) * TT:(j % 2 + 1) * TT],
                                          pm[m % 2][:, :], AF.Identity, bias=bc
                                          ).then_inc(w1ev, 1)
                    else:
                        scalar.activation(gb[:, (j % 2) * TT:(j % 2 + 1) * TT],
                                          pm[m % 2][:, :], AF.Gelu, bias=bc
                                          ).then_inc(w1ev, 1)

        @b.vector
        def _(vector):
            for tt in range(NTT):
                vector.wait_ge(tail, 5 * tt + 1)
                P.join(vector, "vector")
                ins = vector.tensor_sub(stat[:, 1 * TT:2 * TT], stat[:, 2 * TT:3 * TT],
                                        stat[:, 1 * TT:2 * TT])
                ins.then_inc(tail, 1)
                vector.wait_ge(tail, 5 * tt + 3)
                P.join(vector, "vector")
                ins = vector.reciprocal(stat[:, 1 * TT:2 * TT], stat[:, 2 * TT:3 * TT])
                ins.then_inc(tail, 2)
                vector.wait_ge(bmm, 2 * (tt + 1))
                if tt >= 1:
                    vector.wait_ge(w1mm, NM1 * tt)
                for ch in range(NCH):
                    off = 1 if ch < 4 else 0
                    par = ch % 2
                    P.join(vector, "vector")
                    ins = vector.tensor_sub(t1[:, par * TT:(par + 1) * TT],
                                            xg[:, ch * 513 + off:ch * 513 + off + TT],
                                            b_mu[:, :])
                    P.mark("vector", ins)
                    P.join(vector, "vector")
                    ins = vector.tensor_mul(t1[:, par * TT:(par + 1) * TT],
                                            t1[:, par * TT:(par + 1) * TT], b_rs[:, :])
                    P.mark("vector", ins)
                    P.join(vector, "vector")
                    ins = vector.tensor_scalar(xnb[:, ch * TT:(ch + 1) * TT],
                                               t1[:, par * TT:(par + 1) * TT],
                                               lngc[:, ch:ch + 1], lnbc[:, ch:ch + 1],
                                               ALU.mult, ALU.add)
                    ins.then_inc(appl, 1)
                for mc in range(NCH):
                    g = tt * NCH + mc
                    vector.wait_ge(w2mm, g + 1)
                    if g >= 2:
                        vector.wait_ge(stp2[g % 2], 16 * (g // 2))
                    P.join(vector, "vector")
                    ins = vector.scalar_tensor_tensor(
                        gout[:, (g % 2) * TT:(g % 2 + 1) * TT], pm[mc % 2][:, :],
                        b2c_f[:, mc:mc + 1], xg[:, mc * 513 + 1:(mc + 1) * 513],
                        ALU.add, ALU.add)
                    ins.then_inc(w2ev, 1)


def p2_wout(P):
    emit_proj(P, "states_b", "w_outT", "b_out", [("s0_f", F32)], D, D)


RNN_BLK = 128
NGRP = NCH * BPC
RNN_BW = NGRP * RNN_BLK


def p1_rnn(P):
    nc, b, H = P.nc, P.block, P.hbm
    NB = SEQ // RNN_BLK
    W = NGRP

    with contextlib.ExitStack() as S:
        aload = P.sbuf([128, 2 * 1024], F32, "p1al", S)
        atb = P.sbuf([128, 8 * 1024], BF16, "p1at", S)
        iden = P.sbuf([128, 128], BF16, "p1id", S)
        ubuf = P.sbuf([128, 2 * RNN_BW], BF16, "p1u", S)
        gbuf = P.sbuf([128, 2 * RNN_BW], F32, "p1g", S)
        ibuf = P.sbuf([128, 2 * RNN_BW], F32, "p1c", S)
        sblk = P.sbuf([128, 2 * RNN_BW], BF16, "p1s", S)
        sinit = P.sbuf([128, W], BF16, "p1s0", S)
        stil = P.sbuf([128, 2 * W], F32, "p1stil", S)
        t1b = P.sbuf([128, 2 * W], F32, "p1t1", S)
        ps = P.psum([128, 2 * W], F32, "p1ps", S)

        kld = P.sem("p1kld")
        aldp = [P.sem("p1a0"), P.sem("p1a1")]
        acast = P.sem("p1ac")
        ldp = [P.sem("p1l0"), P.sem("p1l1")]
        init_done = P.sem("p1init")
        mm_done = P.sem("p1mm")
        act_done = P.sem("p1tanh")
        dve_done = P.sem("p1dve")
        stp = [P.sem("p1st0"), P.sem("p1st1")]

        cv0 = P.chain_count("vector")
        chv = P.chain_sem("vector")

        def tile3(buf, t):
            B = t // RNN_BLK
            toff = t - B * RNN_BLK
            par = B % 2
            v = buf[:, par * RNN_BW:(par + 1) * RNN_BW]
            return v.rearrange("p (g t) -> p g t", t=RNN_BLK)[:, :, toff:toff + 1]

        @b.sync
        def _(sync):
            sync.dma_start(out=iden[:], in_=H["k_ident"]).then_inc(kld, 16)
            for j in range(NCH):
                if j >= 2:
                    sync.wait_ge(acast, j - 1)
                sync.dma_start(out=aload[:, (j % 2) * 1024:(j % 2 + 1) * 1024],
                               in_=H["aT"][j * 128:(j + 1) * 128, :]).then_inc(aldp[j % 2], 16)
            for B in range(NB):
                par = B % 2
                if B >= 2:
                    sync.wait_ge(dve_done, (B - 1) * RNN_BLK)
                t0 = B * RNN_BLK
                for ch in range(NCH):
                    for bb in range(BPC):
                        gcol = par * RNN_BW + (ch * BPC + bb) * RNN_BLK
                        for (buf, hn) in ((ubuf, "u_b"), (gbuf, "g_f"), (ibuf, "inp_f")):
                            sync.dma_start(
                                out=buf[:, gcol:gcol + RNN_BLK],
                                in_=H[hn][ch * 128:(ch + 1) * 128,
                                          bb * SEQ + t0:bb * SEQ + t0 + RNN_BLK]
                            ).then_inc(ldp[par], 16)

        @b.gpsimd
        def _(gpsimd):
            for j in range(NCH):
                gpsimd.wait_ge(aldp[j % 2], 16 * (j // 2 + 1))
                gpsimd.tensor_copy(atb[:, j * 1024:(j + 1) * 1024],
                                   aload[:, (j % 2) * 1024:(j % 2 + 1) * 1024]).then_inc(acast, 1)
            for B in range(NB):
                par = B % 2
                gpsimd.wait_ge(dve_done, (B + 1) * RNN_BLK)
                t0 = B * RNN_BLK
                for ch in range(NCH):
                    for bb in range(BPC):
                        gcol = par * RNN_BW + (ch * BPC + bb) * RNN_BLK
                        gpsimd.dma_start(
                            out=H["states_b"][ch * 128:(ch + 1) * 128,
                                              bb * SEQ + t0:bb * SEQ + t0 + RNN_BLK],
                            in_=sblk[:, gcol:gcol + RNN_BLK]).then_inc(stp[par], 16)
            gpsimd.wait_ge(stp[0], 16 * NGRP * ((NB + 1) // 2))
            gpsimd.wait_ge(stp[1], 16 * NGRP * (NB // 2))

        @b.tensor
        def _(tensor):
            tensor.wait_ge(acast, NCH)
            tensor.wait_ge(kld, 16)
            tensor.wait_ge(init_done, 1)
            for t in range(SEQ):
                B = t // RNN_BLK
                par2 = t % 2
                if t % RNN_BLK == 0:
                    tensor.wait_ge(ldp[B % 2], 16 * 3 * NGRP * (B // 2 + 1))
                if t >= 2:
                    tensor.wait_ge(act_done, t - 1)
                if t >= 1:
                    tensor.wait_ge(dve_done, t)
                s_src = sinit[:].rearrange("p (g t) -> p g t", t=1) if t == 0 \
                    else tile3(sblk, t - 1)
                ut = tile3(ubuf, t)
                for ec in range(NCH):
                    tensor.matmul(ps[:, par2 * W + ec * BPC:par2 * W + (ec + 1) * BPC],
                                  iden[:], ut[:, ec * BPC:(ec + 1) * BPC, :],
                                  start=True, stop=False)
                    for dc in range(NCH):
                        ins = tensor.matmul(
                            ps[:, par2 * W + ec * BPC:par2 * W + (ec + 1) * BPC],
                            atb[:, dc * 1024 + ec * 128:dc * 1024 + (ec + 1) * 128],
                            s_src[:, dc * BPC:(dc + 1) * BPC, :],
                            start=False, stop=(dc == NCH - 1))
                ins.then_inc(mm_done, 1)

        @b.scalar
        def _(scalar):
            for t in range(SEQ):
                par2 = t % 2
                scalar.wait_ge(mm_done, t + 1)
                if t >= 2:
                    scalar.wait_ge(chv, cv0 + 2 * (t - 1))
                scalar.activation(stil[:, par2 * W:(par2 + 1) * W],
                                  ps[:, par2 * W:(par2 + 1) * W],
                                  AF.Tanh).then_inc(act_done, 1)

        @b.vector
        def _(vector):
            vector.memset(sinit[:], 0.0).then_inc(init_done, 1)
            for t in range(SEQ):
                B = t // RNN_BLK
                par2 = t % 2
                vector.wait_ge(act_done, t + 1)
                if t % RNN_BLK == 0 and B >= 2:
                    vector.wait_ge(stp[B % 2], 16 * NGRP * (B // 2))
                P.join(vector, "vector")
                ins = vector.tensor_sub(t1b[:, par2 * W:(par2 + 1) * W],
                                        stil[:, par2 * W:(par2 + 1) * W],
                                        tile3(ibuf, t))
                P.mark("vector", ins)
                P.join(vector, "vector")
                ins = vector.tensor_mul(t1b[:, par2 * W:(par2 + 1) * W],
                                        t1b[:, par2 * W:(par2 + 1) * W],
                                        tile3(gbuf, t))
                P.mark("vector", ins)
                P.join(vector, "vector")
                vector.tensor_add(tile3(sblk, t), t1b[:, par2 * W:(par2 + 1) * W],
                                  tile3(ibuf, t)).then_inc(dve_done, 1)


def p9_combine(P):
    nc, b, H = P.nc, P.block, P.hbm
    with contextlib.ExitStack() as S:
        xf = P.sbuf([128, NCH * TT], F32, "c9x", S)
        sf = P.sbuf([128, NCH * TT], F32, "c9s", S)
        gf = P.sbuf([128, NCH * TT], F32, "c9g", S)
        sgf = P.sbuf([128, 2 * TT], F32, "c9sg", S)
        x1f = P.sbuf([128, NCH * TT], F32, "c9x1", S)
        x1b = P.sbuf([128, NCH * TT], BF16, "c9xb", S)
        sq = P.sbuf([128, NCH * TT], BF16, "c9sq", S)
        xnb = P.sbuf([128, NCH * TT], BF16, "c9xn", S)
        t1 = P.sbuf([128, 2 * TT], F32, "c9t1", S)
        stat = P.sbuf([1, 6 * TT], F32, "c9st", S)
        ones128 = P.sbuf([128, 1], BF16, "c9o", S)
        onesk1f = P.sbuf([1, 128], F32, "c9of", S)
        p_sum = P.psum([1, TT], F32, "c9ps", S)
        p_sq = P.psum([1, TT], F32, "c9pq", S)
        b_mu = P.psum([128, TT], F32, "c9bm", S)
        b_rs = P.psum([128, TT], F32, "c9br", S)

        kld = P.sem("c9kld")
        ld = P.sem("c9ld")
        silu_d = P.sem("c9si")
        comb = P.sem("c9cb")
        psq_s = P.sem("c9psq")
        smm = P.sem("c9smm")
        tail = P.sem("c9tl")
        bmm = P.sem("c9bmm")
        appl = P.sem("c9ap")
        st = P.sem("c9st")

        NST = 16 * 16

        @b.sync
        def _(sync):
            sync.dma_start(out=ones128[:], in_=H["k_ones128"]).then_inc(kld, 16)
            sync.dma_start(out=onesk1f[:], in_=H["k_onesk1f"]).then_inc(kld, 16)
            for tt in range(NTT):
                if tt >= 1:
                    sync.wait_ge(appl, 8 * tt)
                for ch in range(NCH):
                    sync.dma_start(out=xf[:, ch * TT:(ch + 1) * TT],
                                   in_=H["xT"][ch * 128:(ch + 1) * 128, tt * TT:(tt + 1) * TT]
                                   ).then_inc(ld, 16)
                    sync.dma_start(out=sf[:, ch * TT:(ch + 1) * TT],
                                   in_=H["s0_f"][ch * 128:(ch + 1) * 128, tt * TT:(tt + 1) * TT]
                                   ).then_inc(ld, 16)
                    sync.dma_start(out=gf[:, ch * TT:(ch + 1) * TT],
                                   in_=H["gb_f"][ch * 128:(ch + 1) * 128, tt * TT:(tt + 1) * TT]
                                   ).then_inc(ld, 16)

        @b.scalar
        def _(scalar):
            for tt in range(NTT):
                scalar.wait_ge(ld, 16 * 24 * (tt + 1))
                for ch in range(NCH):
                    par = ch % 2
                    if tt * 8 + ch >= 2:
                        scalar.wait_ge(comb, tt * 8 + ch - 1)
                    scalar.activation(sgf[:, par * TT:(par + 1) * TT],
                                      gf[:, ch * TT:(ch + 1) * TT], AF.Silu
                                      ).then_inc(silu_d, 1)
                scalar.wait_ge(smm, tt * 2 + 1)
                ins = scalar.activation(stat[:, 0:TT], p_sum[:, :], AF.Identity, scale=1.0 / D)
                P.mark("scalar", ins)
                P.join(scalar, "scalar")
                scalar.activation(stat[:, 1 * TT:2 * TT], stat[:, 0:TT], AF.Square)
                scalar.wait_ge(smm, tt * 2 + 2)
                scalar.activation(stat[:, 2 * TT:3 * TT], p_sq[:, :], AF.Identity,
                                  scale=1.0 / D).then_inc(tail, 1)
                scalar.wait_ge(tail, 5 * tt + 2)
                scalar.activation(stat[:, 4 * TT:5 * TT], stat[:, 3 * TT:4 * TT],
                                  AF.Sqrt, bias=1e-5).then_inc(tail, 1)

        @b.vector
        def _(vector):
            for tt in range(NTT):
                for ch in range(NCH):
                    par = ch % 2
                    g = tt * 8 + ch
                    vector.wait_ge(silu_d, g + 1)
                    if tt >= 1 and ch == 0:
                        vector.wait_ge(st, NST * tt)
                        vector.wait_ge(psq_s, 8 * tt)
                        vector.wait_ge(smm, 2 * tt)
                    P.join(vector, "vector")
                    ins = vector.tensor_mul(t1[:, par * TT:(par + 1) * TT],
                                            sf[:, ch * TT:(ch + 1) * TT],
                                            sgf[:, par * TT:(par + 1) * TT])
                    P.mark("vector", ins)
                    P.join(vector, "vector")
                    ins = vector.tensor_add(x1f[:, ch * TT:(ch + 1) * TT],
                                            xf[:, ch * TT:(ch + 1) * TT],
                                            t1[:, par * TT:(par + 1) * TT])
                    P.mark("vector", ins)
                    P.join(vector, "vector")
                    vector.tensor_copy(x1b[:, ch * TT:(ch + 1) * TT],
                                       x1f[:, ch * TT:(ch + 1) * TT]).then_inc(comb, 1)
                vector.wait_ge(tail, 5 * tt + 1)
                P.join(vector, "vector")
                ins = vector.tensor_sub(stat[:, 3 * TT:4 * TT], stat[:, 2 * TT:3 * TT],
                                        stat[:, 1 * TT:2 * TT])
                ins.then_inc(tail, 1)
                vector.wait_ge(tail, 5 * tt + 3)
                P.join(vector, "vector")
                ins = vector.reciprocal(stat[:, 5 * TT:6 * TT], stat[:, 4 * TT:5 * TT])
                ins.then_inc(tail, 2)
                vector.wait_ge(bmm, 2 * (tt + 1))
                for ch in range(NCH):
                    par = ch % 2
                    P.join(vector, "vector")
                    ins = vector.tensor_sub(t1[:, par * TT:(par + 1) * TT],
                                            x1f[:, ch * TT:(ch + 1) * TT], b_mu[:, :])
                    P.mark("vector", ins)
                    P.join(vector, "vector")
                    ins = vector.tensor_mul(xnb[:, ch * TT:(ch + 1) * TT],
                                            t1[:, par * TT:(par + 1) * TT], b_rs[:, :])
                    ins.then_inc(appl, 1)

        @b.gpsimd
        def _(gpsimd):
            for tt in range(NTT):
                for ch in range(NCH):
                    gpsimd.wait_ge(comb, tt * 8 + ch + 1)
                    gpsimd.tensor_mul(sq[:, ch * TT:(ch + 1) * TT],
                                      x1b[:, ch * TT:(ch + 1) * TT],
                                      x1b[:, ch * TT:(ch + 1) * TT]).then_inc(psq_s, 1)
                for ch in range(NCH):
                    gpsimd.dma_start(out=H["x1_f"][ch * 128:(ch + 1) * 128,
                                                   tt * TT:(tt + 1) * TT],
                                     in_=x1f[:, ch * TT:(ch + 1) * TT]).then_inc(st, 16)
                gpsimd.wait_ge(appl, 8 * (tt + 1))
                for ch in range(NCH):
                    gpsimd.dma_start(out=H["xn1_b"][ch * 128:(ch + 1) * 128,
                                                    tt * TT:(tt + 1) * TT],
                                     in_=xnb[:, ch * TT:(ch + 1) * TT]).then_inc(st, 16)
            gpsimd.wait_ge(st, NST * NTT)

        @b.tensor
        def _(tensor):
            tensor.wait_ge(kld, 32)
            for tt in range(NTT):
                if tt >= 1:
                    tensor.wait_ge(tail, 5 * (tt - 1) + 2)
                tensor.wait_ge(comb, 8 * (tt + 1))
                for ch in range(NCH):
                    ins = tensor.matmul(p_sum[:, :], ones128[:],
                                        x1b[:, ch * TT:(ch + 1) * TT],
                                        start=(ch == 0), stop=(ch == NCH - 1))
                ins.then_inc(smm, 1)
                tensor.wait_ge(psq_s, 8 * (tt + 1))
                for ch in range(NCH):
                    ins = tensor.matmul(p_sq[:, :], ones128[:],
                                        sq[:, ch * TT:(ch + 1) * TT],
                                        start=(ch == 0), stop=(ch == NCH - 1))
                ins.then_inc(smm, 1)
                tensor.wait_ge(tail, 5 * tt + 5)
                if tt >= 1:
                    tensor.wait_ge(appl, 8 * tt)
                tensor.matmul(b_mu[:, :], onesk1f[:], stat[:, 0:TT],
                              start=True, stop=True).then_inc(bmm, 1)
                tensor.matmul(b_rs[:, :], onesk1f[:], stat[:, 5 * TT:6 * TT],
                              start=True, stop=True).then_inc(bmm, 1)


def p10a_inproj(P):
    nc, b, H = P.nc, P.block, P.hbm
    Kdim, Mdim = D, 2 * DI
    KC, MC = Kdim // 128, Mdim // 128
    with contextlib.ExitStack() as S:
        wload = P.sbuf([128, 2 * 128], F32, "iawl", S)
        wsb = P.sbuf([128, KC * Mdim], BF16, "iaw", S)
        xsb = P.sbuf([128, KC * TT], BF16, "iax", S)
        osb = P.sbuf([128, 2 * TT], BF16, "iao", S)
        p_m = [P.psum([128, TT], F32, "iap", S) for _ in range(2)]
        wldp = [P.sem("iaw0"), P.sem("iaw1")]
        wcast = P.sem("iawc")
        ld = P.sem("iald")
        pj = P.sem("iapj")
        pje = P.sem("iapje")
        stp2 = [P.sem("ias0"), P.sem("ias1")]

        @b.sync
        def _(sync):
            for j in range(KC * MC):
                kc, mc = j // MC, j % MC
                if j >= 2:
                    sync.wait_ge(wcast, j - 1)
                sync.dma_start(out=wload[:, (j % 2) * 128:(j % 2 + 1) * 128],
                               in_=H["inprojT"][kc * 128:(kc + 1) * 128, mc * 128:(mc + 1) * 128]
                               ).then_inc(wldp[j % 2], 16)
            for tt in range(NTT):
                if tt >= 1:
                    sync.wait_ge(pj, tt * MC)
                for kc in range(KC):
                    sync.dma_start(out=xsb[:, kc * TT:(kc + 1) * TT],
                                   in_=H["xn1_b"][kc * 128:(kc + 1) * 128,
                                                  tt * TT:(tt + 1) * TT]).then_inc(ld, 16)

        @b.gpsimd
        def _(gpsimd):
            for j in range(KC * MC):
                kc, mc = j // MC, j % MC
                gpsimd.wait_ge(wldp[j % 2], 16 * (j // 2 + 1))
                gpsimd.tensor_copy(wsb[:, kc * Mdim + mc * 128:kc * Mdim + (mc + 1) * 128],
                                   wload[:, (j % 2) * 128:(j % 2 + 1) * 128]).then_inc(wcast, 1)
            for tt in range(NTT):
                for mc in range(MC):
                    g = tt * MC + mc
                    gpsimd.wait_ge(pje, g + 1)
                    if mc < 16:
                        dsth = H["u0_b"][mc * 128:(mc + 1) * 128, tt * TT:(tt + 1) * TT]
                    else:
                        dsth = H["zs_b"][(mc - 16) * 128:(mc - 15) * 128, tt * TT:(tt + 1) * TT]
                    gpsimd.dma_start(out=dsth, in_=osb[:, (g % 2) * TT:(g % 2 + 1) * TT]
                                     ).then_inc(stp2[g % 2], 16)
            NG = NTT * MC
            gpsimd.wait_ge(stp2[0], 16 * ((NG + 1) // 2))
            gpsimd.wait_ge(stp2[1], 16 * (NG // 2))

        @b.tensor
        def _(tensor):
            tensor.wait_ge(wcast, KC * MC)
            for tt in range(NTT):
                tensor.wait_ge(ld, 16 * KC * (tt + 1))
                for mc in range(MC):
                    g = tt * MC + mc
                    if g >= 2:
                        tensor.wait_ge(pje, g - 1)
                    for kc in range(KC):
                        ins = tensor.matmul(
                            p_m[mc % 2][:, :],
                            wsb[:, kc * Mdim + mc * 128:kc * Mdim + (mc + 1) * 128],
                            xsb[:, kc * TT:(kc + 1) * TT],
                            start=(kc == 0), stop=(kc == KC - 1))
                    ins.then_inc(pj, 1)

        @b.scalar
        def _(scalar):
            for tt in range(NTT):
                for mc in range(MC):
                    g = tt * MC + mc
                    scalar.wait_ge(pj, g + 1)
                    if g >= 2:
                        scalar.wait_ge(stp2[g % 2], 16 * (g // 2))
                    fn = AF.Identity if mc < 16 else AF.Silu
                    scalar.activation(osb[:, (g % 2) * TT:(g % 2 + 1) * TT],
                                      p_m[mc % 2][:, :], fn).then_inc(pje, 1)


def p10b_conv(P):
    nc, b, H = P.nc, P.block, P.hbm
    KC = DI // 128
    with contextlib.ExitStack() as S:
        u0 = P.sbuf([128, KC * (TT + 3)], BF16, "cbu0", S)
        zs = P.sbuf([128, KC * TT], BF16, "cbzs", S)
        wxp = P.sbuf([128, KC * 64], BF16, "cbwx", S)
        wxl = P.sbuf([128, 2 * 64], F32, "cbwxl", S)
        convw = P.sbuf([128, KC * 4], F32, "cbcw", S)
        convb = P.sbuf([128, KC], F32, "cbcb", S)
        dtw = P.sbuf([128, KC], F32, "cbdw", S)
        dtb = P.sbuf([128, KC], F32, "cbdb", S)
        onesk1f = P.sbuf([1, 128], F32, "cbok", S)
        acc = P.sbuf([128, 2 * TT], F32, "cbac", S)
        u2b = P.sbuf([128, KC * TT], BF16, "cbu2", S)
        bcf = P.sbuf([33, TT], F32, "cbbc", S)
        dpre = P.sbuf([128, 2 * TT], F32, "cbdp", S)
        tso = P.sbuf([128, 2 * TT], F32, "cbts", S)
        delt = P.sbuf([128, 2 * TT], F32, "cbdl", S)
        dtef = P.sbuf([128, 2 * TT], F32, "cbdt", S)
        dub = P.sbuf([128, 2 * TT], BF16, "cbdu", S)
        p_bc = P.psum([33, TT], F32, "cbpb", S)
        p_dr = P.psum([128, TT], F32, "cbpd", S)

        kld = P.sem("cbkld")
        wldp = [P.sem("cbw0"), P.sem("cbw1")]
        wcast = P.sem("cbwc")
        ld = P.sem("cbld")
        cvd = P.sem("cbcv")
        xpm = P.sem("cbxm")
        xpe = P.sem("cbxe")
        drb = P.sem("cbdr")
        dpd = P.sem("cbdpd")
        spd = P.sem("cbsp")
        dl = P.sem("cbdle")
        ded = P.sem("cbded")
        dted = P.sem("cbdtd")
        st = P.sem("cbst")
        hz = P.sem("cbhz")
        stdt = [P.sem("cbsd0"), P.sem("cbsd1")]

        NST = (KC + 1) * 16

        @b.sync
        def _(sync):
            sync.dma_start(out=convw[:], in_=H["convw"]).then_inc(kld, 16)
            sync.dma_start(out=convb[:], in_=H["convb"]).then_inc(kld, 16)
            sync.dma_start(out=dtw[:], in_=H["dtw"]).then_inc(kld, 16)
            sync.dma_start(out=dtb[:], in_=H["dtb"]).then_inc(kld, 16)
            sync.dma_start(out=onesk1f[:], in_=H["k_onesk1f"]).then_inc(kld, 16)
            for j in range(KC):
                if j >= 2:
                    sync.wait_ge(wcast, j - 1)
                sync.dma_start(out=wxl[:, (j % 2) * 64:(j % 2) * 64 + 33],
                               in_=H["xprojT"][j * 128:(j + 1) * 128, :]).then_inc(wldp[j % 2], 16)
            for tt in range(NTT):
                tts = tt % NTTS
                if tt >= 1:
                    sync.wait_ge(xpm, tt)
                    sync.wait_ge(dted, 16 * tt)
                for ch in range(KC):
                    if tts == 0:
                        sync.dma_start(out=u0[:, ch * (TT + 3) + 3:(ch + 1) * (TT + 3)],
                                       in_=H["u0_b"][ch * 128:(ch + 1) * 128,
                                                     tt * TT:(tt + 1) * TT]).then_inc(ld, 16)
                    else:
                        sync.dma_start(out=u0[:, ch * (TT + 3):(ch + 1) * (TT + 3)],
                                       in_=H["u0_b"][ch * 128:(ch + 1) * 128,
                                                     tt * TT - 3:(tt + 1) * TT]).then_inc(ld, 16)
                    sync.dma_start(out=zs[:, ch * TT:(ch + 1) * TT],
                                   in_=H["zs_b"][ch * 128:(ch + 1) * 128,
                                                 tt * TT:(tt + 1) * TT]).then_inc(ld, 16)

        @b.gpsimd
        def _(gpsimd):
            gpsimd.wait_ge(kld, 16 * 5)
            for j in range(KC):
                gpsimd.wait_ge(wldp[j % 2], 16 * (j // 2 + 1))
                ins = gpsimd.memset(wxp[:, j * 64:(j + 1) * 64], 0.0)
                P.mark("gpsimd", ins)
                P.join(gpsimd, "gpsimd")
                gpsimd.tensor_copy(wxp[:, j * 64:j * 64 + 33],
                                   wxl[:, (j % 2) * 64:(j % 2) * 64 + 33]).then_inc(wcast, 1)
            for tt in range(NTT):
                tts = tt % NTTS
                gpsimd.wait_ge(ld, 16 * 2 * KC * (tt + 1))
                if tts == 0:
                    if tt >= 1:
                        gpsimd.wait_ge(cvd, 16 * tt)
                    for ch in range(KC):
                        ins = gpsimd.memset(u0[:, ch * (TT + 3):ch * (TT + 3) + 3], 0.0)
                    ins.then_inc(hz, 1)
                for ch in range(KC):
                    gpsimd.wait_ge(dted, 16 * tt + ch + 1)
                    gpsimd.dma_start(out=H["u2_b"][ch * 128:(ch + 1) * 128,
                                                   tt * TT:(tt + 1) * TT],
                                     in_=u2b[:, ch * TT:(ch + 1) * TT]).then_inc(st, 16)
                    gpsimd.dma_start(out=H["dte_f"][ch * 128:(ch + 1) * 128,
                                                    tt * TT:(tt + 1) * TT],
                                     in_=dtef[:, (ch % 2) * TT:(ch % 2 + 1) * TT]
                                     ).then_inc(stdt[ch % 2], 16)
                    gpsimd.dma_start(out=H["du_b"][ch * 128:(ch + 1) * 128,
                                                   tt * TT:(tt + 1) * TT],
                                     in_=dub[:, (ch % 2) * TT:(ch % 2 + 1) * TT]
                                     ).then_inc(stdt[ch % 2], 16)
                gpsimd.wait_ge(xpe, tt + 1)
                gpsimd.dma_start(out=H["bc_f"][:, tt * TT:(tt + 1) * TT],
                                 in_=bcf[:]).then_inc(st, 16)
            gpsimd.wait_ge(st, NST * NTT)
            gpsimd.wait_ge(stdt[0], 32 * 8 * NTT)
            gpsimd.wait_ge(stdt[1], 32 * 8 * NTT)

        @b.vector
        def _(vector):
            for tt in range(NTT):
                vector.wait_ge(ld, 16 * 2 * KC * (tt + 1))
                if tt % NTTS == 0:
                    vector.wait_ge(hz, tt // NTTS + 1)
                if tt >= 1:
                    vector.wait_ge(st, NST * tt)
                for ch in range(KC):
                    par = ch % 2
                    base = ch * (TT + 3)
                    P.join(vector, "vector")
                    ins = vector.tensor_scalar_mul(acc[:, par * TT:(par + 1) * TT],
                                                   u0[:, base:base + TT],
                                                   convw[:, ch * 4:ch * 4 + 1])
                    P.mark("vector", ins)
                    for k in range(1, 4):
                        P.join(vector, "vector")
                        ins = vector.scalar_tensor_tensor(
                            acc[:, par * TT:(par + 1) * TT],
                            u0[:, base + k:base + k + TT],
                            convw[:, ch * 4 + k:ch * 4 + k + 1],
                            acc[:, par * TT:(par + 1) * TT],
                            ALU.mult, ALU.add)
                        P.mark("vector", ins)
                    P.join(vector, "vector")
                    ins = vector.scalar_tensor_tensor(
                        u2b[:, ch * TT:(ch + 1) * TT],
                        acc[:, par * TT:(par + 1) * TT],
                        convb[:, ch:ch + 1],
                        zs[:, ch * TT:(ch + 1) * TT],
                        ALU.add, ALU.mult)
                    ins.then_inc(cvd, 1)
                vector.wait_ge(xpm, tt + 1)
                P.join(vector, "vector")
                ins = vector.tensor_copy(bcf[:], p_bc[:, :])
                ins.then_inc(xpe, 1)
                vector.wait_ge(drb, tt + 1)
                for ch in range(KC):
                    par = ch % 2
                    gc = 16 * tt + ch
                    if gc >= 2:
                        vector.wait_ge(spd, gc - 1)
                    P.join(vector, "vector")
                    ins = vector.tensor_scalar(dpre[:, par * TT:(par + 1) * TT],
                                               p_dr[:, :], dtw[:, ch:ch + 1],
                                               dtb[:, ch:ch + 1], ALU.mult, ALU.add)
                    ins.then_inc(dpd, 1)
                    vector.wait_ge(spd, gc + 1)
                    if gc >= 2:
                        vector.wait_ge(ded, gc - 1)
                    P.join(vector, "vector")
                    ins = vector.scalar_tensor_tensor(
                        delt[:, par * TT:(par + 1) * TT],
                        dpre[:, par * TT:(par + 1) * TT], 0.0,
                        tso[:, par * TT:(par + 1) * TT], ALU.max, ALU.add)
                    ins.then_inc(dl, 1)
                    vector.wait_ge(ded, gc + 1)
                    if gc >= 2:
                        vector.wait_ge(stdt[par], 32 * (gc // 2))
                    P.join(vector, "vector")
                    ins = vector.tensor_mul(dub[:, par * TT:(par + 1) * TT],
                                            dtef[:, par * TT:(par + 1) * TT],
                                            u2b[:, ch * TT:(ch + 1) * TT])
                    ins.then_inc(dted, 1)

        @b.scalar
        def _(scalar):
            for tt in range(NTT):
                for ch in range(KC):
                    par = ch % 2
                    gc = 16 * tt + ch
                    scalar.wait_ge(dpd, gc + 1)
                    if gc >= 2:
                        scalar.wait_ge(dl, gc - 1)
                    ins = scalar.activation(tso[:, par * TT:(par + 1) * TT],
                                            dpre[:, par * TT:(par + 1) * TT], AF.Abs)
                    P.mark("scalar", ins)
                    P.join(scalar, "scalar")
                    ins = scalar.activation(tso[:, par * TT:(par + 1) * TT],
                                            tso[:, par * TT:(par + 1) * TT], AF.Exp,
                                            scale=-1.0)
                    P.mark("scalar", ins)
                    P.join(scalar, "scalar")
                    scalar.activation(tso[:, par * TT:(par + 1) * TT],
                                      tso[:, par * TT:(par + 1) * TT], AF.Ln,
                                      bias=1.0).then_inc(spd, 1)
                    scalar.wait_ge(dl, gc + 1)
                    if gc >= 2:
                        scalar.wait_ge(dted, gc - 1)
                        scalar.wait_ge(stdt[par], 32 * (gc // 2))
                    scalar.activation(dtef[:, par * TT:(par + 1) * TT],
                                      delt[:, par * TT:(par + 1) * TT],
                                      AF.Exp).then_inc(ded, 1)

        @b.tensor
        def _(tensor):
            tensor.wait_ge(wcast, KC)
            tensor.wait_ge(kld, 16 * 5)
            for tt in range(NTT):
                tensor.wait_ge(cvd, 16 * (tt + 1))
                if tt >= 1:
                    tensor.wait_ge(xpe, tt)
                for kc in range(KC):
                    ins = tensor.matmul(p_bc[0:33, :], wxp[:, kc * 64:kc * 64 + 33],
                                        u2b[:, kc * TT:(kc + 1) * TT],
                                        start=(kc == 0), stop=(kc == KC - 1))
                ins.then_inc(xpm, 1)
                tensor.wait_ge(xpe, tt + 1)
                if tt >= 1:
                    tensor.wait_ge(dpd, 16 * tt)
                tensor.matmul(p_dr[:, :], onesk1f[:], bcf[0:1, :],
                              start=True, stop=True).then_inc(drb, 1)


def p11_scan_old(P):
    nc, b, H = P.nc, P.block, P.hbm
    KC = DI // 128
    with contextlib.ExitStack() as S:
        alog = P.sbuf([128, KC * DS], F32, "snal", S)
        asb = P.sbuf([128, KC * DS], F32, "snA", S)
        onesk1f = P.sbuf([1, 128], F32, "snok", S)
        iden = P.sbuf([128, 128], BF16, "snid", S)
        dvec = P.sbuf([128, KC], F32, "snD", S)
        btile = P.sbuf([1, 2 * SEQ], F32, "snbt", S)
        ctile = P.sbuf([1, 2 * SEQ], F32, "snct", S)
        dte = P.sbuf([128, 2 * SEQ], F32, "sndte", S)
        dub = P.sbuf([128, 2 * SEQ], BF16, "sndu", S)
        u2b = P.sbuf([128, 2 * SEQ], BF16, "snu2", S)
        brt = P.sbuf([128, 2 * SEQ], BF16, "snbr", S)
        crt = P.sbuf([128, 2 * SEQ], BF16, "sncr", S)
        af = P.sbuf([128, 2 * SEQ], F32, "sna", S)
        binb = P.sbuf([128, 2 * SEQ], BF16, "snb", S)
        hb = P.sbuf([128, 2 * SEQ], BF16, "snh", S)
        hcb = P.sbuf([128, 2 * SEQ], BF16, "snhc", S)
        yob = P.sbuf([128, 2 * SEQ], BF16, "snyo", S)
        p_bc = [P.psum([128, 512], F32, "snpb", S) for _ in range(2)]
        p_y = [P.psum([128, 512], F32, "snpy", S) for _ in range(4)]

        kld = P.sem("snkld")
        acast = P.sem("snac")
        ldt = P.sem("snldt")
        ldbc = [P.sem("snlb0"), P.sem("snlb1")]
        bcm = P.sem("snbcm")
        bce = P.sem("snbce")
        aex = P.sem("snaex")
        bin_d = P.sem("snbin")
        scn = P.sem("snscn")
        hcd = P.sem("snhc")
        yac = P.sem("snyac")
        yev = P.sem("snyev")
        stp = [P.sem("snst0"), P.sem("snst1")]

        def sidx(bb, ic, s):
            return (bb * KC + ic) * DS + s

        @b.sync
        def _(sync):
            sync.dma_start(out=alog[:].rearrange("p (c s) -> p c s", s=DS),
                           in_=H["alog"].rearrange("(c p) s -> p c s", p=128)
                           ).then_inc(kld, 16)
            sync.dma_start(out=onesk1f[:], in_=H["k_onesk1f"]).then_inc(kld, 16)
            sync.dma_start(out=iden[:], in_=H["k_ident"]).then_inc(kld, 16)
            sync.dma_start(out=dvec[:], in_=H["dvec"]).then_inc(kld, 16)
            for bb in range(BPC):
                for ic in range(KC):
                    u = bb * KC + ic
                    par = u % 2
                    if u >= 2:
                        sync.wait_ge(yev, 4 * (u - 1))
                    for (dst, hn) in ((dte, "dte_f"), (dub, "du_b"), (u2b, "u2_b")):
                        sync.dma_start(out=dst[:, par * SEQ:(par + 1) * SEQ],
                                       in_=H[hn][ic * 128:(ic + 1) * 128,
                                                 bb * SEQ:(bb + 1) * SEQ]).then_inc(ldt, 16)
                    for s in range(DS):
                        g = (bb * KC + ic) * DS + s
                        parg = g % 2
                        if g >= 2:
                            sync.wait_ge(bcm, 8 * (g - 1))
                        sync.dma_start(out=btile[0:1, parg * SEQ:(parg + 1) * SEQ],
                                       in_=H["bc_f"][1 + s:2 + s, bb * SEQ:(bb + 1) * SEQ]
                                       ).then_inc(ldbc[parg], 16)
                        sync.dma_start(out=ctile[0:1, parg * SEQ:(parg + 1) * SEQ],
                                       in_=H["bc_f"][17 + s:18 + s, bb * SEQ:(bb + 1) * SEQ]
                                       ).then_inc(ldbc[parg], 16)

        @b.tensor
        def _(tensor):
            tensor.wait_ge(kld, 16 * 4)
            tensor.wait_ge(acast, KC + 1)
            for bb in range(BPC):
                for ic in range(KC):
                    u = bb * KC + ic
                    for s in range(DS):
                        g = sidx(bb, ic, s)
                        parg = g % 2
                        tensor.wait_ge(ldbc[parg], 32 * (g // 2 + 1))
                        for tc in range(4):
                            if g >= 1 or tc >= 1:
                                tensor.wait_ge(bce, 8 * g + 2 * tc)
                            tensor.matmul(p_bc[0][:, :], onesk1f[:],
                                          btile[0:1, parg * SEQ + tc * 512:
                                                parg * SEQ + (tc + 1) * 512],
                                          start=True, stop=True).then_inc(bcm, 1)
                            tensor.matmul(p_bc[1][:, :], onesk1f[:],
                                          ctile[0:1, parg * SEQ + tc * 512:
                                                parg * SEQ + (tc + 1) * 512],
                                          start=True, stop=True).then_inc(bcm, 1)
                        tensor.wait_ge(hcd, g + 1)
                        if s == 0 and u >= 1:
                            tensor.wait_ge(yev, 4 * u)
                        for tc in range(4):
                            ins = tensor.matmul(
                                p_y[tc][:, :], iden[:],
                                hcb[:, parg * SEQ + tc * 512:parg * SEQ + (tc + 1) * 512],
                                start=(s == 0), stop=(s == DS - 1))
                        ins.then_inc(yac, 1)

        @b.scalar
        def _(scalar):
            scalar.wait_ge(kld, 16 * 4)
            for c in range(KC):
                scalar.activation(asb[:, c * DS:(c + 1) * DS],
                                  alog[:, c * DS:(c + 1) * DS], AF.Exp).then_inc(acast, 1)
            for bb in range(BPC):
                for ic in range(KC):
                    u = bb * KC + ic
                    par = u % 2
                    for s in range(DS):
                        g = sidx(bb, ic, s)
                        parg = g % 2
                        for tc in range(4):
                            scalar.wait_ge(bcm, 8 * g + 2 * tc + 2)
                            if g >= 2 and tc == 0:
                                scalar.wait_ge(bin_d, g - 1)
                                scalar.wait_ge(hcd, g - 1)
                            ins = scalar.activation(
                                brt[:, parg * SEQ + tc * 512:parg * SEQ + (tc + 1) * 512],
                                p_bc[0][:, :], AF.Identity)
                            P.mark("scalar", ins)
                            scalar.activation(
                                crt[:, parg * SEQ + tc * 512:parg * SEQ + (tc + 1) * 512],
                                p_bc[1][:, :], AF.Identity).then_inc(bce, 2)
                        scalar.wait_ge(ldt, 16 * 3 * (u + 1))
                        if g >= 2:
                            scalar.wait_ge(scn, g - 1)
                        scalar.activation(af[:, parg * SEQ:(parg + 1) * SEQ],
                                          dte[:, par * SEQ:(par + 1) * SEQ], AF.Exp,
                                          scale=asb[:, ic * DS + s:ic * DS + s + 1]
                                          ).then_inc(aex, 1)

        @b.gpsimd
        def _(gpsimd):
            for bb in range(BPC):
                for ic in range(KC):
                    u = bb * KC + ic
                    par = u % 2
                    if u >= 1:
                        up = u - 1
                        gpsimd.wait_ge(yev, 4 * (up + 1))
                        gpsimd.dma_start(
                            out=H["y_b"][(up % KC) * 128:(up % KC + 1) * 128,
                                         (up // KC) * SEQ:(up // KC + 1) * SEQ],
                            in_=yob[:, (up % 2) * SEQ:(up % 2 + 1) * SEQ]
                            ).then_inc(stp[up % 2], 16)
                    for s in range(DS):
                        g = sidx(bb, ic, s)
                        parg = g % 2
                        gpsimd.wait_ge(bce, 8 * g + 8)
                        gpsimd.wait_ge(ldt, 16 * 3 * (u + 1))
                        if g >= 2:
                            gpsimd.wait_ge(scn, g - 1)
                        gpsimd.tensor_mul(binb[:, parg * SEQ:(parg + 1) * SEQ],
                                          dub[:, par * SEQ:(par + 1) * SEQ],
                                          brt[:, parg * SEQ:(parg + 1) * SEQ]
                                          ).then_inc(bin_d, 1)
                        gpsimd.wait_ge(scn, g + 1)
                        if g >= 2:
                            gpsimd.wait_ge(yac, g - 1)
                        gpsimd.tensor_mul(hcb[:, parg * SEQ:(parg + 1) * SEQ],
                                          hb[:, parg * SEQ:(parg + 1) * SEQ],
                                          crt[:, parg * SEQ:(parg + 1) * SEQ]).then_inc(hcd, 1)
            up = BPC * KC - 1
            gpsimd.wait_ge(yev, 4 * (up + 1))
            gpsimd.dma_start(
                out=H["y_b"][(up % KC) * 128:(up % KC + 1) * 128,
                             (up // KC) * SEQ:(up // KC + 1) * SEQ],
                in_=yob[:, (up % 2) * SEQ:(up % 2 + 1) * SEQ]).then_inc(stp[up % 2], 16)
            gpsimd.wait_ge(stp[0], 16 * ((BPC * KC + 1) // 2))
            gpsimd.wait_ge(stp[1], 16 * (BPC * KC // 2))

        @b.vector
        def _(vector):
            vector.wait_ge(acast, KC)
            P.join(vector, "vector")
            ins = vector.tensor_scalar_mul(asb[:], asb[:], -1.0)
            ins.then_inc(acast, 1)
            for bb in range(BPC):
                for ic in range(KC):
                    u = bb * KC + ic
                    par = u % 2
                    for s in range(DS):
                        g = sidx(bb, ic, s)
                        parg = g % 2
                        vector.wait_ge(aex, g + 1)
                        vector.wait_ge(bin_d, g + 1)
                        if g >= 2:
                            vector.wait_ge(hcd, g - 1)
                        P.join(vector, "vector")
                        ins = vector.tensor_tensor_scan(
                            hb[:, parg * SEQ:(parg + 1) * SEQ],
                            af[:, parg * SEQ:(parg + 1) * SEQ],
                            binb[:, parg * SEQ:(parg + 1) * SEQ],
                            0.0, ALU.mult, ALU.add)
                        ins.then_inc(scn, 1)
                    vector.wait_ge(yac, sidx(bb, ic, DS - 1) + 1)
                    if u >= 2:
                        vector.wait_ge(stp[u % 2], 16 * (u // 2))
                    for tc in range(4):
                        P.join(vector, "vector")
                        ins = vector.scalar_tensor_tensor(
                            yob[:, par * SEQ + tc * 512:par * SEQ + (tc + 1) * 512],
                            u2b[:, par * SEQ + tc * 512:par * SEQ + (tc + 1) * 512],
                            dvec[:, ic:ic + 1],
                            p_y[tc][:, :], ALU.mult, ALU.add)
                        ins.then_inc(yev, 1)


def p11_scan(P):
    nc, b, H = P.nc, P.block, P.hbm
    KC = DI // 128
    NPAIR = DS * 4
    with contextlib.ExitStack() as S:
        alog = P.sbuf([128, KC * DS], F32, "snal", S)
        asb = P.sbuf([128, KC * DS], F32, "snA", S)
        onesk1f = P.sbuf([1, 128], F32, "snok", S)
        iden = P.sbuf([128, 128], BF16, "snid", S)
        dvec = P.sbuf([128, KC], F32, "snD", S)
        btile = P.sbuf([1, 2 * 512], F32, "snbal", S)
        ctile = P.sbuf([1, 2 * 512], F32, "sncal", S)
        brt_all = P.sbuf([128, DS * SEQ], BF16, "snbra", S)
        crt_all = P.sbuf([128, DS * SEQ], BF16, "sncra", S)
        dte = P.sbuf([128, SEQ], F32, "sndte", S)
        dub = P.sbuf([128, SEQ], BF16, "sndu", S)
        u2b = P.sbuf([128, SEQ], BF16, "snu2", S)
        af = P.sbuf([128, SEQ], F32, "sna", S)
        binb = P.sbuf([128, 2 * SEQ], BF16, "snb", S)
        hb = P.sbuf([128, SEQ], BF16, "snh", S)
        hcb = P.sbuf([128, SEQ], BF16, "snhc", S)
        yob = P.sbuf([128, SEQ], BF16, "snyo", S)
        p_bc = [P.psum([128, 512], F32, "snpb", S) for _ in range(4)]
        p_y = [P.psum([128, 512], F32, "snpy", S) for _ in range(4)]

        kld = P.sem("snkld")
        acast = P.sem("snac")
        bld = P.sem("snbld")
        bcm = P.sem("snbcm")
        bce = P.sem("snbce")
        ldt = P.sem("snldt")
        aex = P.sem("snaex")
        bin_d = P.sem("snbin")
        scn = P.sem("snscn")
        hcd = P.sem("snhc")
        yac = P.sem("snyac")
        yev = P.sem("snyev")
        stp = P.sem("snstp")

        @b.sync
        def _(sync):
            sync.dma_start(out=alog[:].rearrange("p (c s) -> p c s", s=DS),
                           in_=H["alog"].rearrange("(c p) s -> p c s", p=128)
                           ).then_inc(kld, 16)
            sync.dma_start(out=onesk1f[:], in_=H["k_onesk1f"]).then_inc(kld, 16)
            sync.dma_start(out=iden[:], in_=H["k_ident"]).then_inc(kld, 16)
            sync.dma_start(out=dvec[:], in_=H["dvec"]).then_inc(kld, 16)
            for bb in range(BPC):
                for s in range(DS):
                    for tc in range(4):
                        j = bb * NPAIR + s * 4 + tc
                        if j >= 2:
                            sync.wait_ge(bcm, 2 * j - 2)
                        sync.dma_start(
                            out=btile[0:1, (j % 2) * 512:(j % 2 + 1) * 512],
                            in_=H["bc_f"][1 + s:2 + s, bb * SEQ + tc * 512:
                                          bb * SEQ + (tc + 1) * 512]
                            ).then_inc(bld, 16)
                        sync.dma_start(
                            out=ctile[0:1, (j % 2) * 512:(j % 2 + 1) * 512],
                            in_=H["bc_f"][1 + DS + s:2 + DS + s, bb * SEQ + tc * 512:
                                          bb * SEQ + (tc + 1) * 512]
                            ).then_inc(bld, 16)
                for ic in range(KC):
                    u = bb * KC + ic
                    sync.wait_ge(aex, DS * u)
                    sync.wait_ge(bin_d, DS * u)
                    sync.wait_ge(yev, 4 * u)
                    for (dst, hn) in ((dte, "dte_f"), (dub, "du_b"), (u2b, "u2_b")):
                        sync.dma_start(out=dst[:],
                                       in_=H[hn][ic * 128:(ic + 1) * 128,
                                                 bb * SEQ:(bb + 1) * SEQ]
                                       ).then_inc(ldt, 16)

        @b.tensor
        def _(tensor):
            tensor.wait_ge(kld, 16 * 4)
            for bb in range(BPC):
                for s in range(DS):
                    for tc in range(4):
                        j = bb * NPAIR + s * 4 + tc
                        tensor.wait_ge(bld, 32 * (j + 1))
                        if j >= 2:
                            tensor.wait_ge(bce, 2 * j - 2)
                        tensor.matmul(p_bc[j % 2][:, :], onesk1f[:],
                                      btile[0:1, (j % 2) * 512:(j % 2 + 1) * 512],
                                      start=True, stop=True).then_inc(bcm, 1)
                        tensor.matmul(p_bc[2 + j % 2][:, :], onesk1f[:],
                                      ctile[0:1, (j % 2) * 512:(j % 2 + 1) * 512],
                                      start=True, stop=True).then_inc(bcm, 1)
                for ic in range(KC):
                    u = bb * KC + ic
                    for s in range(DS):
                        g = u * DS + s
                        tensor.wait_ge(hcd, g + 1)
                        if s == 0:
                            tensor.wait_ge(yev, 4 * u)
                        for tc in range(4):
                            ins = tensor.matmul(
                                p_y[tc][:, :], iden[:],
                                hcb[:, tc * 512:(tc + 1) * 512],
                                start=(s == 0), stop=(s == DS - 1))
                        ins.then_inc(yac, 1)

        @b.scalar
        def _(scalar):
            scalar.wait_ge(kld, 16 * 4)
            for c in range(KC):
                scalar.activation(asb[:, c * DS:(c + 1) * DS],
                                  alog[:, c * DS:(c + 1) * DS], AF.Exp).then_inc(acast, 1)
            scalar.wait_ge(acast, KC + 1)
            for bb in range(BPC):
                if bb >= 1:
                    scalar.wait_ge(bin_d, DS * KC * bb)
                    scalar.wait_ge(hcd, DS * KC * bb)
                for s in range(DS):
                    for tc in range(4):
                        j = bb * NPAIR + s * 4 + tc
                        scalar.wait_ge(bcm, 2 * j + 2)
                        ins = scalar.activation(
                            brt_all[:, s * SEQ + tc * 512:s * SEQ + (tc + 1) * 512],
                            p_bc[j % 2][:, :], AF.Identity)
                        ins.then_inc(bce, 1)
                        scalar.activation(
                            crt_all[:, s * SEQ + tc * 512:s * SEQ + (tc + 1) * 512],
                            p_bc[2 + j % 2][:, :], AF.Identity).then_inc(bce, 1)
                for ic in range(KC):
                    u = bb * KC + ic
                    for s in range(DS):
                        g = u * DS + s
                        scalar.wait_ge(ldt, 48 * (u + 1))
                        if g >= 1:
                            scalar.wait_ge(scn, g)
                        scalar.activation(af[:], dte[:], AF.Exp,
                                          scale=asb[:, ic * DS + s:ic * DS + s + 1]
                                          ).then_inc(aex, 1)

        @b.gpsimd
        def _(gpsimd):
            for bb in range(BPC):
                for ic in range(KC):
                    u = bb * KC + ic
                    if u >= 1:
                        up = u - 1
                        gpsimd.wait_ge(yev, 4 * u)
                        gpsimd.dma_start(
                            out=H["y_b"][(up % KC) * 128:(up % KC + 1) * 128,
                                         (up // KC) * SEQ:(up // KC + 1) * SEQ],
                            in_=yob[:]).then_inc(stp, 16)
                    for s in range(DS):
                        g = u * DS + s
                        gpsimd.wait_ge(bce, 2 * NPAIR * (bb + 1))
                        gpsimd.wait_ge(ldt, 48 * (u + 1))
                        if g >= 2:
                            gpsimd.wait_ge(scn, g - 1)
                        gpsimd.tensor_mul(binb[:, (g % 2) * SEQ:(g % 2 + 1) * SEQ],
                                          dub[:],
                                          brt_all[:, s * SEQ:(s + 1) * SEQ]
                                          ).then_inc(bin_d, 1)
            up = BPC * KC - 1
            gpsimd.wait_ge(yev, 4 * (up + 1))
            gpsimd.dma_start(
                out=H["y_b"][(up % KC) * 128:(up % KC + 1) * 128,
                             (up // KC) * SEQ:(up // KC + 1) * SEQ],
                in_=yob[:]).then_inc(stp, 16)
            gpsimd.wait_ge(stp, 16 * BPC * KC)

        @b.vector
        def _(vector):
            vector.wait_ge(acast, KC)
            P.join(vector, "vector")
            ins = vector.tensor_scalar_mul(asb[:], asb[:], -1.0)
            ins.then_inc(acast, 1)
            for bb in range(BPC):
                for ic in range(KC):
                    u = bb * KC + ic
                    for s in range(DS):
                        g = u * DS + s
                        vector.wait_ge(aex, g + 1)
                        vector.wait_ge(bin_d, g + 1)
                        P.join(vector, "vector")
                        ins = vector.tensor_tensor_scan(
                            hb[:],
                            af[:],
                            binb[:, (g % 2) * SEQ:(g % 2 + 1) * SEQ],
                            0.0, ALU.mult, ALU.add)
                        ins.then_inc(scn, 1)
                        if g >= 1:
                            vector.wait_ge(yac, g)
                        P.join(vector, "vector")
                        ins = vector.tensor_mul(hcb[:], hb[:],
                                                crt_all[:, s * SEQ:(s + 1) * SEQ])
                        ins.then_inc(hcd, 1)
                    vector.wait_ge(yac, DS * (u + 1))
                    if u >= 1:
                        vector.wait_ge(stp, 16 * u)
                    for tc in range(4):
                        P.join(vector, "vector")
                        ins = vector.scalar_tensor_tensor(
                            yob[:, tc * 512:(tc + 1) * 512],
                            u2b[:, tc * 512:(tc + 1) * 512],
                            dvec[:, ic:ic + 1],
                            p_y[tc][:, :], ALU.mult, ALU.add)
                        ins.then_inc(yev, 1)


def p12_outproj(P):
    emit_proj(P, "y_b", "outprojT", None, [("s1_f", F32)], DI, D)


def p13_moe(P):
    nc, b, H = P.nc, P.block, P.hbm
    NT4 = SEQ // TT
    with contextlib.ExitStack() as S:
        wload = P.sbuf([128, 2 * 128], F32, "mwl", S)
        wd1 = P.sbuf([128, NEXP * NCH * 2 * 128], BF16, "mw1", S)
        wd2 = P.sbuf([128, NEXP * 2 * NCH * 128], BF16, "mw2", S)
        wad = P.sbuf([128, NCH * 4 * 128], BF16, "mwa", S)
        wau = P.sbuf([128, 4 * NCH * 128], BF16, "mwu", S)
        wbs = P.sbuf([128, NCH * NCH * 128], BF16, "mwb", S)
        bd1c = P.sbuf([128, NEXP * 2], F32, "mb1", S)
        bd2c = P.sbuf([128, NEXP * NCH], F32, "mb2", S)
        badcc = P.sbuf([128, 4], F32, "mbad", S)
        bbc = P.sbuf([128, NCH], F32, "mbb", S)
        bauc = P.sbuf([8, D], F32, "mbau", S)
        baub = P.sbuf([8, D], BF16, "mbaub", S)
        ones128f = P.sbuf([128, 1], F32, "mo1", S)
        ones11 = P.sbuf([1, 1], F32, "mo11", S)
        s8c = P.sbuf([8, 128], F32, "ms8", S)
        m84 = P.sbuf([8, 4], F32, "mm84", S)
        ones8 = P.sbuf([8, TT], F32, "mo8", S)
        xn1 = P.sbuf([128, NCH * SEQ], BF16, "mxn", S)
        x1f = P.sbuf([128, NCH * TT], F32, "mx1", S)
        s1f = P.sbuf([128, NCH * TT], F32, "ms1", S)
        h1b = P.sbuf([128, 2 * TT], BF16, "mh1", S)
        db = P.sbuf([128, 2 * TT], BF16, "md", S)
        dsq = P.sbuf([128, 2 * TT], BF16, "mdsq", S)
        lacc = P.sbuf([128, 64], F32, "mla", S)
        ltmp = P.sbuf([128, 2], F32, "mlt", S)
        lrow = P.sbuf([1, 64], F32, "mlr", S)
        loss8 = P.sbuf([1, 8], F32, "mls", S)
        lmin = P.sbuf([1, 1], F32, "mlm", S)
        ohrow = P.sbuf([1, 8], F32, "moh", S)
        oh8 = P.sbuf([8, 1], F32, "mo8c", S)
        rhs84 = P.sbuf([8, 4], F32, "mr84", S)
        ohR = P.sbuf([128, 4], F32, "mohR", S)
        ohones = P.sbuf([8, TT], BF16, "mooh", S)
        arel = P.sbuf([128, 4 * TT], BF16, "mar", S)
        g1s = P.sbuf([128, 2 * TT], BF16, "mg1", S)
        t1 = P.sbuf([128, 2 * TT], F32, "mt1", S)
        outf = P.sbuf([128, 2 * TT], F32, "mout", S)
        pA = [P.psum([128, 512], F32, "mpa", S) for _ in range(2)]
        pB = [P.psum([128, 512], F32, "mpb", S) for _ in range(2)]
        pL = P.psum([128, 512], F32, "mpl", S)

        kld = P.sem("mkld")
        wldp = [P.sem("mw0"), P.sem("mw1s")]
        wcast = P.sem("mwc")
        ldx = P.sem("mldx")
        ldr = P.sem("mldr")
        h1m = P.sem("mh1m")
        h1e = P.sem("mh1e")
        rcm = P.sem("mrcm")
        rce = P.sem("mrce")
        sqd = P.sem("msqd")
        lin = P.sem("mlin")
        ohd = P.sem("mohd")
        ohfin = P.sem("mohf")
        aam = P.sem("maam")
        aae = P.sem("maae")
        asd = P.sem("masd")
        bmm_ = P.sem("mbmm")
        bev = P.sem("mbev")
        fin = P.sem("mfin")
        stp2 = [P.sem("mst0"), P.sem("mst1")]

        NW1 = NEXP * NCH * 2
        NW2 = NEXP * 2 * NCH
        NWA = NCH * 4
        NWU = 4 * NCH
        NWB = NCH * NCH
        WALL = NW1 + NW2 + NWA + NWU + NWB

        def wsrc(j):
            if j < NW1:
                e, kc, mc = j // 16, (j % 16) // 2, j % 2
                return (H["wd1"][e, kc * 128:(kc + 1) * 128, mc * 128:(mc + 1) * 128],
                        wd1[:, j * 128:(j + 1) * 128])
            j2 = j - NW1
            if j2 < NW2:
                e, kc, mc = j2 // 16, (j2 % 16) // 8, j2 % 8
                return (H["wd2"][e, kc * 128:(kc + 1) * 128, mc * 128:(mc + 1) * 128],
                        wd2[:, j2 * 128:(j2 + 1) * 128])
            j3 = j2 - NW2
            if j3 < NWA:
                kc, mc = j3 // 4, j3 % 4
                return (H["wadc"][kc * 128:(kc + 1) * 128, mc * 128:(mc + 1) * 128],
                        wad[:, j3 * 128:(j3 + 1) * 128])
            j4 = j3 - NWA
            if j4 < NWU:
                kc, mc = j4 // 8, j4 % 8
                return (H["wauc"][kc * 128:(kc + 1) * 128, mc * 128:(mc + 1) * 128],
                        wau[:, j4 * 128:(j4 + 1) * 128])
            j5 = j4 - NWU
            kc, mc = j5 // 8, j5 % 8
            return (H["wbase"][kc * 128:(kc + 1) * 128, mc * 128:(mc + 1) * 128],
                    wbs[:, j5 * 128:(j5 + 1) * 128])

        @b.sync
        def _(sync):
            sync.dma_start(out=bd1c[:].rearrange("p (e c) -> p e c", c=2),
                           in_=H["bd1"]).then_inc(kld, 16)
            sync.dma_start(out=bd2c[:].rearrange("p (e c) -> p e c", c=NCH),
                           in_=H["bd2"]).then_inc(kld, 16)
            sync.dma_start(out=badcc[:], in_=H["badc"]).then_inc(kld, 16)
            sync.dma_start(out=bbc[:], in_=H["bbase"]).then_inc(kld, 16)
            sync.dma_start(out=bauc[:], in_=H["bauc"]).then_inc(kld, 16)
            sync.dma_start(out=ones128f[:], in_=H["k_ones128f"]).then_inc(kld, 16)
            sync.dma_start(out=ones11[:], in_=H["k_onesk1f"][:, 0:1]).then_inc(kld, 16)
            sync.dma_start(out=s8c[:], in_=H["k_s8"]).then_inc(kld, 16)
            sync.dma_start(out=m84[:], in_=H["k_m84"]).then_inc(kld, 16)
            sync.dma_start(out=ones8[:], in_=H["k_ones8"]).then_inc(kld, 16)
            for j in range(WALL):
                if j >= 2:
                    sync.wait_ge(wcast, j - 1)
                srcw, _dst = wsrc(j)
                sync.dma_start(out=wload[:, (j % 2) * 128:(j % 2 + 1) * 128],
                               in_=srcw).then_inc(wldp[j % 2], 16)
            for sb in range(BPC):
                if sb >= 1:
                    sync.wait_ge(fin, 8 * NT4 * sb)
                for ch in range(NCH):
                    sync.dma_start(out=xn1[:, ch * SEQ:(ch + 1) * SEQ],
                                   in_=H["xn1_b"][ch * 128:(ch + 1) * 128,
                                                  sb * SEQ:(sb + 1) * SEQ]).then_inc(ldx, 16)
                for t4 in range(NT4):
                    gt = sb * NT4 + t4
                    if gt >= 1:
                        sync.wait_ge(fin, 8 * gt)
                    for ch in range(NCH):
                        sync.dma_start(out=x1f[:, ch * TT:(ch + 1) * TT],
                                       in_=H["x1_f"][ch * 128:(ch + 1) * 128,
                                                     sb * SEQ + t4 * TT:sb * SEQ + (t4 + 1) * TT]
                                       ).then_inc(ldr, 16)
                        sync.dma_start(out=s1f[:, ch * TT:(ch + 1) * TT],
                                       in_=H["s1_f"][ch * 128:(ch + 1) * 128,
                                                     sb * SEQ + t4 * TT:sb * SEQ + (t4 + 1) * TT]
                                       ).then_inc(ldr, 16)

        @b.gpsimd
        def _(gpsimd):
            gpsimd.wait_ge(kld, 16 * 10)
            gpsimd.tensor_copy(baub[:], bauc[:])
            for j in range(WALL):
                gpsimd.wait_ge(wldp[j % 2], 16 * (j // 2 + 1))
                _src, dstw = wsrc(j)
                gpsimd.tensor_copy(dstw, wload[:, (j % 2) * 128:(j % 2 + 1) * 128]
                                   ).then_inc(wcast, 1)
            for sb in range(BPC):
                for t4 in range(NT4):
                    gt = sb * NT4 + t4
                    for mc in range(NCH):
                        gf = gt * NCH + mc
                        gpsimd.wait_ge(fin, gf + 1)
                        gpsimd.dma_start(
                            out=H["outT"][mc * 128:(mc + 1) * 128,
                                          sb * SEQ + t4 * TT:sb * SEQ + (t4 + 1) * TT],
                            in_=outf[:, (gf % 2) * TT:(gf % 2 + 1) * TT]
                            ).then_inc(stp2[gf % 2], 16)
            NGF = BPC * NT4 * NCH
            gpsimd.wait_ge(stp2[0], 16 * ((NGF + 1) // 2))
            gpsimd.wait_ge(stp2[1], 16 * (NGF // 2))

        @b.tensor
        def _(tensor):
            tensor.wait_ge(wcast, WALL)
            tensor.wait_ge(kld, 16 * 10)
            for sb in range(BPC):
                tensor.wait_ge(ldx, 16 * NCH * (sb + 1))
                for t4 in range(NT4):
                    gt = sb * NT4 + t4
                    for e in range(NEXP):
                        ge = gt * NEXP + e
                        for mc in range(2):
                            ih = 2 * ge + mc
                            if ih >= 2:
                                tensor.wait_ge(h1e, ih - 1)
                            for kc in range(NCH):
                                ins = tensor.matmul(
                                    pA[mc % 2][:, :],
                                    wd1[:, (e * 16 + kc * 2 + mc) * 128:
                                        (e * 16 + kc * 2 + mc + 1) * 128],
                                    xn1[:, kc * SEQ + t4 * TT:kc * SEQ + (t4 + 1) * TT],
                                    start=(kc == 0), stop=(kc == NCH - 1))
                            ins.then_inc(h1m, 1)
                        tensor.wait_ge(h1e, 2 * ge + 2)
                        for mc in range(NCH):
                            ir = 8 * ge + mc
                            if ir >= 2:
                                tensor.wait_ge(rce, ir - 1)
                            for kc in range(2):
                                ins = tensor.matmul(
                                    pB[mc % 2][:, :],
                                    wd2[:, (e * 16 + kc * 8 + mc) * 128:
                                        (e * 16 + kc * 8 + mc + 1) * 128],
                                    h1b[:, kc * TT:(kc + 1) * TT],
                                    start=(kc == 0), stop=(kc == 1))
                            ins.then_inc(rcm, 1)
                tensor.wait_ge(sqd, 8 * NEXP * NT4 * (sb + 1))
                if sb >= 1:
                    tensor.wait_ge(ohd, 6 * sb)
                tensor.matmul(pL[0:1, 0:64], ones128f[:], lacc[:], start=True,
                              stop=True).then_inc(ohd, 1)
                tensor.wait_ge(ohd, 6 * sb + 3)
                tensor.matmul(pL[0:8, 64:65], ohrow[:], ones11[:], start=True,
                              stop=True).then_inc(ohd, 1)
                tensor.wait_ge(ohd, 6 * sb + 5)
                tensor.matmul(pL[0:128, 128:132], s8c[:], rhs84[:], start=True,
                              stop=True).then_inc(ohd, 1)
                tensor.wait_ge(ohfin, 2 * (sb + 1))
                for t4 in range(NT4):
                    gt = sb * NT4 + t4
                    for mc in range(4):
                        ia = 4 * gt + mc
                        if ia >= 2:
                            tensor.wait_ge(aae, ia - 1)
                        for kc in range(NCH):
                            ins = tensor.matmul(
                                pA[mc % 2][:, :],
                                wad[:, (kc * 4 + mc) * 128:(kc * 4 + mc + 1) * 128],
                                xn1[:, kc * SEQ + t4 * TT:kc * SEQ + (t4 + 1) * TT],
                                start=(kc == 0), stop=(kc == NCH - 1))
                        ins.then_inc(aam, 1)
                    tensor.wait_ge(asd, 4 * (gt + 1))
                    for mc in range(NCH):
                        gf = gt * NCH + mc
                        if gf >= 2:
                            tensor.wait_ge(bev, gf - 1)
                        for kc in range(NCH):
                            tensor.matmul(
                                pB[mc % 2][:, :],
                                wbs[:, (kc * 8 + mc) * 128:(kc * 8 + mc + 1) * 128],
                                xn1[:, kc * SEQ + t4 * TT:kc * SEQ + (t4 + 1) * TT],
                                start=(kc == 0), stop=False)
                        for kc in range(4):
                            tensor.matmul(
                                pB[mc % 2][:, :],
                                wau[:, (kc * 8 + mc) * 128:(kc * 8 + mc + 1) * 128],
                                arel[:, kc * TT:(kc + 1) * TT],
                                start=False, stop=False)
                        ins = tensor.matmul(pB[mc % 2][:, :],
                                            baub[:, mc * 128:(mc + 1) * 128],
                                            ohones[:], start=False, stop=True)
                        ins.then_inc(bmm_, 1)

        @b.scalar
        def _(scalar):
            for sb in range(BPC):
                for t4 in range(NT4):
                    gt = sb * NT4 + t4
                    for e in range(NEXP):
                        ge = gt * NEXP + e
                        for mc in range(2):
                            ih = 2 * ge + mc
                            scalar.wait_ge(h1m, ih + 1)
                            if ge >= 1 and mc == 0:
                                scalar.wait_ge(rcm, 8 * ge)
                            scalar.activation(h1b[:, mc * TT:(mc + 1) * TT],
                                              pA[mc % 2][:, :], AF.Relu,
                                              bias=bd1c[:, e * 2 + mc:e * 2 + mc + 1]
                                              ).then_inc(h1e, 1)
                for t4 in range(NT4):
                    gt = sb * NT4 + t4
                    for mc in range(4):
                        ia = 4 * gt + mc
                        scalar.wait_ge(aam, ia + 1)
                        if gt >= 1 and mc == 0:
                            scalar.wait_ge(bmm_, 8 * gt)
                        scalar.activation(arel[:, mc * TT:(mc + 1) * TT],
                                          pA[mc % 2][:, :], AF.Relu,
                                          bias=badcc[:, mc:mc + 1]).then_inc(aae, 1)
                    for mc in range(NCH):
                        gf = gt * NCH + mc
                        scalar.wait_ge(bmm_, gf + 1)
                        if gf >= 2:
                            scalar.wait_ge(fin, gf - 1)
                        scalar.activation(g1s[:, (gf % 2) * TT:(gf % 2 + 1) * TT],
                                          pB[mc % 2][:, :], AF.Silu,
                                          bias=bbc[:, mc:mc + 1]).then_inc(bev, 1)

        @b.vector
        def _(vector):
            for sb in range(BPC):
                if sb >= 1:
                    vector.wait_ge(ohd, 6 * sb)
                P.join(vector, "vector")
                ins = vector.memset(lacc[:], 0.0)
                ins.then_inc(lin, 1)
                for t4 in range(NT4):
                    gt = sb * NT4 + t4
                    for e in range(NEXP):
                        ge = gt * NEXP + e
                        for mc in range(NCH):
                            ir = 8 * ge + mc
                            parr = ir % 2
                            vector.wait_ge(rcm, ir + 1)
                            if ir >= 2:
                                vector.wait_ge(sqd, ir - 1)
                            P.join(vector, "vector")
                            ins = vector.scalar_tensor_tensor(
                                db[:, parr * TT:(parr + 1) * TT],
                                pB[mc % 2][:, :],
                                bd2c[:, e * NCH + mc:e * NCH + mc + 1],
                                xn1[:, mc * SEQ + t4 * TT:mc * SEQ + (t4 + 1) * TT],
                                ALU.add, ALU.subtract)
                            ins.then_inc(rce, 1)
                            vector.wait_ge(rce, ir + 1)
                            ins = vector.scalar_tensor_tensor(
                                dsq[:, parr * TT:(parr + 1) * TT],
                                db[:, parr * TT:(parr + 1) * TT], 1.0,
                                db[:, parr * TT:(parr + 1) * TT],
                                ALU.mult, ALU.mult,
                                accum_out=ltmp[:, parr:parr + 1])
                            P.mark("vector", ins)
                            P.join(vector, "vector")
                            ins = vector.tensor_add(lacc[:, e * NCH + mc:e * NCH + mc + 1],
                                                    lacc[:, e * NCH + mc:e * NCH + mc + 1],
                                                    ltmp[:, parr:parr + 1])
                            ins.then_inc(sqd, 1)
                vector.wait_ge(ohd, 6 * sb + 1)
                P.join(vector, "vector")
                ins = vector.tensor_copy(lrow[:], pL[0:1, 0:64])
                P.mark("vector", ins)
                P.join(vector, "vector")
                ins = vector.tensor_reduce(loss8[:],
                                           lrow[:].rearrange("o (e c) -> o e c", c=NCH),
                                           AX.X, ALU.add)
                P.mark("vector", ins)
                P.join(vector, "vector")
                ins = vector.tensor_reduce(lmin[:], loss8[:], AX.X, ALU.min)
                P.mark("vector", ins)
                P.join(vector, "vector")
                ins = vector.tensor_single_scalar(ohrow[:], loss8[:], lmin[:, 0:1],
                                                  ALU.is_le)
                ins.then_inc(ohd, 2)
                vector.wait_ge(ohd, 6 * sb + 4)
                P.join(vector, "vector")
                ins = vector.tensor_copy(oh8[:], pL[0:8, 64:65])
                P.mark("vector", ins)
                P.join(vector, "vector")
                ins = vector.tensor_scalar_mul(rhs84[:], m84[:], oh8[:, 0:1])
                ins.then_inc(ohd, 1)
                vector.wait_ge(ohd, 6 * sb + 6)
                P.join(vector, "vector")
                ins = vector.tensor_copy(ohR[:], pL[0:128, 128:132])
                ins.then_inc(ohfin, 1)
                P.join(vector, "vector")
                ins = vector.tensor_scalar_mul(ohones[:], ones8[:], oh8[:, 0:1])
                ins.then_inc(ohfin, 1)
                for t4 in range(NT4):
                    gt = sb * NT4 + t4
                    vector.wait_ge(ldr, 16 * 2 * NCH * (gt + 1))
                    for mc in range(4):
                        ia = 4 * gt + mc
                        vector.wait_ge(aae, ia + 1)
                        P.join(vector, "vector")
                        ins = vector.tensor_scalar_mul(arel[:, mc * TT:(mc + 1) * TT],
                                                       arel[:, mc * TT:(mc + 1) * TT],
                                                       ohR[:, mc:mc + 1])
                        ins.then_inc(asd, 1)
                    for mc in range(NCH):
                        gf = gt * NCH + mc
                        vector.wait_ge(bev, gf + 1)
                        if gf >= 2:
                            vector.wait_ge(stp2[gf % 2], 16 * (gf // 2))
                        P.join(vector, "vector")
                        ins = vector.tensor_mul(t1[:, (gf % 2) * TT:(gf % 2 + 1) * TT],
                                                s1f[:, mc * TT:(mc + 1) * TT],
                                                g1s[:, (gf % 2) * TT:(gf % 2 + 1) * TT])
                        P.mark("vector", ins)
                        P.join(vector, "vector")
                        ins = vector.tensor_add(outf[:, (gf % 2) * TT:(gf % 2 + 1) * TT],
                                                x1f[:, mc * TT:(mc + 1) * TT],
                                                t1[:, (gf % 2) * TT:(gf % 2 + 1) * TT])
                        ins.then_inc(fin, 1)


def _mk_attn(l):
    src_n = "xn_f" if l == 0 else "gb_f"
    return (f"p3l{l}a", lambda P, l=l, s=src_n: emit_l0_attn(P, l, s, "ga_f"),
            [src_n] + l0_attn_consts(l), ["ga_f"])


def _mk_ffn(l):
    return (f"p3l{l}f", lambda P, l=l: emit_l0_ffn(P, l, "ga_f", "gb_f"),
            ["ga_f"] + l0_ffn_consts(l), ["gb_f"])


PHASES_TAIL = [
    ("p1", p1_rnn, ["u_b", "g_f", "inp_f", "aT", "k_ident"], ["states_b"]),
    ("p2", p2_wout, ["states_b", "w_outT", "b_out", "k_ones1"], ["s0_f"]),
] + [p for l in range(L0DEPTH) for p in (_mk_attn(l), _mk_ffn(l))] + [
    ("p9", p9_combine, ["xT", "s0_f", "gb_f", "k_ones128", "k_onesk1f"],
     ["x1_f", "xn1_b"]),
    ("p10a", p10a_inproj, ["xn1_b", "inprojT"], ["u0_b", "zs_b"]),
    ("p10b", p10b_conv, ["u0_b", "zs_b", "convw", "convb", "dtw", "dtb", "xprojT",
                         "k_onesk1f"], ["u2_b", "dte_f", "du_b", "bc_f"]),
    ("p11", p11_scan, ["dte_f", "du_b", "u2_b", "bc_f", "alog", "dvec",
                       "k_onesk1f", "k_ident"], ["y_b"]),
    ("p12", p12_outproj, ["y_b", "outprojT", "k_ones1"], ["s1_f"]),
    ("p13", p13_moe, ["xn1_b", "x1_f", "s1_f", "wd1", "bd1", "wd2", "bd2", "wadc",
                      "badc", "wauc", "bauc", "wbase", "bbase", "k_ones128f",
                      "k_onesk1f", "k_s8", "k_m84", "k_ones8"], ["outT"]),
]
PHASES = [
    ("p0", p0_ln_proj,
     ["xT", "w_inT", "w_gateT", "w_bT", "b_in", "b_gate",
      "k_ones1", "k_ones128", "k_onesk1", "k_onesk1f"],
     ["xn_f", "xn_b", "u_b", "g_f", "inp_f"]),
] + PHASES_TAIL


def _np_dt(dt):
    return {F32: np.float32, BF16: ml_dtypes.bfloat16}[dt]


def run_phases(cores_host, phases=None, state=None):
    reg = hbm_registry()
    if state is None:
        state = [dict(c) for c in cores_host]
    plist = PHASES if phases is None else [p for p in PHASES if p[0] in phases]
    for (name, emit, reads, writes) in plist:
        nc = bass.Bass("TRN2", target_bir_lowering=False)
        hbm = {}
        for r in list(dict.fromkeys(list(reads) + ["k_scal", "k_scal_b"])):
            sh, dt, _ = reg[r]
            hbm[r] = nc.dram_tensor(r, sh, dt, kind="ExternalInput").ap()
        for w in writes:
            sh, dt, _ = reg[w]
            hbm[w] = nc.dram_tensor(w, sh, dt, kind="ExternalOutput").ap()
        with contextlib.ExitStack() as es:
            block = es.enter_context(nc.Block())
            P = Prog(nc, block, es, hbm)
            emit_setup(P)
            emit(P)
        rdall = list(dict.fromkeys(list(reads) + ["k_scal", "k_scal_b"]))
        in_maps = [{r: np.ascontiguousarray(state[c][r], _np_dt(reg[r][1]))
                    for r in rdall} for c in range(NCORE)]
        res = run_bass_kernel_spmd(nc, in_maps, core_ids=list(range(NCORE)))
        for c in range(NCORE):
            state[c].update(res.results[c])
    return state


def run_fused(cores_host):
    reg = hbm_registry()
    nc = bass.Bass("TRN2", target_bir_lowering=False)
    hbm = {}
    for name, (sh, dt, is_in) in reg.items():
        if is_in:
            hbm[name] = nc.dram_tensor(name, sh, dt, kind="ExternalInput").ap()
        elif name == "outT":
            hbm[name] = nc.dram_tensor(name, sh, dt, kind="ExternalOutput").ap()
        else:
            hbm[name] = nc.dram_tensor(name, sh, dt, kind="Internal").ap()
    with contextlib.ExitStack() as es:
        block = es.enter_context(nc.Block())
        P = Prog(nc, block, es, hbm, pooled=(os.environ.get("FUSE_POOL", "1") == "1"))
        if P.pooled:
            P.init_pool(64)
        emit_setup(P)
        plist = PHASES[:int(os.environ.get("FUSE_N", str(len(PHASES))))]
        skip = set(os.environ.get("FUSE_SKIP", "").split(","))
        plist = [p for p in plist if p[0] not in skip]
        for i, (name, emit, reads, writes) in enumerate(plist):
            emit(P)
            if i < len(plist) - 1:
                P.phase_boundary()
    in_names = [n for n, (sh, dt, is_in) in reg.items() if is_in]
    in_maps = [{n: np.ascontiguousarray(cores_host[c][n], _np_dt(reg[n][1]))
                for n in in_names} for c in range(NCORE)]
    res = run_bass_kernel_spmd(nc, in_maps, core_ids=list(range(NCORE)))
    return [{"outT": res.results[c]["outT"]} for c in range(NCORE)]


def kernel(**inputs):
    cores_host = build_host_inputs(inputs)
    if FUSED:
        state = run_fused(cores_host)
    else:
        state = run_phases(cores_host)
    outs = []
    for c in range(NCORE):
        oT = np.asarray(state[c]["outT"], np.float32)
        outs.append(oT.reshape(D, BPC, SEQ).transpose(1, 2, 0))
    return np.concatenate(outs, 0)
